# version 01
# baseline (speedup 1.0000x reference)
import contextlib
import numpy as np
import ml_dtypes
import concourse.bass as bass
import concourse.mybir as mybir
from concourse.bass_utils import run_bass_kernel_spmd

F32 = mybir.dt.float32
BF16 = mybir.dt.bfloat16
I32 = mybir.dt.int32
AF = mybir.ActivationFunctionType
ALU = mybir.AluOpType
AX = mybir.AxisListType
NPBF16 = ml_dtypes.bfloat16

B, L, D = 4, 8192, 1024
H, DK, LOW, LEN = 16, 64, 20, 200
E, HDIM = 8, 4096
NT = 4096
TT = 256
NTILE = NT // TT
IDX200 = np.linspace(np.float32(0.0), np.float32(L - 1), LEN,
                     dtype=np.float32).astype(np.int32)


class MS:
    def __init__(self, stack, nc, name, step=1):
        self.sem = stack.enter_context(nc.semaphore(name))
        self.step = step
        self.n = 0


class Sched:

    def __init__(self, stack, nc):
        self.nc = nc
        self.ops = {e: [] for e in ("t", "v", "s", "g", "y")}
        self.ms = {e: MS(stack, nc, f"ms_{e}") for e in ("t", "v", "s", "g")}
        self.dy = MS(stack, nc, "dma_y", 16)
        self.dg = MS(stack, nc, "dma_g", 16)
        self.chain = {e: MS(stack, nc, f"chain_{e}") for e in ("v", "s")}
        self._cur_eng = None

    def G(self, e, instr):
        cs = self.chain[self._cur_eng]
        cs.n += 1
        instr.then_inc(cs.sem, 1)
        e.wait_ge(cs.sem, cs.n)
        return instr

    def _waits(self, e, waits):
        for (st, val) in waits:
            sem = st.sem if isinstance(st, MS) else self.ms[st].sem
            e.wait_ge(sem, val)

    def add(self, eng, fn, inc=False, waits=()):
        ms = self.ms[eng]
        tok = None
        if inc:
            ms.n += 1
            tok = ms.n
        wl = [w for w in waits if w is not None]

        def run(e, _fn=fn, _tok=tok, _wl=wl, _ms=ms, _eng=eng):
            self._cur_eng = _eng
            self._waits(e, _wl)
            last = _fn(e)
            if _tok is not None:
                last.then_inc(_ms.sem, 1)
                if _eng in ("v", "s"):
                    e.wait_ge(_ms.sem, _tok)
            elif _eng in ("v", "s"):
                self.G(e, last)
        self.ops[eng].append(run)
        return tok

    def dma(self, eng, fn, waits=()):
        stream = self.dy if eng == "y" else self.dg
        stream.n += 16
        tok = stream.n
        wl = [w for w in waits if w is not None]

        def run(e, _fn=fn, _wl=wl, _st=stream):
            self._waits(e, _wl)
            _fn(e).then_inc(_st.sem, 16)
        self.ops[eng].append(run)
        return tok

    def wait(self, eng, waits):
        wl = [w for w in waits if w is not None]
        self.ops[eng].append(lambda e, _wl=wl: self._waits(e, _wl))

    def emit(self, blk):
        for eng, meth in (("t", blk.tensor), ("v", blk.vector),
                          ("s", blk.scalar), ("g", blk.gpsimd),
                          ("y", blk.sync)):
            ops = self.ops[eng]
            if not ops:
                continue

            def body(e, _ops=tuple(ops)):
                for fn in _ops:
                    fn(e)
            meth(body)
        self.ops = {e: [] for e in self.ops}


def _consts():
    ind16 = np.zeros((128, 8 * 16), np.float32)
    indT16 = np.zeros((16, 8 * 128), np.float32)
    for mc in range(8):
        for p in range(128):
            j = 2 * mc + p // 64
            ind16[p, mc * 16 + j] = 1.0
            indT16[j, mc * 128 + p] = 1.0
    return dict(ident=np.eye(128, dtype=NPBF16), ind16=ind16, indT16=indT16)


CONSTS = _consts()


def ln_tile(sc, sb, src, n_blk, out_f32, out_bf16, waits, src_off=0,
            extra_fin_waits=()):
    st = sb["st"]
    nb = n_blk

    def x3():
        return sb[src][:, src_off:src_off + nb * 1024].rearrange(
            "p (b f) -> p b f", b=nb)
    t_s1 = sc.add("v", lambda e: e.reduce_sum(st[:, 0:nb], x3(), AX.X),
                  inc=True, waits=waits)
    t_sq = sc.add("s", lambda e: e.activation(
        sb["sqx"][:, 0:nb * 1024].rearrange("p (b f) -> p b f", b=nb),
        x3(), AF.Square), inc=True, waits=waits)

    def stats_ops(e):
        sc.G(e, e.reduce_sum(st[:, nb:2 * nb],
                             sb["sqx"][:, 0:nb * 1024].rearrange("p (b f) -> p b f", b=nb),
                             AX.X))
        sc.G(e, e.tensor_scalar(st[:, 2 * nb:3 * nb], st[:, 0:nb], 1.0 / 1024, None, ALU.mult))
        sc.G(e, e.tensor_scalar(st[:, 3 * nb:4 * nb], st[:, nb:2 * nb], 1.0 / 1024, None, ALU.mult))
        sc.G(e, e.tensor_tensor(st[:, 4 * nb:5 * nb], st[:, 2 * nb:3 * nb],
                                st[:, 2 * nb:3 * nb], ALU.mult))
        sc.G(e, e.tensor_tensor(st[:, 3 * nb:4 * nb], st[:, 3 * nb:4 * nb],
                                st[:, 4 * nb:5 * nb], ALU.subtract))
        return e.tensor_scalar(st[:, 3 * nb:4 * nb], st[:, 3 * nb:4 * nb],
                               1e-5, None, ALU.add)
    t_v2 = sc.add("v", stats_ops, inc=True, waits=[("s", t_sq), ("v", t_s1)])
    t_sd = sc.add("s", lambda e: e.activation(st[:, 4 * nb:5 * nb],
                                              st[:, 3 * nb:4 * nb], AF.Sqrt),
                  inc=True, waits=[("v", t_v2)])

    def fin(e):
        sc.G(e, e.reciprocal(st[:, 5 * nb:6 * nb], st[:, 4 * nb:5 * nb]))
        last = None
        for b in range(nb):
            dst = (sb[out_f32] if out_f32 else sb[out_bf16])
            last = e.tensor_scalar(
                dst[:, b * 1024:(b + 1) * 1024],
                sb[src][:, src_off + b * 1024: src_off + (b + 1) * 1024],
                st[:, 2 * nb + b:2 * nb + b + 1],
                st[:, 5 * nb + b:5 * nb + b + 1],
                ALU.subtract, ALU.mult)
        if out_f32 and out_bf16:
            sc.G(e, last)
            last = e.tensor_copy(sb[out_bf16][:, 0:nb * 1024],
                                 sb[out_f32][:, 0:nb * 1024])
        return last
    return sc.add("v", fin, inc=True,
                  waits=[("s", t_sd)] + [w for w in extra_fin_waits if w])


def quant_1024(sc, sb, src, src_off, dst, dst_off, wait_first):
    def s3(n, o=0):
        return sb[n][:, o:o + 1024].rearrange("p (a b) -> p a b", a=32)

    def v1(e):
        sc.G(e, e.reduce_max(sb["qsc"][:, 32:64], s3(src, src_off), AX.X,
                             apply_absolute_value=True))
        sc.G(e, e.tensor_scalar(sb["qsc"][:, 0:32], sb["qsc"][:, 32:64],
                                1.0 / 7.5, 1e-30, ALU.mult, ALU.max))
        return e.reciprocal(sb["qrc"][:, 0:32], sb["qsc"][:, 0:32])
    t1 = sc.add("v", v1, inc=True, waits=wait_first)

    def v2(e):
        z2 = sb["z2"][:, 0:1024]; z3 = sb["z3"][:, 0:1024]
        z4 = sb["z4"][:, 0:1024]; z5 = sb["z5"][:, 0:1024]
        zi = z3.bitcast(I32)
        sc.G(e, e.tensor_tensor(s3("z2"), s3(src, src_off),
                                sb["qrc"][:, 0:32].unsqueeze(2)
                                .broadcast_to([128, 32, 32]), ALU.mult))
        sc.G(e, e.scalar_tensor_tensor(z3, z2, -1.0, z2, ALU.mult, ALU.max))
        sc.G(e, e.tensor_scalar(z4, z3, 2.0, None, ALU.is_lt))
        sc.G(e, e.tensor_scalar(z5, z3, 4.0, None, ALU.is_lt))
        sc.G(e, e.tensor_scalar(z3, z5, -0.25, 0.5, ALU.mult, ALU.add))
        sc.G(e, e.scalar_tensor_tensor(z3, z4, -0.125, z3, ALU.mult, ALU.add))
        sc.G(e, e.tensor_scalar(z4, z4, 4.0, None, ALU.mult))
        sc.G(e, e.tensor_scalar(z5, z5, 2.0, 2.0, ALU.mult, ALU.add))
        sc.G(e, e.tensor_tensor(z5, z4, z5, ALU.add))
        sc.G(e, e.tensor_tensor(z2, z2, z5, ALU.mult))
        zi = z4.bitcast(I32)
        sc.G(e, e.tensor_copy(zi, z2))
        sc.G(e, e.tensor_copy(z2, zi))
        sc.G(e, e.tensor_tensor(z2, z2, z3, ALU.mult))
        return e.tensor_tensor(
            sb[dst][:, dst_off:dst_off + 1024].rearrange("p (a b) -> p a b", a=32),
            s3("z2"),
            sb["qsc"][:, 0:32].unsqueeze(2).broadcast_to([128, 32, 32]),
            ALU.mult)
    return sc.add("v", v2, inc=True, waits=[("v", t1)])


def transpose_16(sc, sb, src, dst, wait_first, guard_slots, ident="ident",
                 nslots=2):
    evt = {}
    last_t = None
    for g4 in range(4):
        slot = g4 % nslots

        def tp(e, _g=g4, _s=slot):
            last = None
            for j in range(4):
                bi = _g * 4 + j
                blkT, kc = bi // 8, bi % 8
                last = e.transpose(
                    sb["ptp"][:, _s * 1024 + j * 128:_s * 1024 + (j + 1) * 128],
                    sb[src][:, blkT * 1024 + kc * 128: blkT * 1024 + (kc + 1) * 128],
                    sb[ident][:, :])
            return last
        w = list(wait_first)
        if g4 >= nslots:
            w.append(("v", evt[g4 - nslots]))
        elif guard_slots.get(slot) is not None:
            w.append(("v", guard_slots[slot]))
        last_t = sc.add("t", tp, inc=True, waits=w)

        def ev(e, _g=g4, _s=slot):
            last = None
            for j in range(4):
                bi = _g * 4 + j
                blkT, kc = bi // 8, bi % 8
                last = e.tensor_copy(
                    sb[dst][:, kc * 256 + blkT * 128: kc * 256 + (blkT + 1) * 128],
                    sb["ptp"][:, _s * 1024 + j * 128:_s * 1024 + (j + 1) * 128])
            return last
        evt[g4] = sc.add("v", ev, inc=True, waits=[("t", last_t)])
    return evt, last_t


def build_l0(debug=False):
    nc = bass.Bass("TRN2", target_bir_lowering=False)
    ap = {}
    for name, shape, dt in [
        ("x", [NT, D], F32), ("xs", [256, D], F32),
        ("wq", [D, D], F32), ("wk", [D, D], F32), ("wo", [D, D], F32),
        ("wep", [256, H * LOW], F32), ("wrp", [256, H * LOW], F32),
        ("wcz_a", [64, DK], F32), ("wcz_b", [64, DK], F32),
        ("bqc", [128, 8], F32), ("bkc", [128, 8], F32),
        ("bc2", [128, 1], F32), ("bob", [128, D], F32), ("fbb", [128, D], F32),
        ("fw", [D, D], F32),
        ("ident", [128, 128], BF16), ("ind16", [128, 128], F32),
        ("indT16", [16, 1024], F32),
    ]:
        ap[name] = nc.dram_tensor(name, shape, dt, kind="ExternalInput").ap()
    x1_out = nc.dram_tensor("x1", [NT, D], F32, kind="ExternalOutput").ap()

    with contextlib.ExitStack() as stack:
        sb = {}

        def S(name, shape, dt=F32):
            sb[name] = stack.enter_context(nc.sbuf_tensor("sb_" + name, shape, dt))

        def P(name, shape, dt=F32):
            sb[name] = stack.enter_context(nc.psum_tensor("ps_" + name, shape, dt))

        S("wq", [128, 8 * 1024], BF16); S("wk", [128, 8 * 1024], BF16)
        S("wo", [128, 8 * 1024], BF16)
        S("wd", [128, 8 * 1024], BF16); S("wdT", [128, 8 * 1024], BF16)
        S("webd", [128, 8 * 128], BF16); S("wrbd", [128, 8 * 128], BF16)
        S("wcz_a", [64, DK], BF16); S("wcz_b", [64, DK], BF16)
        S("bqc", [128, 8]); S("bkc", [128, 8])
        S("bc2", [128, 1]); S("bob", [128, D]); S("fbb", [128, D])
        S("ident", [128, 128], BF16)
        S("ind16", [128, 128]); S("indT16", [16, 1024])
        S("st", [128, 16])
        S("rn", [16, 256]); S("nrm", [16, 256])
        P("ptp", [128, 2 * 1024], BF16)
        P("pqk", [128, 2 * 512])
        P("pbig", [128, 1024])

        S("s_sb", [128, 2048]); S("sqx", [128, 2048])
        sc = Sched(stack, nc)
        with contextlib.ExitStack() as setup_stack:
            for (nm, shp, dt) in (("xs", [128, 2 * 1024], F32),
                                  ("xsn", [128, 2 * 1024], BF16),
                                  ("wetmp", [64, 2 * 320], BF16),
                                  ("wep", [128, 2 * 320], BF16),
                                  ("wrp", [128, 2 * 320], BF16),
                                  ("z2", [128, 1024], F32),
                                  ("z3", [128, 1024], F32),
                                  ("z4", [128, 1024], F32),
                                  ("z5", [128, 1024], F32),
                                  ("qsc", [128, 64], F32),
                                  ("qrc", [128, 64], F32)):
                sb[nm] = setup_stack.enter_context(
                    nc.sbuf_tensor("sbs_" + nm, shp, dt))
            _emit_l0_setup(nc, sc, sb, ap)
        S("sx", [128, 2048])
        S("xn", [128, 2048]); S("xnb", [128, 2048], BF16)
        S("xnT", [128, 8 * 256], BF16)
        S("qfm", [128, 8 * 256]); S("kfm", [128, 8 * 256])
        S("qnb", [128, 8 * 256], BF16); S("knb", [128, 8 * 256], BF16)
        S("esc", [64, 8 * 256], BF16); S("rsc", [64, 8 * 256], BF16)
        S("attnf", [128, 8 * 256], BF16); S("odds", [64, 8 * 256], BF16)
        S("g_sb", [128, 2048])
        S("z2", [128, 1024]); S("z3", [128, 1024])
        S("z4", [128, 1024]); S("z5", [128, 1024])
        S("xdb", [128, 2048], BF16)
        S("qsc", [128, 64]); S("qrc", [128, 64])
        _emit_l0_main(nc, sc, sb, ap, x1_out)
        if debug:
            with nc.Block() as blk:
                for nm in ("xn", "xnb", "xnT", "qfm", "kfm", "qnb", "knb",
                           "esc", "rsc", "attnf", "s_sb", "g_sb", "xdb",
                           "wd", "wdT", "webd", "wrbd"):
                    t = sb[nm]
                    o = nc.dram_tensor("dbg_" + nm, list(t.ap().shape),
                                       t.ap().dtype, kind="ExternalOutput").ap()
                    sc.dma("y", lambda e, _t=t, _o=o: e.dma_start(out=_o, in_=_t[:, :]))
                sc.wait("y", [(sc.dy, sc.dy.n)])
                sc.emit(blk)
    return nc


def _emit_l0_setup(nc, sc, sb, ap):
    with nc.Block() as blk:
        loads_y = [
            lambda e: e.dma_start(out=sb["xs"][:, :].rearrange("p (c f) -> p c f", c=2),
                                  in_=ap["xs"].rearrange("(c p) f -> p c f", p=128)),
            lambda e: e.dma_start(out=sb["bqc"][:, :], in_=ap["bqc"]),
            lambda e: e.dma_start(out=sb["bkc"][:, :], in_=ap["bkc"]),
            lambda e: e.dma_start(out=sb["bc2"][:, :], in_=ap["bc2"]),
            lambda e: e.dma_start(out=sb["bob"][:, :], in_=ap["bob"]),
            lambda e: e.dma_start(out=sb["fbb"][:, :], in_=ap["fbb"]),
            lambda e: e.dma_start(out=sb["ident"][:, :], in_=ap["ident"]),
            lambda e: e.dma_start(out=sb["ind16"][:, :], in_=ap["ind16"]),
            lambda e: e.dma_start(out=sb["indT16"][:, :], in_=ap["indT16"]),
        ]
        for f in loads_y:
            t_ldf = sc.dma("y", f)
        loads_g = [
            lambda e: e.dma_start(out=sb["wq"][:, :].rearrange("p (kc n) -> p kc n", kc=8),
                                  in_=ap["wq"].rearrange("(kc p) n -> p kc n", p=128)),
            lambda e: e.dma_start(out=sb["wk"][:, :].rearrange("p (kc n) -> p kc n", kc=8),
                                  in_=ap["wk"].rearrange("(kc p) n -> p kc n", p=128)),
            lambda e: e.dma_start(out=sb["wo"][:, :].rearrange("p (kc n) -> p kc n", kc=8),
                                  in_=ap["wo"].rearrange("(kc p) n -> p kc n", p=128)),
            lambda e: e.dma_start(out=sb["wep"][:, :].rearrange("p (c f) -> p c f", c=2),
                                  in_=ap["wep"].rearrange("(c p) f -> p c f", p=128)),
            lambda e: e.dma_start(out=sb["wrp"][:, :].rearrange("p (c f) -> p c f", c=2),
                                  in_=ap["wrp"].rearrange("(c p) f -> p c f", p=128)),
            lambda e: e.dma_start(out=sb["wcz_a"][:, :], in_=ap["wcz_a"]),
            lambda e: e.dma_start(out=sb["wcz_b"][:, :], in_=ap["wcz_b"]),
        ]
        for f in loads_g:
            t_ldc = sc.dma("g", f)
        sc.wait("y", [(sc.dy, t_ldf)])
        sc.wait("g", [(sc.dg, t_ldc)])
        sc.wait("v", [(sc.dy, t_ldf), (sc.dg, t_ldc)])
        sc.emit(blk)

    with nc.Block() as blk:
        t_ln = ln_tile(sc, sb, src="xs", n_blk=2, out_f32=None, out_bf16="xsn",
                       waits=[])

        def we_mms(e, wsrc):
            last = None
            for h in range(H):
                for lc in range(2):
                    last = e.matmul(
                        sb["pqk"][0:64, h * 20:(h + 1) * 20],
                        sb["xsn"][:, lc * 1024 + h * 64: lc * 1024 + (h + 1) * 64],
                        sb[wsrc][:, lc * 320 + h * 20: lc * 320 + (h + 1) * 20],
                        start=(lc == 0), stop=(lc == 1))
            return last
        t_we = sc.add("t", lambda e: we_mms(e, "wep"), inc=True,
                      waits=[("v", t_ln)])
        t_wev = sc.add("v", lambda e: e.tensor_copy(sb["wetmp"][0:64, 0:320],
                                                    sb["pqk"][0:64, 0:320]),
                       inc=True, waits=[("t", t_we)])
        t_wr = sc.add("t", lambda e: we_mms(e, "wrp"), inc=True,
                      waits=[("v", t_wev)])
        t_wrv = sc.add("v", lambda e: e.tensor_copy(sb["wetmp"][0:64, 320:640],
                                                    sb["pqk"][0:64, 0:320]),
                       inc=True, waits=[("t", t_wr)])
        t_z = sc.add("g", lambda e: (e.memset(sb["webd"][:, :], 0.0),
                                     e.memset(sb["wrbd"][:, :], 0.0))[-1],
                     inc=True)
        scat_toks = []
        for (src_off, dst) in ((0, "webd"), (320, "wrbd")):
            for par in range(2):
                def one(e, _so=src_off, _d=dst, _p=par):
                    return e.dma_start(
                        out=sb[_d][_p * 64:(_p + 1) * 64, :]
                        .rearrange("p (mc c) -> p mc c", mc=8)[:, :, _p * 32:_p * 32 + 20],
                        in_=sb["wetmp"][0:64, _so:_so + 320]
                        .rearrange("p (mc two c) -> p mc two c", mc=8, two=2)[:, :, _p, :],
                        allow_slow_non_contiguous=True)
                scat_toks.append(sc.dma("g", one,
                                        waits=[("v", t_wrv), ("g", t_z)]))

        for oc in range(8):
            prev_v = sc.ms["v"].n
            t_l = sc.dma("y", lambda e, _oc=oc: e.dma_start(
                out=sb["s_sb"][:, 0:1024],
                in_=ap["fw"].rearrange("(oc p) i -> oc p i", p=128)[_oc]),
                waits=[("v", prev_v)])
            quant_1024(sc, sb, src="s_sb", src_off=0, dst="wd",
                       dst_off=oc * 1024, wait_first=[(sc.dy, t_l)])
        evt = {}
        for g4 in range(16):
            slot = g4 % 2

            def tp(e, _g=g4, _s=slot):
                last = None
                for j in range(4):
                    bi = _g * 4 + j
                    oc, icc = bi // 8, bi % 8
                    last = e.transpose(
                        sb["ptp"][:, _s * 1024 + j * 128:_s * 1024 + (j + 1) * 128],
                        sb["wd"][:, oc * 1024 + icc * 128: oc * 1024 + (icc + 1) * 128],
                        sb["ident"][:, :])
                return last
            w = [("v", sc.ms["v"].n)] if g4 < 2 else [("v", evt[g4 - 2])]
            t_t = sc.add("t", tp, inc=True, waits=w)

            def ev(e, _g=g4, _s=slot):
                last = None
                for j in range(4):
                    bi = _g * 4 + j
                    oc, icc = bi // 8, bi % 8
                    last = e.tensor_copy(
                        sb["wdT"][:, icc * 1024 + oc * 128: icc * 1024 + (oc + 1) * 128],
                        sb["ptp"][:, _s * 1024 + j * 128:_s * 1024 + (j + 1) * 128])
                return last
            evt[g4] = sc.add("v", ev, inc=True, waits=[("t", t_t)])
        sc.wait("t", [("v", sc.ms["v"].n)] +
                [(sc.dg, max(scat_toks))])
        sc.wait("v", [("g", sc.ms["g"].n), (sc.dg, max(scat_toks))])
        sc.wait("s", [("v", sc.ms["v"].n)])
        sc.emit(blk)


def _emit_l0_main(nc, sc, sb, ap, x1_out):
    with nc.Block() as blk:
        guards = {}
        for i in range(NTILE):
            off = 0
            w_load = []
            if i - 1 in guards:
                w_load.append(("v", guards[i - 1]["gate"]))
            t_x = sc.dma("y", lambda e, _i=i, _o=off: e.dma_start(
                out=sb["sx"][:, _o:_o + 2048].rearrange("p (b f) -> p b f", b=2),
                in_=ap["x"].rearrange("(i b p) f -> i p b f", b=2, p=128)[_i]),
                waits=w_load)
            guards[i] = _l0_tile(sc, sb, i, off, t_x, x1_out,
                                 guards.get(i - 1))
        sc.wait("y", [(sc.dy, sc.dy.n)])
        sc.emit(blk)


def _l0_tile(sc, sb, i, off, t_x, x1_out, pg):
    g = {}
    extra = []
    if pg:
        extra.append(("t", pg["ssq_t"]))
        extra.append(("t", pg["xd_tp_t"]))
    t_ln = ln_tile(sc, sb, src="sx", n_blk=2, out_f32="xn", out_bf16="xnb",
                   waits=[(sc.dy, t_x)], src_off=off, extra_fin_waits=extra)
    g["ln"] = t_ln

    tp_guard = {0: pg["xd_ev2"] if pg else None, 1: pg["xd_ev3"] if pg else None}
    w0 = [("v", t_ln)]
    if pg:
        w0.append(("v", pg["proj_last"]))
        w0.append(("t", pg["ffn_t"]))
    evt, t_tp_last = transpose_16(sc, sb, "xnb", "xnT", w0, tp_guard)
    g["tp_ev2"], g["tp_ev3"] = evt[2], evt[3]

    t_proj_ev = {}
    t_proj_mm = {}
    idx = 0
    for (wname, bname, oname) in (("wq", "bqc", "qfm"), ("wk", "bkc", "kfm")):
        for mc in range(8):
            slot = idx % 2

            def mm(e, _w=wname, _mc=mc, _s=slot):
                last = None
                for kc in range(8):
                    last = e.matmul(
                        sb["pqk"][:, _s * 512:_s * 512 + 256],
                        sb[_w][:, kc * 1024 + _mc * 128: kc * 1024 + (_mc + 1) * 128],
                        sb["xnT"][:, kc * 256:(kc + 1) * 256],
                        start=(kc == 0), stop=(kc == 7))
                return last
            w = [("v", evt[3])]
            if idx >= 2:
                w.append(("v", t_proj_ev[idx - 2]))
            elif pg:
                w.append(("v", pg["at_ev"]))
                w.append(("v", pg["qn_v"]))
            t_proj_mm[idx] = sc.add("t", mm, inc=True, waits=w)
            t_proj_ev[idx] = sc.add(
                "v", lambda e, _b=bname, _o=oname, _mc=mc, _s=slot: e.tensor_scalar(
                    sb[_o][:, _mc * 256:(_mc + 1) * 256],
                    sb["pqk"][:, _s * 512:_s * 512 + 256],
                    sb[_b][:, _mc:_mc + 1], None, ALU.add),
                inc=True, waits=[("t", t_proj_mm[idx])])
            idx += 1
    g["proj_last"] = t_proj_ev[15]
    g["proj_t"] = t_proj_mm[15]

    t_nrm = {}
    t_ssq_all = None
    for (j, (src, dstb)) in enumerate((("qfm", "qnb"), ("kfm", "knb"))):
        wait_src = t_proj_ev[7] if j == 0 else t_proj_ev[15]
        w_sq = [("v", wait_src), ("v", t_ln)]
        if j == 1:
            w_sq.append(("t", t_ssq_all))
        t_sq = sc.add("s", lambda e, _s=src: e.activation(
            sb["sqx"][:, 0:2048], sb[_s][:, 0:2048], AF.Square),
            inc=True, waits=w_sq)
        w_ssq = [("s", t_sq)]
        if j == 0:
            w_ssq.append(("v", t_proj_ev[14]))
        else:
            w_ssq.append(("v", t_nrm[(0, 7)]))

        def ssq(e):
            last = None
            for mc in range(8):
                last = e.matmul(
                    sb["pqk"][0:16, 0:256],
                    sb["ind16"][:, mc * 16:(mc + 1) * 16],
                    sb["sqx"][:, mc * 256:(mc + 1) * 256],
                    start=(mc == 0), stop=(mc == 7))
            return last
        t_ssq = sc.add("t", ssq, inc=True, waits=w_ssq)
        t_ssq_all = t_ssq
        t_n = sc.add("s", lambda e: e.activation(
            sb["nrm"][:, 0:256], sb["pqk"][0:16, 0:256], AF.Sqrt),
            inc=True, waits=[("t", t_ssq)])
        t_rn = sc.add("v", lambda e: (
            sc.G(e, e.tensor_scalar(sb["nrm"][:, 0:256], sb["nrm"][:, 0:256],
                                    1e-12, None, ALU.max)),
            e.reciprocal(sb["rn"][:, 0:256], sb["nrm"][:, 0:256]))[-1],
            inc=True, waits=[("s", t_n)])
        for mc in range(8):
            slot = 1 - (mc % 2)

            def bc(e, _mc=mc, _s=slot):
                return e.matmul(sb["pqk"][:, _s * 512:_s * 512 + 256],
                                sb["indT16"][:, _mc * 128:(_mc + 1) * 128],
                                sb["rn"][:, 0:256], start=True, stop=True)
            w = [("v", t_rn)]
            if mc >= 2:
                w.append(("v", t_nrm[(j, mc - 2)]))
            elif mc == 1 and j == 0:
                pass
            t_bc = sc.add("t", bc, inc=True, waits=w)
            t_nrm[(j, mc)] = sc.add(
                "v", lambda e, _src=src, _d=dstb, _mc=mc, _s=slot: e.tensor_tensor(
                    sb[_d][:, _mc * 256:(_mc + 1) * 256],
                    sb[_src][:, _mc * 256:(_mc + 1) * 256],
                    sb["pqk"][:, _s * 512:_s * 512 + 256], ALU.mult),
                inc=True, waits=[("t", t_bc)])
    g["qn_v"] = t_nrm[(1, 7)]
    g["ssq_t"] = t_ssq_all

    t_esc_ev = {}
    t_esc_mm = {}
    for (j, (wbd, qsrc, dst)) in enumerate((("webd", "qnb", "esc"),
                                            ("wrbd", "knb", "rsc"))):
        wait_src = t_nrm[(0, 7)] if j == 0 else t_nrm[(1, 7)]
        for pair in range(4):
            slot = pair % 2

            def mm(e, _w=wbd, _q=qsrc, _p=pair, _s=slot):
                last = None
                for h2 in range(2):
                    mc = _p * 2 + h2
                    last = e.matmul(
                        sb["pqk"][0:64, _s * 512 + h2 * 256:_s * 512 + (h2 + 1) * 256],
                        sb[_w][:, mc * 128:mc * 128 + 64],
                        sb[_q][:, mc * 256:(mc + 1) * 256],
                        start=True, stop=True)
                return last
            w = [("v", wait_src)]
            if (j, pair) == (0, 0):
                w.append(("v", t_nrm[(1, 7)]))
            if pair >= 2:
                w.append(("v", t_esc_ev[(j, pair - 2)]))
            elif j == 1:
                w.append(("v", t_esc_ev[(0, 2 + pair)]))
            t_esc_mm[(j, pair)] = sc.add("t", mm, inc=True, waits=w)
            t_esc_ev[(j, pair)] = sc.add(
                "v", lambda e, _d=dst, _p=pair, _s=slot: e.tensor_copy(
                    sb[_d][0:64, _p * 512:(_p + 1) * 512],
                    sb["pqk"][0:64, _s * 512:_s * 512 + 512]),
                inc=True, waits=[("t", t_esc_mm[(j, pair)])])
    g["esc_t"] = t_esc_mm[(1, 3)]

    t_at_ev = {}
    t_at_mm = {}
    for mc in range(8):
        def mm(e, _mc=mc):
            last = None
            for par in range(2):
                rb = par * 32
                out = sb["pqk"][0:64, par * 512 + (_mc % 2) * 256:
                                par * 512 + (_mc % 2) * 256 + 256]
                for (wz, ssrc) in (("wcz_a", "esc"), ("wcz_b", "rsc")):
                    last = e.matmul(
                        out, sb[wz][rb:rb + 20, :],
                        sb[ssrc][rb:rb + 20, _mc * 256:(_mc + 1) * 256],
                        start=(wz == "wcz_a"), stop=(wz == "wcz_b"))
            return last
        w = [("v", t_esc_ev[(0, mc // 2)]), ("v", t_esc_ev[(1, mc // 2)])]
        if mc == 0:
            w.append(("v", t_esc_ev[(1, 2)]))
            w.append(("v", t_esc_ev[(1, 3)]))
        else:
            w.append(("v", t_at_ev[mc - 1]))
        if mc == 0 and pg:
            w.append((sc.dg, pg["odd_dma"]))
        t_at_mm[mc] = sc.add("t", mm, inc=True, waits=w)

        def ev(e, _mc=mc):
            e.tensor_scalar(sb["attnf"][0:64, _mc * 256:(_mc + 1) * 256],
                            sb["pqk"][0:64, (_mc % 2) * 256:(_mc % 2) * 256 + 256],
                            sb["bc2"][0:64, 0:1], None, ALU.add)
            return e.tensor_scalar(
                sb["odds"][0:64, _mc * 256:(_mc + 1) * 256],
                sb["pqk"][0:64, 512 + (_mc % 2) * 256:512 + (_mc % 2) * 256 + 256],
                sb["bc2"][0:64, 0:1], None, ALU.add)
        t_at_ev[mc] = sc.add("v", ev, inc=True, waits=[("t", t_at_mm[mc])])
    g["at_ev"] = t_at_ev[7]
    w_odd = [("v", t_at_ev[7])]
    if pg:
        w_odd.append(("t", pg["smm_t"]))
    t_odd = sc.dma("g", lambda e: e.dma_start(
        out=sb["attnf"][64:128, :], in_=sb["odds"][0:64, :]), waits=w_odd)
    g["odd_dma"] = t_odd

    t_s_ev = {}
    t_s_mm = {}
    for tc in range(2):
        def mm(e, _tc=tc):
            last = None
            for nc_ in range(2):
                for kc in range(8):
                    last = e.matmul(
                        sb["pbig"][:, nc_ * 512:(nc_ + 1) * 512],
                        sb["attnf"][:, kc * 256 + _tc * 128: kc * 256 + _tc * 128 + 128],
                        sb["wo"][:, kc * 1024 + nc_ * 512: kc * 1024 + (nc_ + 1) * 512],
                        start=(kc == 0), stop=(kc == 7))
            return last
        w = [(sc.dg, t_odd)]
        if tc == 1:
            w.append(("v", t_s_ev[0]))
        elif pg:
            w.append(("s", pg["ffn_s"]))
        t_s_mm[tc] = sc.add("t", mm, inc=True, waits=w)
        wse = [("t", t_s_mm[tc])]
        if tc == 0 and pg:
            wse.append((sc.dy, pg["x1_dma"]))
        t_s_ev[tc] = sc.add("v", lambda e, _tc=tc: e.tensor_tensor(
            sb["s_sb"][:, _tc * 1024:(_tc + 1) * 1024],
            sb["pbig"][:, 0:1024], sb["bob"][:, :], ALU.add),
            inc=True, waits=wse)
    g["smm_t"] = t_s_mm[1]

    t_q1 = quant_1024(sc, sb, src="xn", src_off=0, dst="xdb", dst_off=0,
                      wait_first=[("v", t_ln)] +
                      ([("t", pg["xd_tp_t"])] if pg else []))
    t_q2 = quant_1024(sc, sb, src="xn", src_off=1024, dst="xdb", dst_off=1024,
                      wait_first=[("v", t_q1)])
    xd_guard = {0: evt[2], 1: evt[3]}
    w_xd = [("v", t_q2), ("t", g["proj_t"])]
    xevt, t_xd_tp = transpose_16(sc, sb, "xdb", "xnT", w_xd, xd_guard)
    g["xd_ev2"], g["xd_ev3"] = xevt[2], xevt[3]
    g["xd_tp_t"] = t_xd_tp

    t_ff_s = None
    for tc in range(2):
        def mm(e, _tc=tc):
            last = None
            for nc_ in range(2):
                for ic in range(8):
                    last = e.matmul(
                        sb["pbig"][:, nc_ * 512:(nc_ + 1) * 512],
                        sb["xnT"][:, ic * 256 + _tc * 128: ic * 256 + _tc * 128 + 128],
                        sb["wdT"][:, ic * 1024 + nc_ * 512: ic * 1024 + (nc_ + 1) * 512],
                        start=(ic == 0), stop=(ic == 7))
            return last
        w = [("v", xevt[3])]
        w.append(("v", t_s_ev[1]) if tc == 0 else ("v", t_ff_v))
        t_mm = sc.add("t", mm, inc=True, waits=w)
        g["ffn_t"] = t_mm
        t_ff_v = sc.add("v", lambda e, _tc=tc: e.tensor_tensor(
            sb["g_sb"][:, _tc * 1024:(_tc + 1) * 1024],
            sb["pbig"][:, 0:1024], sb["fbb"][:, :], ALU.add),
            inc=True, waits=[("t", t_mm)])
        t_ff_s = sc.add("s", lambda e, _tc=tc: e.activation(
            sb["g_sb"][:, _tc * 1024:(_tc + 1) * 1024],
            sb["g_sb"][:, _tc * 1024:(_tc + 1) * 1024], AF.Silu),
            inc=True, waits=[("v", t_ff_v)])
    g["ffn_s"] = t_ff_s

    def gate(e):
        sc.G(e, e.tensor_tensor(sb["s_sb"][:, 0:2048], sb["s_sb"][:, 0:2048],
                                sb["g_sb"][:, 0:2048], ALU.mult))
        return e.tensor_tensor(sb["s_sb"][:, 0:2048], sb["s_sb"][:, 0:2048],
                               sb["sx"][:, off:off + 2048], ALU.add)
    w_gate = [("s", t_ff_s)]
    if pg:
        w_gate.append((sc.dy, pg["x1_dma"]))
    t_gate = sc.add("v", gate, inc=True, waits=w_gate)
    g["gate"] = t_gate
    g["x1_dma"] = sc.dma("y", lambda e, _i=i: e.dma_start(
        out=x1_out.rearrange("(i b p) f -> i p b f", b=2, p=128)[_i],
        in_=sb["s_sb"][:, 0:2048].rearrange("p (b f) -> p b f", b=2)),
        waits=[("v", t_gate)])
    return g


def _np32(a):
    return np.ascontiguousarray(np.asarray(a, dtype=np.float32))


def _l0_inmaps(inputs):
    x = _np32(inputs["L0_attn_x"])
    Wq = _np32(inputs["L0_attn_Wq"]); Wk = _np32(inputs["L0_attn_Wk"])
    Wo = _np32(inputs["L0_attn_Wo"])
    We = _np32(inputs["L0_attn_We"]); Wr = _np32(inputs["L0_attn_Wr"])
    Wc = _np32(inputs["L0_attn_Wc"])
    bq = _np32(inputs["L0_attn_bq"]); bk = _np32(inputs["L0_attn_bk"])
    bc = _np32(inputs["L0_attn_bc"]); bo = _np32(inputs["L0_attn_bo"])
    fw = _np32(inputs["L0_ffn_weight"]); fb = _np32(inputs["L0_ffn_bias"])

    wep = np.zeros((256, H * LOW), np.float32)
    wep[:LEN] = We.transpose(1, 0, 2).reshape(LEN, H * LOW)
    wrp = np.zeros((256, H * LOW), np.float32)
    wrp[:LEN] = Wr.transpose(1, 0, 2).reshape(LEN, H * LOW)
    wcz_a = np.zeros((64, DK), np.float32)
    wcz_a[0:20] = Wc[0:20]; wcz_a[32:52] = Wc[0:20]
    wcz_b = np.zeros((64, DK), np.float32)
    wcz_b[0:20] = Wc[20:40]; wcz_b[32:52] = Wc[20:40]
    shared = dict(
        wq=Wq, wk=Wk, wo=Wo, wep=wep, wrp=wrp, wcz_a=wcz_a, wcz_b=wcz_b,
        bqc=np.ascontiguousarray(bq.reshape(8, 128).T),
        bkc=np.ascontiguousarray(bk.reshape(8, 128).T),
        bc2=np.ascontiguousarray(np.tile(bc, 2)[:, None]),
        bob=np.ascontiguousarray(np.broadcast_to(bo, (128, D))),
        fbb=np.ascontiguousarray(np.broadcast_to(fb, (128, D))),
        fw=fw, ident=CONSTS["ident"], ind16=CONSTS["ind16"],
        indT16=CONSTS["indT16"],
    )
    maps = []
    for c in range(8):
        b, hh = c // 2, c % 2
        m = dict(shared)
        m["x"] = np.ascontiguousarray(x[b, hh * NT:(hh + 1) * NT])
        xs = np.zeros((256, D), np.float32)
        xs[:LEN] = x[b, IDX200]
        m["xs"] = xs
        maps.append(m)
    return maps


def build_l1a():
    nc = bass.Bass("TRN2", target_bir_lowering=False)
    ap = {}
    for name, shape, dt in [
        ("x1", [NT, D], F32),
        ("wqT", [D, D], F32), ("wkT", [D, D], F32), ("wvT", [D, D], F32),
        ("bqc", [128, 8], F32),
        ("bkb", [128, D], F32), ("bvb", [128, D], F32),
        ("wg", [D, 8], F32), ("bgb", [128, 8], F32),
        ("ident", [128, 128], BF16), ("ones128", [128, 1], BF16),
    ]:
        ap[name] = nc.dram_tensor(name, shape, dt, kind="ExternalInput").ap()
    o_xnf = nc.dram_tensor("xnf", [128, 8 * NT], BF16, kind="ExternalOutput").ap()
    o_qh = nc.dram_tensor("qh", [128, 8 * NT], BF16, kind="ExternalOutput").ap()
    o_kvp = nc.dram_tensor("kvp", [64, 1040], F32, kind="ExternalOutput").ap()
    o_w = nc.dram_tensor("wgate", [128, 32 * 8], F32, kind="ExternalOutput").ap()

    with contextlib.ExitStack() as stack:
        sb = {}

        def S(name, shape, dt=F32):
            sb[name] = stack.enter_context(nc.sbuf_tensor("sb_" + name, shape, dt))

        def P(name, shape, dt=F32):
            sb[name] = stack.enter_context(nc.psum_tensor("ps_" + name, shape, dt))

        S("wqT", [128, 8 * 1024], BF16); S("wkT", [128, 8 * 1024], BF16)
        S("wvT", [128, 8 * 1024], BF16)
        S("wg", [128, 8 * 8], BF16)
        S("bqc", [128, 8]); S("bkb", [128, D]); S("bvb", [128, D])
        S("bgb", [128, 8])
        S("ident", [128, 128], BF16); S("ones128", [128, 1], BF16)
        S("st", [128, 16]); S("sqx", [128, 2048])
        S("sx", [128, 2048])
        S("xn", [128, 2048]); S("xnb", [128, 2048], BF16)
        S("xnT", [128, 8 * 256], BF16)
        S("qf", [128, 8 * 256])
        S("ze", [128, 8 * 256])
        S("qeb", [128, 8 * 256], BF16)
        S("kb", [128, 2 * 1024], BF16); S("vb", [128, 2 * 1024], BF16)
        S("ktmp", [128, 1024]); S("ktm2", [128, 1024])
        S("lg", [128, 32 * 8])
        S("pr", [128, 32 * 8])
        S("cnt", [128, 32 * 8])
        S("tc1", [128, 32]); S("tc2", [128, 32])
        S("kvs", [64, 1040])
        P("ptp", [128, 1024], BF16)
        P("pqk", [128, 2 * 512])
        P("pbig", [128, 1024])
        P("pkv", [64, 1024])
        P("pks", [64, 16])

        sc = Sched(stack, nc)
        _emit_l1a(nc, sc, sb, ap, o_xnf, o_qh, o_kvp, o_w)
    return nc


def _emit_l1a(nc, sc, sb, ap, o_xnf, o_qh, o_kvp, o_w):
    with nc.Block() as blk:
        for f in [
            lambda e: e.dma_start(out=sb["bqc"][:, :], in_=ap["bqc"]),
            lambda e: e.dma_start(out=sb["bkb"][:, :], in_=ap["bkb"]),
            lambda e: e.dma_start(out=sb["bvb"][:, :], in_=ap["bvb"]),
            lambda e: e.dma_start(out=sb["bgb"][:, :], in_=ap["bgb"]),
            lambda e: e.dma_start(out=sb["ident"][:, :], in_=ap["ident"]),
            lambda e: e.dma_start(out=sb["ones128"][:, :], in_=ap["ones128"]),
        ]:
            t_ldf = sc.dma("y", f)
        for nm in ("wqT", "wkT", "wvT"):
            t_ldc = sc.dma("g", lambda e, _n=nm: e.dma_start(
                out=sb[_n][:, :].rearrange("p (kc n) -> p kc n", kc=8),
                in_=ap[_n].rearrange("(kc p) n -> p kc n", p=128)))
        t_ldc = sc.dma("g", lambda e: e.dma_start(
            out=sb["wg"][:, :].rearrange("p (kc n) -> p kc n", kc=8),
            in_=ap["wg"].rearrange("(kc p) n -> p kc n", p=128)))
        sc.wait("y", [(sc.dy, t_ldf)])
        sc.wait("g", [(sc.dg, t_ldc)])
        sc.wait("v", [(sc.dy, t_ldf), (sc.dg, t_ldc)])
        sc.emit(blk)

    with nc.Block() as blk:
        guards = {}
        for i in range(NTILE):
            w_load = []
            if i - 1 in guards:
                w_load.append(("v", guards[i - 1]["done"]))
            t_x = sc.dma("y", lambda e, _i=i: e.dma_start(
                out=sb["sx"][:, :].rearrange("p (b f) -> p b f", b=2),
                in_=ap["x1"].rearrange("(i b p) f -> i p b f", b=2, p=128)[_i]),
                waits=w_load)
            guards[i] = _l1a_tile(nc, sc, sb, ap, i, t_x, o_xnf, o_qh,
                                  guards.get(i - 1))
        t_kvev = sc.add("v", lambda e: (
            sc.G(e, e.tensor_copy(sb["kvs"][0:64, 0:1024], sb["pkv"][:, :])),
            e.tensor_copy(sb["kvs"][0:64, 1024:1040], sb["pks"][:, :]))[-1],
            inc=True, waits=[("t", guards[NTILE - 1]["kv_t"])])
        sc.dma("y", lambda e: e.dma_start(out=o_kvp, in_=sb["kvs"][0:64, :]),
               waits=[("v", t_kvev)])
        _emit_gate(nc, sc, sb, o_w, wait=("v", guards[NTILE - 1]["lg_v"]))
        sc.wait("y", [(sc.dy, sc.dy.n)])
        sc.wait("g", [(sc.dg, sc.dg.n)])
        sc.emit(blk)


def _l1a_tile(nc, sc, sb, ap, i, t_x, o_xnf, o_qh, pg):
    g = {}
    extra = []
    if pg:
        extra.append(("t", pg["lastuse_t"]))
    t_ln = ln_tile(sc, sb, src="sx", n_blk=2, out_f32="xn", out_bf16="xnb",
                   waits=[(sc.dy, t_x)], src_off=0, extra_fin_waits=extra)
    tp_guard = {0: pg["tp_ev3"] if pg else None}
    w0 = [("v", t_ln)]
    if pg:
        w0.append(("t", pg["lastuse_t"]))
        w0.append((sc.dy, pg["xnf_dma"]))
    evt, t_tp_last = transpose_16(sc, sb, "xnb", "xnT", w0, tp_guard, nslots=1)
    g["tp_ev3"] = evt[3]
    t_xnf = sc.dma("y", lambda e, _i=i: e.dma_start(
        out=o_xnf.rearrange("p (kc i t) -> p kc i t", kc=8, i=NTILE)[:, :, _i, :],
        in_=sb["xnT"][:, :].rearrange("p (kc t) -> p kc t", kc=8)),
        waits=[("v", evt[3])])
    g["xnf_dma"] = t_xnf

    t_q_ev = {}
    for mc in range(8):
        slot = mc % 2

        def mm(e, _mc=mc, _s=slot):
            last = None
            for kc in range(8):
                last = e.matmul(
                    sb["pqk"][:, _s * 512:_s * 512 + 256],
                    sb["wqT"][:, kc * 1024 + _mc * 128: kc * 1024 + (_mc + 1) * 128],
                    sb["xnT"][:, kc * 256:(kc + 1) * 256],
                    start=(kc == 0), stop=(kc == 7))
            return last
        w = [("v", evt[3])]
        if mc >= 2:
            w.append(("v", t_q_ev[mc - 2]))
        elif pg:
            w.append(("v", pg["q_done"]))
        t_mm = sc.add("t", mm, inc=True, waits=w)
        wev = [("t", t_mm)]
        if pg and mc < 2:
            wev.append((sc.dy, pg["qh_dma"]))
        t_ev = sc.add("v", lambda e, _mc=mc, _s=slot: (
            sc.G(e, e.tensor_scalar(sb["qf"][:, _mc * 256:(_mc + 1) * 256],
                                    sb["pqk"][:, _s * 512:_s * 512 + 256],
                                    sb["bqc"][:, _mc:_mc + 1], None, ALU.add)),
            e.tensor_scalar(sb["ze"][:, _mc * 256:(_mc + 1) * 256],
                            sb["qf"][:, _mc * 256:(_mc + 1) * 256],
                            0.0, None, ALU.min))[-1],
            inc=True, waits=wev)
        t_ex = sc.add("s", lambda e, _mc=mc: e.activation(
            sb["ze"][:, _mc * 256:(_mc + 1) * 256],
            sb["ze"][:, _mc * 256:(_mc + 1) * 256], AF.Exp),
            inc=True, waits=[("v", t_ev)])
        t_q_ev[mc] = sc.add("v", lambda e, _mc=mc: e.scalar_tensor_tensor(
            sb["qeb"][:, _mc * 256:(_mc + 1) * 256],
            sb["qf"][:, _mc * 256:(_mc + 1) * 256], 0.0,
            sb["ze"][:, _mc * 256:(_mc + 1) * 256],
            ALU.max, ALU.add), inc=True, waits=[("s", t_ex)])
    g["q_done"] = t_q_ev[7]
    t_qh = sc.dma("y", lambda e, _i=i: e.dma_start(
        out=o_qh.rearrange("p (mc i t) -> p mc i t", mc=8, i=NTILE)[:, :, _i, :],
        in_=sb["qeb"][:, :].rearrange("p (mc t) -> p mc t", mc=8)),
        waits=[("v", t_q_ev[7])])
    g["qh_dma"] = t_qh

    t_kv_ready = {}
    for (j, (wname, bname, oname)) in enumerate((("wkT", "bkb", "kb"),
                                                 ("wvT", "bvb", "vb"))):
        for tc in range(2):
            def mm(e, _w=wname, _tc=tc):
                last = None
                for nc_ in range(2):
                    for kc in range(8):
                        last = e.matmul(
                            sb["pbig"][:, nc_ * 512:(nc_ + 1) * 512],
                            sb["xnT"][:, kc * 256 + _tc * 128: kc * 256 + _tc * 128 + 128],
                            sb[_w][:, kc * 1024 + nc_ * 512: kc * 1024 + (nc_ + 1) * 512],
                            start=(kc == 0), stop=(kc == 7))
                return last
            idx = j * 2 + tc
            w = [("v", evt[3])]
            if idx == 0:
                if pg:
                    w.append(("t", pg["kv_t"]))
                    w.append(("v", pg["kv_done"]))
            else:
                w.append(("v", t_kv_ready[idx - 1]))
            t_mm = sc.add("t", mm, inc=True, waits=w)
            if j == 0:
                t_ev = sc.add("v", lambda e, _tc=tc: (
                    sc.G(e, e.tensor_tensor(sb["ktmp"][:, :],
                                            sb["pbig"][:, 0:1024],
                                            sb["bkb"][:, :], ALU.add)),
                    e.tensor_scalar(sb["ktm2"][:, :], sb["ktmp"][:, :],
                                    0.0, None, ALU.min))[-1],
                    inc=True, waits=[("t", t_mm)])
                t_ex = sc.add("s", lambda e: e.activation(
                    sb["ktm2"][:, :], sb["ktm2"][:, :], AF.Exp),
                    inc=True, waits=[("v", t_ev)])
                t_kv_ready[idx] = sc.add("v", lambda e, _tc=tc: e.scalar_tensor_tensor(
                    sb["kb"][:, _tc * 1024:(_tc + 1) * 1024],
                    sb["ktmp"][:, :], 0.0, sb["ktm2"][:, :],
                    ALU.max, ALU.add), inc=True, waits=[("s", t_ex)])
            else:
                t_kv_ready[idx] = sc.add("v", lambda e, _tc=tc: e.tensor_tensor(
                    sb["vb"][:, _tc * 1024:(_tc + 1) * 1024],
                    sb["pbig"][:, 0:1024], sb["bvb"][:, :], ALU.add),
                    inc=True, waits=[("t", t_mm)])
    g["kv_done"] = t_kv_ready[3]

    def kvacc(e):
        last = None
        for tc in range(2):
            for h in range(H):
                first = (i == 0 and tc == 0 and h in (0, 8))
                firstks = (i == 0 and tc == 0 and h == 0)
                lastmm = (i == NTILE - 1 and tc == 1)
                last = e.matmul(
                    sb["pkv"][:, h * 64:(h + 1) * 64],
                    sb["kb"][:, tc * 1024 + h * 64: tc * 1024 + (h + 1) * 64],
                    sb["vb"][:, tc * 1024 + h * 64: tc * 1024 + (h + 1) * 64],
                    start=first, stop=lastmm, skip_group_check=True)
                last = e.matmul(
                    sb["pks"][:, h:h + 1],
                    sb["kb"][:, tc * 1024 + h * 64: tc * 1024 + (h + 1) * 64],
                    sb["ones128"][:, :],
                    start=firstks, stop=(lastmm and h == H - 1),
                    skip_group_check=True)
        return last
    g["kv_t"] = sc.add("t", kvacc, inc=True,
                       waits=[("v", t_kv_ready[1]), ("v", t_kv_ready[3])])

    def gmm(e):
        last = None
        for tc in range(2):
            for kc in range(8):
                last = e.matmul(
                    sb["pqk"][0:128, 0:8] if False else
                    sb["pqk"][:, tc * 512:tc * 512 + 8],
                    sb["xnT"][:, kc * 256 + tc * 128: kc * 256 + tc * 128 + 128],
                    sb["wg"][:, kc * 8:(kc + 1) * 8],
                    start=(kc == 0), stop=(kc == 7))
        return last
    t_g_mm = sc.add("t", gmm, inc=True,
                    waits=[("v", t_q_ev[7]), ("v", t_q_ev[6])])
    g["lg_v"] = sc.add("v", lambda e, _i=i: (
        sc.G(e, e.tensor_tensor(sb["lg"][:, (2 * _i) * 8:(2 * _i + 1) * 8],
                                sb["pqk"][:, 0:8], sb["bgb"][:, :], ALU.add)),
        e.tensor_tensor(sb["lg"][:, (2 * _i + 1) * 8:(2 * _i + 2) * 8],
                        sb["pqk"][:, 512:520], sb["bgb"][:, :], ALU.add))[-1],
        inc=True, waits=[("t", t_g_mm)])
    g["lastuse_t"] = t_g_mm
    g["done"] = g["lg_v"]
    return g


def _emit_gate(nc, sc, sb, o_w, wait):
    lg3 = sb["lg"][:, :].rearrange("p (b e) -> p b e", b=32)
    pr3 = sb["pr"][:, :].rearrange("p (b e) -> p b e", b=32)

    def soft1(e):
        sc.G(e, e.reduce_max(sb["tc1"][:, :], lg3, AX.X))
        sc.G(e, e.tensor_tensor(pr3, lg3,
                                sb["tc1"][:, :].unsqueeze(2).broadcast_to([128, 32, 8]),
                                ALU.subtract))
        return e.engine_nop()
    t1 = sc.add("v", soft1, inc=True, waits=[wait])
    t2 = sc.add("s", lambda e: e.activation(sb["pr"][:, :], sb["pr"][:, :], AF.Exp),
                inc=True, waits=[("v", t1)])

    def soft2(e):
        sc.G(e, e.reduce_sum(sb["tc1"][:, :], pr3, AX.X))
        sc.G(e, e.reciprocal(sb["tc2"][:, :], sb["tc1"][:, :]))
        return e.tensor_tensor(pr3, pr3,
                               sb["tc2"][:, :].unsqueeze(2).broadcast_to([128, 32, 8]),
                               ALU.mult)
    t3 = sc.add("v", soft2, inc=True, waits=[("s", t2)])

    def top2(e):
        for j in range(E):
            prj = pr3[:, :, j:j + 1].squeeze(2)
            first = True
            for ii in range(E):
                if ii == j:
                    continue
                op = ALU.is_ge if ii < j else ALU.is_gt
                pri = pr3[:, :, ii:ii + 1].squeeze(2)
                if first:
                    sc.G(e, e.tensor_tensor(
                        sb["cnt"][:, :].rearrange("p (b e) -> p b e", b=32)[:, :, j:j + 1].squeeze(2),
                        pri, prj, op))
                    first = False
                else:
                    sc.G(e, e.tensor_tensor(sb["tc1"][:, :], pri, prj, op))
                    sc.G(e, e.tensor_tensor(
                        sb["cnt"][:, :].rearrange("p (b e) -> p b e", b=32)[:, :, j:j + 1].squeeze(2),
                        sb["cnt"][:, :].rearrange("p (b e) -> p b e", b=32)[:, :, j:j + 1].squeeze(2),
                        sb["tc1"][:, :], ALU.add))
        sc.G(e, e.tensor_scalar(sb["cnt"][:, :], sb["cnt"][:, :], 2.0, None, ALU.is_lt))
        sc.G(e, e.tensor_tensor(sb["pr"][:, :], sb["pr"][:, :], sb["cnt"][:, :], ALU.mult))
        sc.G(e, e.reduce_sum(sb["tc1"][:, :], pr3, AX.X))
        sc.G(e, e.tensor_scalar(sb["tc1"][:, :], sb["tc1"][:, :], 1e-8, None, ALU.add))
        sc.G(e, e.reciprocal(sb["tc2"][:, :], sb["tc1"][:, :]))
        return e.tensor_tensor(pr3, pr3,
                               sb["tc2"][:, :].unsqueeze(2).broadcast_to([128, 32, 8]),
                               ALU.mult)
    t4 = sc.add("v", top2, inc=True, waits=[("v", t3)])
    sc.dma("y", lambda e: e.dma_start(out=o_w, in_=sb["pr"][:, :]),
           waits=[("v", t4)])


def build_moe():
    nc = bass.Bass("TRN2", target_bir_lowering=False)
    ap = {}
    for name, shape, dt in [
        ("xnf", [128, 8 * NT], BF16),
        ("wgate", [128, 32 * 8], F32),
        ("w1", [E, D, HDIM], F32), ("w2", [E, HDIM, D], F32),
        ("b1c", [128, E * 32], F32), ("b2r", [1, E * D], F32),
    ]:
        ap[name] = nc.dram_tensor(name, shape, dt, kind="ExternalInput").ap()
    o_g = nc.dram_tensor("g", [NT, D], F32, kind="ExternalOutput").ap()

    with contextlib.ExitStack() as stack:
        sb = {}

        def S(name, shape, dt=F32):
            sb[name] = stack.enter_context(nc.sbuf_tensor("sb_" + name, shape, dt))

        def P(name, shape, dt=F32):
            sb[name] = stack.enter_context(nc.psum_tensor("ps_" + name, shape, dt))

        S("xnf", [128, 8 * NT], BF16)
        S("gh", [128, 16 * 1024])
        S("w1a", [128, 8 * 512], BF16); S("w1b", [128, 8 * 512], BF16)
        S("w2a", [128, 4 * 1024], BF16); S("w2b", [128, 4 * 1024], BF16)
        S("hb", [128, 2 * 256], BF16)
        S("wgate", [128, 32 * 8])
        S("b1c", [128, E * 32]); S("b2r", [1, E * D], BF16)
        S("onesb", [1, 128], BF16)
        P("ph", [128, 1024])
        P("pgA", [128, 1024]); P("pgB", [128, 1024])

        sc = Sched(stack, nc)
        _emit_moe(nc, sc, sb, ap, o_g)
    return nc


def _emit_moe(nc, sc, sb, ap, o_g):
    with nc.Block() as blk:
        sc.dma("g", lambda e: e.dma_start(
            out=sb["xnf"][:, :], in_=ap["xnf"]))
        sc.dma("y", lambda e: e.dma_start(out=sb["wgate"][:, :], in_=ap["wgate"]))
        t1 = sc.dma("y", lambda e: e.dma_start(out=sb["b1c"][:, :], in_=ap["b1c"]))
        t0 = sc.dma("g", lambda e: e.dma_start(out=sb["b2r"][:, :], in_=ap["b2r"]))
        t2 = sc.add("g", lambda e: e.memset(sb["onesb"][:, :], 1.0), inc=True)
        sc.wait("y", [(sc.dy, t1), (sc.dg, t0)])
        sc.wait("g", [(sc.dy, t1)])
        sc.emit(blk)

    for half in range(2):
        with nc.Block() as blk:
            _emit_moe_half(nc, sc, sb, ap, o_g, half)
            sc.emit(blk)


def _emit_moe_half(nc, sc, sb, ap, o_g, half):
    bufs = [("w1a", "w2a"), ("w1b", "w2b")]
    t_wload = {}
    t_lastuse = {}
    units = [(ei, s) for ei in range(E) for s in range(8)]

    def load_unit(uidx):
        ei, s = units[uidx]
        w1b, w2b = bufs[uidx % 2]
        w = []
        if uidx - 2 >= 0:
            w.append(("t", t_lastuse[uidx - 2]))
        t_a = sc.dma("g", lambda e, _e=ei, _s=s, _b=w1b: e.dma_start(
            out=sb[_b][:, :].rearrange("p (kc j) -> p kc j", kc=8),
            in_=ap["w1"][_e][:, _s * 512:(_s + 1) * 512]
            .rearrange("(kc p) j -> p kc j", p=128)), waits=w)
        t_b = sc.dma("g", lambda e, _e=ei, _s=s, _b=w2b: e.dma_start(
            out=sb[_b][:, :].rearrange("p (jc f) -> p jc f", jc=4),
            in_=ap["w2"][_e][_s * 512:(_s + 1) * 512, :]
            .rearrange("(jc p) f -> p jc f", p=128)))
        t_wload[uidx] = max(t_a, t_b)

    load_unit(0)
    load_unit(1)
    t_stt = {}
    t_gelu = {}
    for uidx, (ei, s) in enumerate(units):
        w1b, w2b = bufs[uidx % 2]
        last_mm = None
        for tt in range(8):
            for jc in range(4):
                slot = jc % 2

                def firstmm(e, _s=s, _tt=tt, _jc=jc, _sl=slot, _w1=w1b, _h=half):
                    last = None
                    for kc in range(8):
                        last = e.matmul(
                            sb["ph"][:, _sl * 512:_sl * 512 + 256],
                            sb[_w1][:, kc * 512 + _jc * 128: kc * 512 + (_jc + 1) * 128],
                            sb["xnf"][:, :].rearrange("p (kc t) -> p kc t", kc=8)
                            [:, kc, _h * 2048 + _tt * 256: _h * 2048 + (_tt + 1) * 256],
                            start=(kc == 0), stop=(kc == 7))
                    return last
                w = [(sc.dg, t_wload[uidx])]
                stepid = tt * 4 + jc
                if stepid >= 2:
                    w.append(("s", t_gelu[(uidx, stepid - 2)]))
                elif uidx >= 1:
                    w.append(("s", t_gelu[(uidx - 1, 31 - (1 - stepid))]))
                t_fm = sc.add("t", firstmm, inc=True, waits=w)
                t_gelu[(uidx, stepid)] = sc.add(
                    "s", lambda e, _e=ei, _s=s, _jc=jc, _sl=slot: e.activation(
                        sb["hb"][:, _sl * 256:(_sl + 1) * 256],
                        sb["ph"][:, _sl * 512:_sl * 512 + 256],
                        AF.Gelu_apprx_tanh,
                        bias=sb["b1c"][:, _e * 32 + _s * 4 + _jc: _e * 32 + _s * 4 + _jc + 1]),
                    inc=True, waits=[("t", t_fm)])

                def secondmm(e, _s=s, _jc=jc, _sl=slot, _w2=w2b):
                    last = None
                    for tc in range(2):
                        pg = sb["pgA"] if tc == 0 else sb["pgB"]
                        for fc in range(2):
                            last = e.matmul(
                                pg[:, fc * 512:(fc + 1) * 512],
                                sb["hb"][:, _sl * 256 + tc * 128: _sl * 256 + tc * 128 + 128],
                                sb[_w2][:, _jc * 1024 + fc * 512: _jc * 1024 + (fc + 1) * 512],
                                start=(_jc == 0 and _s > 0), stop=(_jc == 3),
                                skip_group_check=True)
                    return last
                w2w = [("s", t_gelu[(uidx, stepid)])]
                if jc == 0:
                    wfree = []
                    if (uidx, tt - 1, 1) in t_stt:
                        wfree.append(("v", t_stt[(uidx, tt - 1, 1)]))
                    elif uidx >= 1 and (uidx - 1, 7, 1) in t_stt:
                        wfree.append(("v", t_stt[(uidx - 1, 7, 1)]))
                    if s == 0:
                        def b2mm(e, _e=ei):
                            last = None
                            for tc in range(2):
                                pg = sb["pgA"] if tc == 0 else sb["pgB"]
                                for fc in range(2):
                                    last = e.matmul(
                                        pg[:, fc * 512:(fc + 1) * 512],
                                        sb["onesb"][:, :],
                                        sb["b2r"][:, _e * 1024 + fc * 512: _e * 1024 + (fc + 1) * 512],
                                        start=True, stop=False, skip_group_check=True)
                            return last
                        sc.add("t", b2mm, inc=False, waits=wfree)
                    else:
                        w2w = w2w + wfree
                last_mm = sc.add("t", secondmm, inc=(jc == 3), waits=w2w)
            for tc in range(2):
                blkid = half * 16 + tt * 2 + tc
                first_unit = (uidx == 0)

                def stt(e, _u=uidx, _tt=tt, _tc=tc, _b=blkid, _e=ei, _fu=first_unit):
                    pg = sb["pgA"] if _tc == 0 else sb["pgB"]
                    dst = sb["gh"][:, (_tt * 2 + _tc) * 1024:(_tt * 2 + _tc + 1) * 1024]
                    wcol = sb["wgate"][:, _b * 8 + _e: _b * 8 + _e + 1]
                    if _fu:
                        return e.tensor_scalar(dst, pg[:, 0:1024], wcol, None,
                                               ALU.mult)
                    return e.scalar_tensor_tensor(dst, pg[:, 0:1024], wcol,
                                                  dst, ALU.mult, ALU.add)
                t_stt[(uidx, tt, tc)] = sc.add("v", stt, inc=True,
                                               waits=[("t", last_mm)])
        t_lastuse[uidx] = last_mm
        if uidx + 2 < len(units):
            load_unit(uidx + 2)
    for chunk in range(4):
        t_si = sc.add("s", lambda e, _c=chunk: e.activation(
            sb["gh"][:, _c * 4096:(_c + 1) * 4096],
            sb["gh"][:, _c * 4096:(_c + 1) * 4096], AF.Silu),
            inc=True,
            waits=[("v", t_stt[(63, chunk * 2, 0)]),
                   ("v", t_stt[(63, chunk * 2, 1)]),
                   ("v", t_stt[(63, chunk * 2 + 1, 0)]),
                   ("v", t_stt[(63, chunk * 2 + 1, 1)])])
        sc.dma("y", lambda e, _c=chunk, _h=half: e.dma_start(
            out=o_g.rearrange("(hh c b p) f -> hh c p b f", hh=2, c=4, p=128)[_h, _c],
            in_=sb["gh"][:, _c * 4096:(_c + 1) * 4096]
            .rearrange("p (b f) -> p b f", b=4)),
            waits=[("s", t_si)])
    sc.wait("y", [(sc.dy, sc.dy.n)])
    sc.wait("g", [(sc.dg, sc.dg.n)])


def build_fin(debug=False):
    nc = bass.Bass("TRN2", target_bir_lowering=False)
    ap = {}
    for name, shape, dt in [
        ("x1", [NT, D], F32), ("qh", [128, 8 * NT], BF16),
        ("g", [NT, D], F32),
        ("kvp0", [64, 1040], F32), ("kvp1", [64, 1040], F32),
        ("woT", [D, D], F32), ("bo_b", [128, D], F32),
        ("indT16", [16, 1024], F32),
    ]:
        ap[name] = nc.dram_tensor(name, shape, dt, kind="ExternalInput").ap()
    o_out = nc.dram_tensor("out", [NT, D], F32, kind="ExternalOutput").ap()

    with contextlib.ExitStack() as stack:
        sb = {}

        def S(name, shape, dt=F32):
            sb[name] = stack.enter_context(nc.sbuf_tensor("sb_" + name, shape, dt))

        def P(name, shape, dt=F32):
            sb[name] = stack.enter_context(nc.psum_tensor("ps_" + name, shape, dt))

        S("woT", [128, 8 * 1024], BF16)
        S("bo_b", [128, D]); S("indT16", [16, 1024])
        S("kv0", [64, 1040]); S("kv1", [64, 1040])
        S("kv2", [128, 1040], BF16)
        S("ksi", [128, 16 * 16], BF16)
        S("qt", [128, 8 * 256], BF16)
        S("sx", [128, 2048]); S("gt", [128, 2048])
        S("attnf", [128, 8 * 256], BF16); S("odds", [64, 8 * 256], BF16)
        S("rden", [16, 256]); S("bcs", [128, 2 * 256])
        S("s_sb", [128, 2048])
        P("pnum", [128, 2 * 512])
        P("pnumO", [64, 2 * 512])
        P("pdenA", [16, 256]); P("pdenB", [16, 256])
        P("pbig", [128, 1024])

        sc = Sched(stack, nc)
        _emit_fin(nc, sc, sb, ap, o_out)
        if debug:
            with nc.Block() as blk:
                for nm in ("kv2", "ksi", "rden", "attnf", "odds", "bcs",
                           "s_sb", "qt", "gt", "sx"):
                    t = sb[nm]
                    o = nc.dram_tensor("dbg_" + nm, list(t.ap().shape),
                                       t.ap().dtype, kind="ExternalOutput").ap()
                    sc.dma("y", lambda e, _t=t, _o=o: e.dma_start(out=_o, in_=_t[:, :]))
                sc.wait("y", [(sc.dy, sc.dy.n)])
                sc.emit(blk)
    return nc


def _emit_fin(nc, sc, sb, ap, o_out):
    with nc.Block() as blk:
        t1 = sc.dma("y", lambda e: e.dma_start(out=sb["kv0"][0:64, :], in_=ap["kvp0"]))
        t2 = sc.dma("y", lambda e: e.dma_start(out=sb["kv1"][0:64, :], in_=ap["kvp1"]))
        sc.dma("y", lambda e: e.dma_start(out=sb["bo_b"][:, :], in_=ap["bo_b"]))
        t3 = sc.dma("y", lambda e: e.dma_start(out=sb["indT16"][:, :], in_=ap["indT16"]))
        t4 = sc.dma("g", lambda e: e.dma_start(
            out=sb["woT"][:, :].rearrange("p (kc n) -> p kc n", kc=8),
            in_=ap["woT"].rearrange("(kc p) n -> p kc n", p=128)))
        t5 = sc.add("v", lambda e: (
            sc.G(e, e.tensor_tensor(sb["kv0"][0:64, :], sb["kv0"][0:64, :],
                                    sb["kv1"][0:64, :], ALU.add)),
            e.tensor_copy(sb["kv2"][0:64, :], sb["kv0"][0:64, :]))[-1],
            inc=True, waits=[(sc.dy, max(t1, t2))])
        t6 = sc.add("g", lambda e: e.memset(sb["ksi"][:, :], 0.0), inc=True)
        t7 = sc.dma("g", lambda e: e.dma_start(
            out=sb["kv2"][64:128, :], in_=sb["kv2"][0:64, :]),
            waits=[("v", t5)])
        def ksi(e):
            last = None
            for h in range(H):
                last = e.tensor_copy(
                    sb["ksi"][0:64, h * 16 + h: h * 16 + h + 1],
                    sb["kv2"][0:64, 1024 + h: 1025 + h])
            return last
        t8 = sc.add("v", ksi, inc=True, waits=[("v", t5), ("g", t6)])
        t9 = sc.dma("g", lambda e: e.dma_start(
            out=sb["ksi"][64:128, :], in_=sb["ksi"][0:64, :]),
            waits=[("v", t8)])
        sc.wait("t", [(sc.dg, max(t4, t7, t9)), (sc.dy, t3)])
        sc.wait("y", [(sc.dy, sc.dy.n)])
        sc.wait("g", [(sc.dg, sc.dg.n)])
        sc.emit(blk)

    with nc.Block() as blk:
        guards = {}
        for i in range(NTILE):
            w_load = []
            if i - 1 in guards:
                w_load.append(("v", guards[i - 1]["done"]))
            t_q = sc.dma("y", lambda e, _i=i: e.dma_start(
                out=sb["qt"][:, :].rearrange("p (mc t) -> p mc t", mc=8),
                in_=ap["qh"].rearrange("p (mc i t) -> p mc i t", mc=8, i=NTILE)[:, :, _i, :]),
                waits=w_load)
            t_x = sc.dma("y", lambda e, _i=i: e.dma_start(
                out=sb["sx"][:, :].rearrange("p (b f) -> p b f", b=2),
                in_=ap["x1"].rearrange("(i b p) f -> i p b f", b=2, p=128)[_i]),
                waits=w_load)
            t_g = sc.dma("y", lambda e, _i=i: e.dma_start(
                out=sb["gt"][:, :].rearrange("p (b f) -> p b f", b=2),
                in_=ap["g"].rearrange("(i b p) f -> i p b f", b=2, p=128)[_i]),
                waits=w_load)
            guards[i] = _fin_tile(nc, sc, sb, ap, i, t_q, t_x, t_g, o_out,
                                  guards.get(i - 1))
        sc.wait("y", [(sc.dy, sc.dy.n)])
        sc.emit(blk)


def _fin_tile(nc, sc, sb, ap, i, t_q, t_x, t_g, o_out, pg):
    g = {}
    def denmm(e):
        last = None
        for h in range(H):
            hb = (h % 2) * 64
            pd = sb["pdenA"] if h % 2 == 0 else sb["pdenB"]
            last = e.matmul(
                pd[:, :],
                sb["ksi"][hb:hb + 64, h * 16:(h + 1) * 16],
                sb["qt"][hb:hb + 64, (h // 2) * 256:(h // 2 + 1) * 256],
                start=(h < 2), stop=(h >= H - 2))
        return last
    w = [(sc.dy, t_q)]
    if pg:
        w.append(("v", pg["rden_v"]))
    t_den = sc.add("t", denmm, inc=True, waits=w)
    t_rden = sc.add("v", lambda e: (
        sc.G(e, e.tensor_copy(sb["rden"][:, :], sb["pdenA"][:, :])),
        sc.G(e, e.tensor_tensor(sb["rden"][:, :], sb["pdenB"][:, :],
                                sb["rden"][:, :], ALU.add)),
        sc.G(e, e.tensor_scalar(sb["rden"][:, :], sb["rden"][:, :], 1e-6,
                                None, ALU.max)),
        e.reciprocal(sb["rden"][:, :], sb["rden"][:, :]))[-1],
        inc=True, waits=[("t", t_den)])
    g["rden_v"] = t_rden

    t_at = {}
    for mc in range(8):
        slot = mc % 2

        def mm(e, _mc=mc, _s=slot):
            last = None
            h0, h1 = 2 * _mc, 2 * _mc + 1
            e.matmul(sb["pnum"][0:64, _s * 512: _s * 512 + 256],
                     sb["kv2"][0:64, h0 * 64: h0 * 64 + 64],
                     sb["qt"][0:64, _mc * 256:(_mc + 1) * 256],
                     start=True, stop=True)
            e.matmul(sb["pnumO"][0:64, _s * 512: _s * 512 + 256],
                     sb["kv2"][64:128, h1 * 64: h1 * 64 + 64],
                     sb["qt"][64:128, _mc * 256:(_mc + 1) * 256],
                     start=True, stop=True)
            last = e.matmul(
                sb["pnum"][:, _s * 512 + 256: _s * 512 + 512],
                sb["indT16"][:, _mc * 128:(_mc + 1) * 128],
                sb["rden"][:, :], start=True, stop=True)
            return last
        w = [("v", t_rden)]
        if mc >= 2:
            w.append(("v", t_at[mc - 2]))
        elif pg:
            w.append(("v", pg["at_done"]))
            w.append((sc.dg, pg["odd_dma"]))
        t_mm = sc.add("t", mm, inc=True, waits=w)

        def ev(e, _mc=mc, _s=slot):
            sc.G(e, e.tensor_copy(sb["bcs"][:, _s * 256:(_s + 1) * 256],
                                  sb["pnum"][:, _s * 512 + 256:_s * 512 + 512]))
            sc.G(e, e.tensor_tensor(
                sb["attnf"][0:64, _mc * 256:(_mc + 1) * 256],
                sb["pnum"][0:64, _s * 512:_s * 512 + 256],
                sb["bcs"][0:64, _s * 256:(_s + 1) * 256], ALU.mult))
            return e.tensor_tensor(
                sb["odds"][0:64, _mc * 256:(_mc + 1) * 256],
                sb["pnumO"][0:64, _s * 512:_s * 512 + 256],
                sb["bcs"][64:128, _s * 256:(_s + 1) * 256], ALU.mult)
        t_at[mc] = sc.add("v", ev, inc=True, waits=[("t", t_mm)])
    g["at_done"] = t_at[7]
    w_odd = [("v", t_at[7])]
    if pg:
        w_odd.append(("t", pg["smm_t"]))
    t_odd = sc.dma("g", lambda e: e.dma_start(
        out=sb["attnf"][64:128, :], in_=sb["odds"][0:64, :]), waits=w_odd)
    g["odd_dma"] = t_odd

    t_s_ev = {}
    for tc in range(2):
        def mm(e, _tc=tc):
            last = None
            for nc_ in range(2):
                for kc in range(8):
                    last = e.matmul(
                        sb["pbig"][:, nc_ * 512:(nc_ + 1) * 512],
                        sb["attnf"][:, kc * 256 + _tc * 128: kc * 256 + _tc * 128 + 128],
                        sb["woT"][:, kc * 1024 + nc_ * 512: kc * 1024 + (nc_ + 1) * 512],
                        start=(kc == 0), stop=(kc == 7))
            return last
        w = [(sc.dg, t_odd)]
        if tc == 1:
            w.append(("v", t_s_ev[0]))
        elif pg:
            w.append(("v", pg["done"]))
        t_mm = sc.add("t", mm, inc=True, waits=w)
        g["smm_t"] = t_mm
        wse = [("t", t_mm)]
        if tc == 0 and pg:
            wse.append((sc.dy, pg["out_dma"]))
        t_s_ev[tc] = sc.add("v", lambda e, _tc=tc: e.tensor_tensor(
            sb["s_sb"][:, _tc * 1024:(_tc + 1) * 1024],
            sb["pbig"][:, 0:1024], sb["bo_b"][:, :], ALU.add),
            inc=True, waits=wse)

    def gate(e):
        sc.G(e, e.tensor_tensor(sb["s_sb"][:, 0:2048], sb["s_sb"][:, 0:2048],
                                sb["gt"][:, 0:2048], ALU.mult))
        return e.tensor_tensor(sb["s_sb"][:, 0:2048], sb["s_sb"][:, 0:2048],
                               sb["sx"][:, 0:2048], ALU.add)
    t_gate = sc.add("v", gate, inc=True,
                    waits=[("v", t_s_ev[1]), (sc.dy, t_x), (sc.dy, t_g)])
    g["done"] = t_gate
    g["out_dma"] = sc.dma("y", lambda e, _i=i: e.dma_start(
        out=o_out.rearrange("(i b p) f -> i p b f", b=2, p=128)[_i],
        in_=sb["s_sb"][:, 0:2048].rearrange("p (b f) -> p b f", b=2)),
        waits=[("v", t_gate)])
    return g


def _l1a_inmaps(inputs, x1s):
    Wq = _np32(inputs["L1_attn_Wq"]); Wk = _np32(inputs["L1_attn_Wk"])
    Wv = _np32(inputs["L1_attn_Wv"])
    bq = _np32(inputs["L1_attn_bq"]); bk = _np32(inputs["L1_attn_bk"])
    bv = _np32(inputs["L1_attn_bv"])
    Wg = _np32(inputs["L1_ffn_Wg"]); bg = _np32(inputs["L1_ffn_bg"])
    shared = dict(
        wqT=np.ascontiguousarray(Wq.T), wkT=np.ascontiguousarray(Wk.T),
        wvT=np.ascontiguousarray(Wv.T),
        bqc=np.ascontiguousarray(bq.reshape(8, 128).T),
        bkb=np.ascontiguousarray(np.broadcast_to(bk, (128, D))),
        bvb=np.ascontiguousarray(np.broadcast_to(bv, (128, D))),
        wg=Wg, bgb=np.ascontiguousarray(np.broadcast_to(bg, (128, 8))),
        ident=CONSTS["ident"],
        ones128=np.ones((128, 1), NPBF16),
    )
    return [dict(shared, x1=x1s[c]) for c in range(8)]


def _moe_inmaps(inputs, xnfs, wgates):
    W1 = _np32(inputs["L1_ffn_W1"]); W2 = _np32(inputs["L1_ffn_W2"])
    b1 = _np32(inputs["L1_ffn_b1"]); b2 = _np32(inputs["L1_ffn_b2"])
    b1c = np.ascontiguousarray(
        b1.reshape(E, 32, 128).transpose(2, 0, 1).reshape(128, E * 32))
    b2r = np.ascontiguousarray(b2.reshape(1, E * D))
    shared = dict(w1=W1, w2=W2, b1c=b1c, b2r=b2r)
    return [dict(shared, xnf=xnfs[c], wgate=wgates[c]) for c in range(8)]


def _fin_inmaps(inputs, x1s, qhs, gs, kvps):
    Wo = _np32(inputs["L1_attn_Wo"]); bo = _np32(inputs["L1_attn_bo"])
    shared = dict(
        woT=np.ascontiguousarray(Wo.T),
        bo_b=np.ascontiguousarray(np.broadcast_to(bo, (128, D))),
        indT16=CONSTS["indT16"],
    )
    maps = []
    for c in range(8):
        partner = c ^ 1
        maps.append(dict(shared, x1=x1s[c], qh=qhs[c], g=gs[c],
                         kvp0=kvps[c], kvp1=kvps[partner]))
    return maps


def kernel(**inputs):
    core_ids = list(range(8))
    res_a = run_bass_kernel_spmd(build_l0(), _l0_inmaps(inputs), core_ids=core_ids)
    x1s = [np.asarray(res_a.results[c]["x1"]) for c in range(8)]
    res_b = run_bass_kernel_spmd(build_l1a(), _l1a_inmaps(inputs, x1s),
                                 core_ids=core_ids)
    xnfs = [np.asarray(res_b.results[c]["xnf"]) for c in range(8)]
    qhs = [np.asarray(res_b.results[c]["qh"]) for c in range(8)]
    kvps = [np.asarray(res_b.results[c]["kvp"]) for c in range(8)]
    wgates = [np.asarray(res_b.results[c]["wgate"]) for c in range(8)]
    res_c = run_bass_kernel_spmd(build_moe(), _moe_inmaps(inputs, xnfs, wgates),
                                 core_ids=core_ids)
    gs = [np.asarray(res_c.results[c]["g"]) for c in range(8)]
    res_d = run_bass_kernel_spmd(build_fin(),
                                 _fin_inmaps(inputs, x1s, qhs, gs, kvps),
                                 core_ids=core_ids)
    out = np.zeros((B, L, D), np.float32)
    for c in range(8):
        b, hh = c // 2, c % 2
        out[b, hh * NT:(hh + 1) * NT] = np.asarray(res_d.results[c]["out"])
    return out


# version 02
# speedup vs baseline: 1.0318x; 1.0318x over previous
import contextlib
import numpy as np
import ml_dtypes
import concourse.bass as bass
import concourse.mybir as mybir
from concourse.bass_utils import run_bass_kernel_spmd

F32 = mybir.dt.float32
BF16 = mybir.dt.bfloat16
I32 = mybir.dt.int32
AF = mybir.ActivationFunctionType
ALU = mybir.AluOpType
AX = mybir.AxisListType
NPBF16 = ml_dtypes.bfloat16

B, L, D = 4, 8192, 1024
H, DK, LOW, LEN = 16, 64, 20, 200
E, HDIM = 8, 4096
NT = 4096
TT = 256
NTILE = NT // TT
IDX200 = np.linspace(np.float32(0.0), np.float32(L - 1), LEN,
                     dtype=np.float32).astype(np.int32)


class MS:
    def __init__(self, stack, nc, name, step=1):
        self.sem = stack.enter_context(nc.semaphore(name))
        self.step = step
        self.n = 0


class Sched:

    def __init__(self, stack, nc):
        self.nc = nc
        self.ops = {e: [] for e in ("t", "v", "s", "g", "y")}
        self.ms = {e: MS(stack, nc, f"ms_{e}") for e in ("t", "v", "s", "g")}
        self.dy = MS(stack, nc, "dma_y", 16)
        self.dg = MS(stack, nc, "dma_g", 16)
        self.chain = {e: MS(stack, nc, f"chain_{e}") for e in ("v", "s")}
        self._cur_eng = None

    def G(self, e, instr):
        cs = self.chain[self._cur_eng]
        cs.n += 1
        instr.then_inc(cs.sem, 1)
        e.wait_ge(cs.sem, cs.n)
        return instr

    def _waits(self, e, waits):
        for (st, val) in waits:
            sem = st.sem if isinstance(st, MS) else self.ms[st].sem
            e.wait_ge(sem, val)

    def add(self, eng, fn, inc=False, waits=()):
        ms = self.ms[eng]
        tok = None
        if inc:
            ms.n += 1
            tok = ms.n
        wl = [w for w in waits if w is not None]

        def run(e, _fn=fn, _tok=tok, _wl=wl, _ms=ms, _eng=eng):
            self._cur_eng = _eng
            self._waits(e, _wl)
            last = _fn(e)
            if _tok is not None:
                last.then_inc(_ms.sem, 1)
                if _eng in ("v", "s"):
                    e.wait_ge(_ms.sem, _tok)
            elif _eng in ("v", "s"):
                self.G(e, last)
        self.ops[eng].append(run)
        return tok

    def dma(self, eng, fn, waits=()):
        stream = self.dy if eng == "y" else self.dg
        stream.n += 16
        tok = stream.n
        wl = [w for w in waits if w is not None]

        def run(e, _fn=fn, _wl=wl, _st=stream):
            self._waits(e, _wl)
            _fn(e).then_inc(_st.sem, 16)
        self.ops[eng].append(run)
        return tok

    def wait(self, eng, waits):
        wl = [w for w in waits if w is not None]
        self.ops[eng].append(lambda e, _wl=wl: self._waits(e, _wl))

    def emit(self, blk):
        for eng, meth in (("t", blk.tensor), ("v", blk.vector),
                          ("s", blk.scalar), ("g", blk.gpsimd),
                          ("y", blk.sync)):
            ops = self.ops[eng]
            if not ops:
                continue

            def body(e, _ops=tuple(ops)):
                for fn in _ops:
                    fn(e)
            meth(body)
        self.ops = {e: [] for e in self.ops}


def _consts():
    ind16 = np.zeros((128, 8 * 16), np.float32)
    indT16 = np.zeros((16, 8 * 128), np.float32)
    for mc in range(8):
        for p in range(128):
            j = 2 * mc + p // 64
            ind16[p, mc * 16 + j] = 1.0
            indT16[j, mc * 128 + p] = 1.0
    return dict(ident=np.eye(128, dtype=NPBF16), ind16=ind16, indT16=indT16)


CONSTS = _consts()


def ln_tile(sc, sb, src, n_blk, out_f32, out_bf16, waits, src_off=0,
            extra_fin_waits=()):
    st = sb["st"]
    nb = n_blk

    def x3():
        return sb[src][:, src_off:src_off + nb * 1024].rearrange(
            "p (b f) -> p b f", b=nb)
    t_s1 = sc.add("v", lambda e: e.reduce_sum(st[:, 0:nb], x3(), AX.X),
                  inc=True, waits=waits)
    t_sq = sc.add("s", lambda e: e.activation(
        sb["sqx"][:, 0:nb * 1024].rearrange("p (b f) -> p b f", b=nb),
        x3(), AF.Square), inc=True, waits=waits)

    def stats_ops(e):
        sc.G(e, e.reduce_sum(st[:, nb:2 * nb],
                             sb["sqx"][:, 0:nb * 1024].rearrange("p (b f) -> p b f", b=nb),
                             AX.X))
        sc.G(e, e.tensor_scalar(st[:, 2 * nb:3 * nb], st[:, 0:nb], 1.0 / 1024, None, ALU.mult))
        sc.G(e, e.tensor_scalar(st[:, 3 * nb:4 * nb], st[:, nb:2 * nb], 1.0 / 1024, None, ALU.mult))
        sc.G(e, e.tensor_tensor(st[:, 4 * nb:5 * nb], st[:, 2 * nb:3 * nb],
                                st[:, 2 * nb:3 * nb], ALU.mult))
        sc.G(e, e.tensor_tensor(st[:, 3 * nb:4 * nb], st[:, 3 * nb:4 * nb],
                                st[:, 4 * nb:5 * nb], ALU.subtract))
        return e.tensor_scalar(st[:, 3 * nb:4 * nb], st[:, 3 * nb:4 * nb],
                               1e-5, None, ALU.add)
    t_v2 = sc.add("v", stats_ops, inc=True, waits=[("s", t_sq), ("v", t_s1)])
    t_sd = sc.add("s", lambda e: e.activation(st[:, 4 * nb:5 * nb],
                                              st[:, 3 * nb:4 * nb], AF.Sqrt),
                  inc=True, waits=[("v", t_v2)])

    def fin(e):
        sc.G(e, e.reciprocal(st[:, 5 * nb:6 * nb], st[:, 4 * nb:5 * nb]))
        last = None
        for b in range(nb):
            dst = (sb[out_f32] if out_f32 else sb[out_bf16])
            last = e.tensor_scalar(
                dst[:, b * 1024:(b + 1) * 1024],
                sb[src][:, src_off + b * 1024: src_off + (b + 1) * 1024],
                st[:, 2 * nb + b:2 * nb + b + 1],
                st[:, 5 * nb + b:5 * nb + b + 1],
                ALU.subtract, ALU.mult)
        if out_f32 and out_bf16:
            sc.G(e, last)
            last = e.tensor_copy(sb[out_bf16][:, 0:nb * 1024],
                                 sb[out_f32][:, 0:nb * 1024])
        return last
    return sc.add("v", fin, inc=True,
                  waits=[("s", t_sd)] + [w for w in extra_fin_waits if w])


def quant_1024(sc, sb, src, src_off, dst, dst_off, wait_first):
    def s3(n, o=0):
        return sb[n][:, o:o + 1024].rearrange("p (a b) -> p a b", a=32)

    def v1(e):
        sc.G(e, e.reduce_max(sb["qsc"][:, 32:64], s3(src, src_off), AX.X,
                             apply_absolute_value=True))
        sc.G(e, e.tensor_scalar(sb["qsc"][:, 0:32], sb["qsc"][:, 32:64],
                                1.0 / 7.5, 1e-30, ALU.mult, ALU.max))
        return e.reciprocal(sb["qrc"][:, 0:32], sb["qsc"][:, 0:32])
    t1 = sc.add("v", v1, inc=True, waits=wait_first)

    def v2(e):
        z2 = sb["z2"][:, 0:1024]; z3 = sb["z3"][:, 0:1024]
        z4 = sb["z4"][:, 0:1024]; z5 = sb["z5"][:, 0:1024]
        zi = z3.bitcast(I32)
        sc.G(e, e.tensor_tensor(s3("z2"), s3(src, src_off),
                                sb["qrc"][:, 0:32].unsqueeze(2)
                                .broadcast_to([128, 32, 32]), ALU.mult))
        sc.G(e, e.scalar_tensor_tensor(z3, z2, -1.0, z2, ALU.mult, ALU.max))
        sc.G(e, e.tensor_scalar(z4, z3, 2.0, None, ALU.is_lt))
        sc.G(e, e.tensor_scalar(z5, z3, 4.0, None, ALU.is_lt))
        sc.G(e, e.tensor_scalar(z3, z5, -0.25, 0.5, ALU.mult, ALU.add))
        sc.G(e, e.scalar_tensor_tensor(z3, z4, -0.125, z3, ALU.mult, ALU.add))
        sc.G(e, e.tensor_scalar(z4, z4, 4.0, None, ALU.mult))
        sc.G(e, e.tensor_scalar(z5, z5, 2.0, 2.0, ALU.mult, ALU.add))
        sc.G(e, e.tensor_tensor(z5, z4, z5, ALU.add))
        sc.G(e, e.tensor_tensor(z2, z2, z5, ALU.mult))
        zi = z4.bitcast(I32)
        sc.G(e, e.tensor_copy(zi, z2))
        sc.G(e, e.tensor_copy(z2, zi))
        sc.G(e, e.tensor_tensor(z2, z2, z3, ALU.mult))
        return e.tensor_tensor(
            sb[dst][:, dst_off:dst_off + 1024].rearrange("p (a b) -> p a b", a=32),
            s3("z2"),
            sb["qsc"][:, 0:32].unsqueeze(2).broadcast_to([128, 32, 32]),
            ALU.mult)
    return sc.add("v", v2, inc=True, waits=[("v", t1)])


def transpose_16(sc, sb, src, dst, wait_first, guard_slots, ident="ident",
                 nslots=2):
    evt = {}
    last_t = None
    for g4 in range(4):
        slot = g4 % nslots

        def tp(e, _g=g4, _s=slot):
            last = None
            for j in range(4):
                bi = _g * 4 + j
                blkT, kc = bi // 8, bi % 8
                last = e.transpose(
                    sb["ptp"][:, _s * 1024 + j * 128:_s * 1024 + (j + 1) * 128],
                    sb[src][:, blkT * 1024 + kc * 128: blkT * 1024 + (kc + 1) * 128],
                    sb[ident][:, :])
            return last
        w = list(wait_first)
        if g4 >= nslots:
            w.append(("v", evt[g4 - nslots]))
        elif guard_slots.get(slot) is not None:
            w.append(("v", guard_slots[slot]))
        last_t = sc.add("t", tp, inc=True, waits=w)

        def ev(e, _g=g4, _s=slot):
            last = None
            for j in range(4):
                bi = _g * 4 + j
                blkT, kc = bi // 8, bi % 8
                last = e.tensor_copy(
                    sb[dst][:, kc * 256 + blkT * 128: kc * 256 + (blkT + 1) * 128],
                    sb["ptp"][:, _s * 1024 + j * 128:_s * 1024 + (j + 1) * 128])
            return last
        evt[g4] = sc.add("v", ev, inc=True, waits=[("t", last_t)])
    return evt, last_t


def build_l0(debug=False):
    nc = bass.Bass("TRN2", target_bir_lowering=False)
    ap = {}
    for name, shape, dt in [
        ("x", [NT, D], F32), ("xs", [256, D], F32),
        ("wq", [D, D], F32), ("wk", [D, D], F32), ("wo", [D, D], F32),
        ("wep", [256, H * LOW], F32), ("wrp", [256, H * LOW], F32),
        ("wcz_a", [64, DK], F32), ("wcz_b", [64, DK], F32),
        ("bqc", [128, 8], F32), ("bkc", [128, 8], F32),
        ("bc2", [128, 1], F32), ("bob", [128, D], F32), ("fbb", [128, D], F32),
        ("fw", [D, D], F32),
        ("ident", [128, 128], BF16), ("ind16", [128, 128], F32),
        ("indT16", [16, 1024], F32),
    ]:
        ap[name] = nc.dram_tensor(name, shape, dt, kind="ExternalInput").ap()
    x1_out = nc.dram_tensor("x1", [NT, D], F32, kind="ExternalOutput").ap()

    with contextlib.ExitStack() as stack:
        sb = {}

        def S(name, shape, dt=F32):
            sb[name] = stack.enter_context(nc.sbuf_tensor("sb_" + name, shape, dt))

        def P(name, shape, dt=F32):
            sb[name] = stack.enter_context(nc.psum_tensor("ps_" + name, shape, dt))

        S("wq", [128, 8 * 1024], BF16); S("wk", [128, 8 * 1024], BF16)
        S("wo", [128, 8 * 1024], BF16)
        S("wd", [128, 8 * 1024], BF16); S("wdT", [128, 8 * 1024], BF16)
        S("webd", [128, 8 * 128], BF16); S("wrbd", [128, 8 * 128], BF16)
        S("wcz_a", [64, DK], BF16); S("wcz_b", [64, DK], BF16)
        S("bqc", [128, 8]); S("bkc", [128, 8])
        S("bc2", [128, 1]); S("bob", [128, D]); S("fbb", [128, D])
        S("ident", [128, 128], BF16)
        S("ind16", [128, 128]); S("indT16", [16, 1024])
        S("st", [128, 16])
        S("rn", [16, 256]); S("nrm", [16, 256])
        P("ptp", [128, 2 * 1024], BF16)
        P("pqk", [128, 2 * 512])
        P("pbig", [128, 1024])

        S("s_sb", [128, 2048]); S("sqx", [128, 2048])
        sc = Sched(stack, nc)
        with contextlib.ExitStack() as setup_stack:
            for (nm, shp, dt) in (("xs", [128, 2 * 1024], F32),
                                  ("xsn", [128, 2 * 1024], BF16),
                                  ("wetmp", [64, 2 * 320], BF16),
                                  ("wep", [128, 2 * 320], BF16),
                                  ("wrp", [128, 2 * 320], BF16),
                                  ("z2", [128, 1024], F32),
                                  ("z3", [128, 1024], F32),
                                  ("z4", [128, 1024], F32),
                                  ("z5", [128, 1024], F32),
                                  ("qsc", [128, 64], F32),
                                  ("qrc", [128, 64], F32)):
                sb[nm] = setup_stack.enter_context(
                    nc.sbuf_tensor("sbs_" + nm, shp, dt))
            _emit_l0_setup(nc, sc, sb, ap)
        S("sx", [128, 2048])
        S("xn", [128, 2048]); S("xnb", [128, 2048], BF16)
        S("xnT", [128, 8 * 256], BF16)
        S("qfm", [128, 8 * 256]); S("kfm", [128, 8 * 256])
        S("qnb", [128, 8 * 256], BF16); S("knb", [128, 8 * 256], BF16)
        S("esc", [64, 8 * 256], BF16); S("rsc", [64, 8 * 256], BF16)
        S("attnf", [128, 8 * 256], BF16); S("odds", [64, 8 * 256], BF16)
        S("g_sb", [128, 2048])
        S("z2", [128, 1024]); S("z3", [128, 1024])
        S("z4", [128, 1024]); S("z5", [128, 1024])
        S("xdb", [128, 2048], BF16)
        S("qsc", [128, 64]); S("qrc", [128, 64])
        _emit_l0_main(nc, sc, sb, ap, x1_out)
        if debug:
            with nc.Block() as blk:
                for nm in ("xn", "xnb", "xnT", "qfm", "kfm", "qnb", "knb",
                           "esc", "rsc", "attnf", "s_sb", "g_sb", "xdb",
                           "wd", "wdT", "webd", "wrbd"):
                    t = sb[nm]
                    o = nc.dram_tensor("dbg_" + nm, list(t.ap().shape),
                                       t.ap().dtype, kind="ExternalOutput").ap()
                    sc.dma("y", lambda e, _t=t, _o=o: e.dma_start(out=_o, in_=_t[:, :]))
                sc.wait("y", [(sc.dy, sc.dy.n)])
                sc.emit(blk)
    return nc


def _emit_l0_setup(nc, sc, sb, ap):
    with nc.Block() as blk:
        loads_y = [
            lambda e: e.dma_start(out=sb["xs"][:, :].rearrange("p (c f) -> p c f", c=2),
                                  in_=ap["xs"].rearrange("(c p) f -> p c f", p=128)),
            lambda e: e.dma_start(out=sb["bqc"][:, :], in_=ap["bqc"]),
            lambda e: e.dma_start(out=sb["bkc"][:, :], in_=ap["bkc"]),
            lambda e: e.dma_start(out=sb["bc2"][:, :], in_=ap["bc2"]),
            lambda e: e.dma_start(out=sb["bob"][:, :], in_=ap["bob"]),
            lambda e: e.dma_start(out=sb["fbb"][:, :], in_=ap["fbb"]),
            lambda e: e.dma_start(out=sb["ident"][:, :], in_=ap["ident"]),
            lambda e: e.dma_start(out=sb["ind16"][:, :], in_=ap["ind16"]),
            lambda e: e.dma_start(out=sb["indT16"][:, :], in_=ap["indT16"]),
        ]
        for f in loads_y:
            t_ldf = sc.dma("y", f)
        loads_g = [
            lambda e: e.dma_start(out=sb["wq"][:, :].rearrange("p (kc n) -> p kc n", kc=8),
                                  in_=ap["wq"].rearrange("(kc p) n -> p kc n", p=128)),
            lambda e: e.dma_start(out=sb["wk"][:, :].rearrange("p (kc n) -> p kc n", kc=8),
                                  in_=ap["wk"].rearrange("(kc p) n -> p kc n", p=128)),
            lambda e: e.dma_start(out=sb["wo"][:, :].rearrange("p (kc n) -> p kc n", kc=8),
                                  in_=ap["wo"].rearrange("(kc p) n -> p kc n", p=128)),
            lambda e: e.dma_start(out=sb["wep"][:, :].rearrange("p (c f) -> p c f", c=2),
                                  in_=ap["wep"].rearrange("(c p) f -> p c f", p=128)),
            lambda e: e.dma_start(out=sb["wrp"][:, :].rearrange("p (c f) -> p c f", c=2),
                                  in_=ap["wrp"].rearrange("(c p) f -> p c f", p=128)),
            lambda e: e.dma_start(out=sb["wcz_a"][:, :], in_=ap["wcz_a"]),
            lambda e: e.dma_start(out=sb["wcz_b"][:, :], in_=ap["wcz_b"]),
        ]
        for f in loads_g:
            t_ldc = sc.dma("g", f)
        sc.wait("y", [(sc.dy, t_ldf)])
        sc.wait("g", [(sc.dg, t_ldc)])
        sc.wait("v", [(sc.dy, t_ldf), (sc.dg, t_ldc)])
        sc.emit(blk)

    with nc.Block() as blk:
        t_ln = ln_tile(sc, sb, src="xs", n_blk=2, out_f32=None, out_bf16="xsn",
                       waits=[])

        def we_mms(e, wsrc):
            last = None
            for h in range(H):
                for lc in range(2):
                    last = e.matmul(
                        sb["pqk"][0:64, h * 20:(h + 1) * 20],
                        sb["xsn"][:, lc * 1024 + h * 64: lc * 1024 + (h + 1) * 64],
                        sb[wsrc][:, lc * 320 + h * 20: lc * 320 + (h + 1) * 20],
                        start=(lc == 0), stop=(lc == 1))
            return last
        t_we = sc.add("t", lambda e: we_mms(e, "wep"), inc=True,
                      waits=[("v", t_ln)])
        t_wev = sc.add("v", lambda e: e.tensor_copy(sb["wetmp"][0:64, 0:320],
                                                    sb["pqk"][0:64, 0:320]),
                       inc=True, waits=[("t", t_we)])
        t_wr = sc.add("t", lambda e: we_mms(e, "wrp"), inc=True,
                      waits=[("v", t_wev)])
        t_wrv = sc.add("v", lambda e: e.tensor_copy(sb["wetmp"][0:64, 320:640],
                                                    sb["pqk"][0:64, 0:320]),
                       inc=True, waits=[("t", t_wr)])
        t_z = sc.add("g", lambda e: (e.memset(sb["webd"][:, :], 0.0),
                                     e.memset(sb["wrbd"][:, :], 0.0))[-1],
                     inc=True)
        scat_toks = []
        for (src_off, dst) in ((0, "webd"), (320, "wrbd")):
            for par in range(2):
                def one(e, _so=src_off, _d=dst, _p=par):
                    return e.dma_start(
                        out=sb[_d][_p * 64:(_p + 1) * 64, :]
                        .rearrange("p (mc c) -> p mc c", mc=8)[:, :, _p * 32:_p * 32 + 20],
                        in_=sb["wetmp"][0:64, _so:_so + 320]
                        .rearrange("p (mc two c) -> p mc two c", mc=8, two=2)[:, :, _p, :],
                        allow_slow_non_contiguous=True)
                scat_toks.append(sc.dma("g", one,
                                        waits=[("v", t_wrv), ("g", t_z)]))

        for oc in range(8):
            prev_v = sc.ms["v"].n
            t_l = sc.dma("y", lambda e, _oc=oc: e.dma_start(
                out=sb["s_sb"][:, 0:1024],
                in_=ap["fw"].rearrange("(oc p) i -> oc p i", p=128)[_oc]),
                waits=[("v", prev_v)])
            quant_1024(sc, sb, src="s_sb", src_off=0, dst="wd",
                       dst_off=oc * 1024, wait_first=[(sc.dy, t_l)])
        evt = {}
        for g4 in range(16):
            slot = g4 % 2

            def tp(e, _g=g4, _s=slot):
                last = None
                for j in range(4):
                    bi = _g * 4 + j
                    oc, icc = bi // 8, bi % 8
                    last = e.transpose(
                        sb["ptp"][:, _s * 1024 + j * 128:_s * 1024 + (j + 1) * 128],
                        sb["wd"][:, oc * 1024 + icc * 128: oc * 1024 + (icc + 1) * 128],
                        sb["ident"][:, :])
                return last
            w = [("v", sc.ms["v"].n)] if g4 < 2 else [("v", evt[g4 - 2])]
            t_t = sc.add("t", tp, inc=True, waits=w)

            def ev(e, _g=g4, _s=slot):
                last = None
                for j in range(4):
                    bi = _g * 4 + j
                    oc, icc = bi // 8, bi % 8
                    last = e.tensor_copy(
                        sb["wdT"][:, icc * 1024 + oc * 128: icc * 1024 + (oc + 1) * 128],
                        sb["ptp"][:, _s * 1024 + j * 128:_s * 1024 + (j + 1) * 128])
                return last
            evt[g4] = sc.add("v", ev, inc=True, waits=[("t", t_t)])
        sc.wait("t", [("v", sc.ms["v"].n)] +
                [(sc.dg, max(scat_toks))])
        sc.wait("v", [("g", sc.ms["g"].n), (sc.dg, max(scat_toks))])
        sc.wait("s", [("v", sc.ms["v"].n)])
        sc.emit(blk)


def _emit_l0_main(nc, sc, sb, ap, x1_out):
    with nc.Block() as blk:
        guards = {}
        for i in range(NTILE):
            off = 0
            w_load = []
            if i - 1 in guards:
                w_load.append(("v", guards[i - 1]["gate"]))
            t_x = sc.dma("y", lambda e, _i=i, _o=off: e.dma_start(
                out=sb["sx"][:, _o:_o + 2048].rearrange("p (b f) -> p b f", b=2),
                in_=ap["x"].rearrange("(i b p) f -> i p b f", b=2, p=128)[_i]),
                waits=w_load)
            guards[i] = _l0_tile(sc, sb, i, off, t_x, x1_out,
                                 guards.get(i - 1))
        sc.wait("y", [(sc.dy, sc.dy.n)])
        sc.emit(blk)


def _l0_tile(sc, sb, i, off, t_x, x1_out, pg):
    g = {}
    extra = []
    if pg:
        extra.append(("t", pg["ssq_t"]))
        extra.append(("t", pg["xd_tp_t"]))
    t_ln = ln_tile(sc, sb, src="sx", n_blk=2, out_f32="xn", out_bf16="xnb",
                   waits=[(sc.dy, t_x)], src_off=off, extra_fin_waits=extra)
    g["ln"] = t_ln

    tp_guard = {0: pg["xd_ev2"] if pg else None, 1: pg["xd_ev3"] if pg else None}
    w0 = [("v", t_ln)]
    if pg:
        w0.append(("v", pg["proj_last"]))
        w0.append(("t", pg["ffn_t"]))
    evt, t_tp_last = transpose_16(sc, sb, "xnb", "xnT", w0, tp_guard)
    g["tp_ev2"], g["tp_ev3"] = evt[2], evt[3]

    t_proj_ev = {}
    t_proj_mm = {}
    idx = 0
    for (wname, bname, oname) in (("wq", "bqc", "qfm"), ("wk", "bkc", "kfm")):
        for mc in range(8):
            slot = idx % 2

            def mm(e, _w=wname, _mc=mc, _s=slot):
                last = None
                for kc in range(8):
                    last = e.matmul(
                        sb["pqk"][:, _s * 512:_s * 512 + 256],
                        sb[_w][:, kc * 1024 + _mc * 128: kc * 1024 + (_mc + 1) * 128],
                        sb["xnT"][:, kc * 256:(kc + 1) * 256],
                        start=(kc == 0), stop=(kc == 7))
                return last
            w = [("v", evt[3])]
            if idx >= 2:
                w.append(("v", t_proj_ev[idx - 2]))
            elif pg:
                w.append(("v", pg["at_ev"]))
                w.append(("v", pg["qn_v"]))
            t_proj_mm[idx] = sc.add("t", mm, inc=True, waits=w)
            t_proj_ev[idx] = sc.add(
                "v", lambda e, _b=bname, _o=oname, _mc=mc, _s=slot: e.tensor_scalar(
                    sb[_o][:, _mc * 256:(_mc + 1) * 256],
                    sb["pqk"][:, _s * 512:_s * 512 + 256],
                    sb[_b][:, _mc:_mc + 1], None, ALU.add),
                inc=True, waits=[("t", t_proj_mm[idx])])
            idx += 1
    g["proj_last"] = t_proj_ev[15]
    g["proj_t"] = t_proj_mm[15]

    t_nrm = {}
    t_ssq_all = None
    for (j, (src, dstb)) in enumerate((("qfm", "qnb"), ("kfm", "knb"))):
        wait_src = t_proj_ev[7] if j == 0 else t_proj_ev[15]
        w_sq = [("v", wait_src), ("v", t_ln)]
        if j == 1:
            w_sq.append(("t", t_ssq_all))
        t_sq = sc.add("s", lambda e, _s=src: e.activation(
            sb["sqx"][:, 0:2048], sb[_s][:, 0:2048], AF.Square),
            inc=True, waits=w_sq)
        w_ssq = [("s", t_sq)]
        if j == 0:
            w_ssq.append(("v", t_proj_ev[14]))
        else:
            w_ssq.append(("v", t_nrm[(0, 7)]))

        def ssq(e):
            last = None
            for mc in range(8):
                last = e.matmul(
                    sb["pqk"][0:16, 0:256],
                    sb["ind16"][:, mc * 16:(mc + 1) * 16],
                    sb["sqx"][:, mc * 256:(mc + 1) * 256],
                    start=(mc == 0), stop=(mc == 7))
            return last
        t_ssq = sc.add("t", ssq, inc=True, waits=w_ssq)
        t_ssq_all = t_ssq
        t_n = sc.add("s", lambda e: e.activation(
            sb["nrm"][:, 0:256], sb["pqk"][0:16, 0:256], AF.Sqrt),
            inc=True, waits=[("t", t_ssq)])
        t_rn = sc.add("v", lambda e: (
            sc.G(e, e.tensor_scalar(sb["nrm"][:, 0:256], sb["nrm"][:, 0:256],
                                    1e-12, None, ALU.max)),
            e.reciprocal(sb["rn"][:, 0:256], sb["nrm"][:, 0:256]))[-1],
            inc=True, waits=[("s", t_n)])
        for mc in range(8):
            slot = 1 - (mc % 2)

            def bc(e, _mc=mc, _s=slot):
                return e.matmul(sb["pqk"][:, _s * 512:_s * 512 + 256],
                                sb["indT16"][:, _mc * 128:(_mc + 1) * 128],
                                sb["rn"][:, 0:256], start=True, stop=True)
            w = [("v", t_rn)]
            if mc >= 2:
                w.append(("v", t_nrm[(j, mc - 2)]))
            elif mc == 1 and j == 0:
                pass
            t_bc = sc.add("t", bc, inc=True, waits=w)
            t_nrm[(j, mc)] = sc.add(
                "v", lambda e, _src=src, _d=dstb, _mc=mc, _s=slot: e.tensor_tensor(
                    sb[_d][:, _mc * 256:(_mc + 1) * 256],
                    sb[_src][:, _mc * 256:(_mc + 1) * 256],
                    sb["pqk"][:, _s * 512:_s * 512 + 256], ALU.mult),
                inc=True, waits=[("t", t_bc)])
    g["qn_v"] = t_nrm[(1, 7)]
    g["ssq_t"] = t_ssq_all

    t_esc_ev = {}
    t_esc_mm = {}
    for (j, (wbd, qsrc, dst)) in enumerate((("webd", "qnb", "esc"),
                                            ("wrbd", "knb", "rsc"))):
        wait_src = t_nrm[(0, 7)] if j == 0 else t_nrm[(1, 7)]
        for pair in range(4):
            slot = pair % 2

            def mm(e, _w=wbd, _q=qsrc, _p=pair, _s=slot):
                last = None
                for h2 in range(2):
                    mc = _p * 2 + h2
                    last = e.matmul(
                        sb["pqk"][0:64, _s * 512 + h2 * 256:_s * 512 + (h2 + 1) * 256],
                        sb[_w][:, mc * 128:mc * 128 + 64],
                        sb[_q][:, mc * 256:(mc + 1) * 256],
                        start=True, stop=True)
                return last
            w = [("v", wait_src)]
            if (j, pair) == (0, 0):
                w.append(("v", t_nrm[(1, 7)]))
            if pair >= 2:
                w.append(("v", t_esc_ev[(j, pair - 2)]))
            elif j == 1:
                w.append(("v", t_esc_ev[(0, 2 + pair)]))
            t_esc_mm[(j, pair)] = sc.add("t", mm, inc=True, waits=w)
            t_esc_ev[(j, pair)] = sc.add(
                "v", lambda e, _d=dst, _p=pair, _s=slot: e.tensor_copy(
                    sb[_d][0:64, _p * 512:(_p + 1) * 512],
                    sb["pqk"][0:64, _s * 512:_s * 512 + 512]),
                inc=True, waits=[("t", t_esc_mm[(j, pair)])])
    g["esc_t"] = t_esc_mm[(1, 3)]

    t_at_ev = {}
    t_at_mm = {}
    for mc in range(8):
        def mm(e, _mc=mc):
            last = None
            for par in range(2):
                rb = par * 32
                out = sb["pqk"][0:64, par * 512 + (_mc % 2) * 256:
                                par * 512 + (_mc % 2) * 256 + 256]
                for (wz, ssrc) in (("wcz_a", "esc"), ("wcz_b", "rsc")):
                    last = e.matmul(
                        out, sb[wz][rb:rb + 20, :],
                        sb[ssrc][rb:rb + 20, _mc * 256:(_mc + 1) * 256],
                        start=(wz == "wcz_a"), stop=(wz == "wcz_b"))
            return last
        w = [("v", t_esc_ev[(0, mc // 2)]), ("v", t_esc_ev[(1, mc // 2)])]
        if mc == 0:
            w.append(("v", t_esc_ev[(1, 2)]))
            w.append(("v", t_esc_ev[(1, 3)]))
        else:
            w.append(("v", t_at_ev[mc - 1]))
        if mc == 0 and pg:
            w.append((sc.dg, pg["odd_dma"]))
        t_at_mm[mc] = sc.add("t", mm, inc=True, waits=w)

        def ev(e, _mc=mc):
            e.tensor_scalar(sb["attnf"][0:64, _mc * 256:(_mc + 1) * 256],
                            sb["pqk"][0:64, (_mc % 2) * 256:(_mc % 2) * 256 + 256],
                            sb["bc2"][0:64, 0:1], None, ALU.add)
            return e.tensor_scalar(
                sb["odds"][0:64, _mc * 256:(_mc + 1) * 256],
                sb["pqk"][0:64, 512 + (_mc % 2) * 256:512 + (_mc % 2) * 256 + 256],
                sb["bc2"][0:64, 0:1], None, ALU.add)
        t_at_ev[mc] = sc.add("v", ev, inc=True, waits=[("t", t_at_mm[mc])])
    g["at_ev"] = t_at_ev[7]
    w_odd = [("v", t_at_ev[7])]
    if pg:
        w_odd.append(("t", pg["smm_t"]))
    t_odd = sc.dma("g", lambda e: e.dma_start(
        out=sb["attnf"][64:128, :], in_=sb["odds"][0:64, :]), waits=w_odd)
    g["odd_dma"] = t_odd

    t_s_ev = {}
    t_s_mm = {}
    for tc in range(2):
        def mm(e, _tc=tc):
            last = None
            for nc_ in range(2):
                for kc in range(8):
                    last = e.matmul(
                        sb["pbig"][:, nc_ * 512:(nc_ + 1) * 512],
                        sb["attnf"][:, kc * 256 + _tc * 128: kc * 256 + _tc * 128 + 128],
                        sb["wo"][:, kc * 1024 + nc_ * 512: kc * 1024 + (nc_ + 1) * 512],
                        start=(kc == 0), stop=(kc == 7))
            return last
        w = [(sc.dg, t_odd)]
        if tc == 1:
            w.append(("v", t_s_ev[0]))
        elif pg:
            w.append(("s", pg["ffn_s"]))
        t_s_mm[tc] = sc.add("t", mm, inc=True, waits=w)
        wse = [("t", t_s_mm[tc])]
        if tc == 0 and pg:
            wse.append((sc.dy, pg["x1_dma"]))
        t_s_ev[tc] = sc.add("v", lambda e, _tc=tc: e.tensor_tensor(
            sb["s_sb"][:, _tc * 1024:(_tc + 1) * 1024],
            sb["pbig"][:, 0:1024], sb["bob"][:, :], ALU.add),
            inc=True, waits=wse)
    g["smm_t"] = t_s_mm[1]

    t_q1 = quant_1024(sc, sb, src="xn", src_off=0, dst="xdb", dst_off=0,
                      wait_first=[("v", t_ln)] +
                      ([("t", pg["xd_tp_t"])] if pg else []))
    t_q2 = quant_1024(sc, sb, src="xn", src_off=1024, dst="xdb", dst_off=1024,
                      wait_first=[("v", t_q1)])
    xd_guard = {0: evt[2], 1: evt[3]}
    w_xd = [("v", t_q2), ("t", g["proj_t"])]
    xevt, t_xd_tp = transpose_16(sc, sb, "xdb", "xnT", w_xd, xd_guard)
    g["xd_ev2"], g["xd_ev3"] = xevt[2], xevt[3]
    g["xd_tp_t"] = t_xd_tp

    t_ff_s = None
    for tc in range(2):
        def mm(e, _tc=tc):
            last = None
            for nc_ in range(2):
                for ic in range(8):
                    last = e.matmul(
                        sb["pbig"][:, nc_ * 512:(nc_ + 1) * 512],
                        sb["xnT"][:, ic * 256 + _tc * 128: ic * 256 + _tc * 128 + 128],
                        sb["wdT"][:, ic * 1024 + nc_ * 512: ic * 1024 + (nc_ + 1) * 512],
                        start=(ic == 0), stop=(ic == 7))
            return last
        w = [("v", xevt[3])]
        w.append(("v", t_s_ev[1]) if tc == 0 else ("v", t_ff_v))
        t_mm = sc.add("t", mm, inc=True, waits=w)
        g["ffn_t"] = t_mm
        t_ff_v = sc.add("v", lambda e, _tc=tc: e.tensor_tensor(
            sb["g_sb"][:, _tc * 1024:(_tc + 1) * 1024],
            sb["pbig"][:, 0:1024], sb["fbb"][:, :], ALU.add),
            inc=True, waits=[("t", t_mm)])
        t_ff_s = sc.add("s", lambda e, _tc=tc: e.activation(
            sb["g_sb"][:, _tc * 1024:(_tc + 1) * 1024],
            sb["g_sb"][:, _tc * 1024:(_tc + 1) * 1024], AF.Silu),
            inc=True, waits=[("v", t_ff_v)])
    g["ffn_s"] = t_ff_s

    def gate(e):
        sc.G(e, e.tensor_tensor(sb["s_sb"][:, 0:2048], sb["s_sb"][:, 0:2048],
                                sb["g_sb"][:, 0:2048], ALU.mult))
        return e.tensor_tensor(sb["s_sb"][:, 0:2048], sb["s_sb"][:, 0:2048],
                               sb["sx"][:, off:off + 2048], ALU.add)
    w_gate = [("s", t_ff_s)]
    if pg:
        w_gate.append((sc.dy, pg["x1_dma"]))
    t_gate = sc.add("v", gate, inc=True, waits=w_gate)
    g["gate"] = t_gate
    g["x1_dma"] = sc.dma("y", lambda e, _i=i: e.dma_start(
        out=x1_out.rearrange("(i b p) f -> i p b f", b=2, p=128)[_i],
        in_=sb["s_sb"][:, 0:2048].rearrange("p (b f) -> p b f", b=2)),
        waits=[("v", t_gate)])
    return g


def _np32(a):
    return np.ascontiguousarray(np.asarray(a, dtype=np.float32))


def _l0_inmaps(inputs):
    x = _np32(inputs["L0_attn_x"])
    Wq = _np32(inputs["L0_attn_Wq"]); Wk = _np32(inputs["L0_attn_Wk"])
    Wo = _np32(inputs["L0_attn_Wo"])
    We = _np32(inputs["L0_attn_We"]); Wr = _np32(inputs["L0_attn_Wr"])
    Wc = _np32(inputs["L0_attn_Wc"])
    bq = _np32(inputs["L0_attn_bq"]); bk = _np32(inputs["L0_attn_bk"])
    bc = _np32(inputs["L0_attn_bc"]); bo = _np32(inputs["L0_attn_bo"])
    fw = _np32(inputs["L0_ffn_weight"]); fb = _np32(inputs["L0_ffn_bias"])

    wep = np.zeros((256, H * LOW), np.float32)
    wep[:LEN] = We.transpose(1, 0, 2).reshape(LEN, H * LOW)
    wrp = np.zeros((256, H * LOW), np.float32)
    wrp[:LEN] = Wr.transpose(1, 0, 2).reshape(LEN, H * LOW)
    wcz_a = np.zeros((64, DK), np.float32)
    wcz_a[0:20] = Wc[0:20]; wcz_a[32:52] = Wc[0:20]
    wcz_b = np.zeros((64, DK), np.float32)
    wcz_b[0:20] = Wc[20:40]; wcz_b[32:52] = Wc[20:40]
    shared = dict(
        wq=Wq, wk=Wk, wo=Wo, wep=wep, wrp=wrp, wcz_a=wcz_a, wcz_b=wcz_b,
        bqc=np.ascontiguousarray(bq.reshape(8, 128).T),
        bkc=np.ascontiguousarray(bk.reshape(8, 128).T),
        bc2=np.ascontiguousarray(np.tile(bc, 2)[:, None]),
        bob=np.ascontiguousarray(np.broadcast_to(bo, (128, D))),
        fbb=np.ascontiguousarray(np.broadcast_to(fb, (128, D))),
        fw=fw, ident=CONSTS["ident"], ind16=CONSTS["ind16"],
        indT16=CONSTS["indT16"],
    )
    maps = []
    for c in range(8):
        b, hh = c // 2, c % 2
        m = dict(shared)
        m["x"] = np.ascontiguousarray(x[b, hh * NT:(hh + 1) * NT])
        xs = np.zeros((256, D), np.float32)
        xs[:LEN] = x[b, IDX200]
        m["xs"] = xs
        maps.append(m)
    return maps


def build_l1a():
    nc = bass.Bass("TRN2", target_bir_lowering=False)
    ap = {}
    for name, shape, dt in [
        ("x1", [NT, D], F32),
        ("wqT", [D, D], F32), ("wkT", [D, D], F32), ("wvT", [D, D], F32),
        ("bqc", [128, 8], F32),
        ("bkb", [128, D], F32), ("bvb", [128, D], F32),
        ("wg", [D, 8], F32), ("bgb", [128, 8], F32),
        ("ident", [128, 128], BF16), ("ones128", [128, 1], BF16),
    ]:
        ap[name] = nc.dram_tensor(name, shape, dt, kind="ExternalInput").ap()
    o_xnf = nc.dram_tensor("xnf", [128, 8 * NT], BF16, kind="ExternalOutput").ap()
    o_qh = nc.dram_tensor("qh", [128, 8 * NT], BF16, kind="ExternalOutput").ap()
    o_kvp = nc.dram_tensor("kvp", [64, 1040], F32, kind="ExternalOutput").ap()
    o_w = nc.dram_tensor("wgate", [128, 32 * 8], F32, kind="ExternalOutput").ap()

    with contextlib.ExitStack() as stack:
        sb = {}

        def S(name, shape, dt=F32):
            sb[name] = stack.enter_context(nc.sbuf_tensor("sb_" + name, shape, dt))

        def P(name, shape, dt=F32):
            sb[name] = stack.enter_context(nc.psum_tensor("ps_" + name, shape, dt))

        S("wqT", [128, 8 * 1024], BF16); S("wkT", [128, 8 * 1024], BF16)
        S("wvT", [128, 8 * 1024], BF16)
        S("wg", [128, 8 * 8], BF16)
        S("bqc", [128, 8]); S("bkb", [128, D]); S("bvb", [128, D])
        S("bgb", [128, 8])
        S("ident", [128, 128], BF16); S("ones128", [128, 1], BF16)
        S("st", [128, 16]); S("sqx", [128, 2048])
        S("sx", [128, 2048])
        S("xn", [128, 2048]); S("xnb", [128, 2048], BF16)
        S("xnT", [128, 8 * 256], BF16)
        S("qf", [128, 8 * 256])
        S("ze", [128, 8 * 256])
        S("qeb", [128, 8 * 256], BF16)
        S("kb", [128, 2 * 1024], BF16); S("vb", [128, 2 * 1024], BF16)
        S("ktmp", [128, 1024]); S("ktm2", [128, 1024])
        S("lg", [128, 32 * 8])
        S("pr", [128, 32 * 8])
        S("cnt", [128, 32 * 8])
        S("tc1", [128, 32]); S("tc2", [128, 32])
        S("kvs", [64, 1040])
        P("ptp", [128, 1024], BF16)
        P("pqk", [128, 2 * 512])
        P("pbig", [128, 1024])
        P("pkv", [64, 1024])
        P("pks", [64, 16])

        sc = Sched(stack, nc)
        _emit_l1a(nc, sc, sb, ap, o_xnf, o_qh, o_kvp, o_w)
    return nc


def _emit_l1a(nc, sc, sb, ap, o_xnf, o_qh, o_kvp, o_w):
    with nc.Block() as blk:
        for f in [
            lambda e: e.dma_start(out=sb["bqc"][:, :], in_=ap["bqc"]),
            lambda e: e.dma_start(out=sb["bkb"][:, :], in_=ap["bkb"]),
            lambda e: e.dma_start(out=sb["bvb"][:, :], in_=ap["bvb"]),
            lambda e: e.dma_start(out=sb["bgb"][:, :], in_=ap["bgb"]),
            lambda e: e.dma_start(out=sb["ident"][:, :], in_=ap["ident"]),
            lambda e: e.dma_start(out=sb["ones128"][:, :], in_=ap["ones128"]),
        ]:
            t_ldf = sc.dma("y", f)
        for nm in ("wqT", "wkT", "wvT"):
            t_ldc = sc.dma("g", lambda e, _n=nm: e.dma_start(
                out=sb[_n][:, :].rearrange("p (kc n) -> p kc n", kc=8),
                in_=ap[_n].rearrange("(kc p) n -> p kc n", p=128)))
        t_ldc = sc.dma("g", lambda e: e.dma_start(
            out=sb["wg"][:, :].rearrange("p (kc n) -> p kc n", kc=8),
            in_=ap["wg"].rearrange("(kc p) n -> p kc n", p=128)))
        sc.wait("y", [(sc.dy, t_ldf)])
        sc.wait("g", [(sc.dg, t_ldc)])
        sc.wait("v", [(sc.dy, t_ldf), (sc.dg, t_ldc)])
        sc.emit(blk)

    with nc.Block() as blk:
        guards = {}
        for i in range(NTILE):
            w_load = []
            if i - 1 in guards:
                w_load.append(("v", guards[i - 1]["done"]))
            t_x = sc.dma("y", lambda e, _i=i: e.dma_start(
                out=sb["sx"][:, :].rearrange("p (b f) -> p b f", b=2),
                in_=ap["x1"].rearrange("(i b p) f -> i p b f", b=2, p=128)[_i]),
                waits=w_load)
            guards[i] = _l1a_tile(nc, sc, sb, ap, i, t_x, o_xnf, o_qh,
                                  guards.get(i - 1))
        t_kvev = sc.add("v", lambda e: (
            sc.G(e, e.tensor_copy(sb["kvs"][0:64, 0:1024], sb["pkv"][:, :])),
            e.tensor_copy(sb["kvs"][0:64, 1024:1040], sb["pks"][:, :]))[-1],
            inc=True, waits=[("t", guards[NTILE - 1]["kv_t"])])
        sc.dma("y", lambda e: e.dma_start(out=o_kvp, in_=sb["kvs"][0:64, :]),
               waits=[("v", t_kvev)])
        _emit_gate(nc, sc, sb, o_w, wait=("v", guards[NTILE - 1]["lg_v"]))
        sc.wait("y", [(sc.dy, sc.dy.n)])
        sc.wait("g", [(sc.dg, sc.dg.n)])
        sc.emit(blk)


def _l1a_tile(nc, sc, sb, ap, i, t_x, o_xnf, o_qh, pg):
    g = {}
    extra = []
    if pg:
        extra.append(("t", pg["lastuse_t"]))
    t_ln = ln_tile(sc, sb, src="sx", n_blk=2, out_f32="xn", out_bf16="xnb",
                   waits=[(sc.dy, t_x)], src_off=0, extra_fin_waits=extra)
    tp_guard = {0: pg["tp_ev3"] if pg else None}
    w0 = [("v", t_ln)]
    if pg:
        w0.append(("t", pg["lastuse_t"]))
        w0.append((sc.dy, pg["xnf_dma"]))
    evt, t_tp_last = transpose_16(sc, sb, "xnb", "xnT", w0, tp_guard, nslots=1)
    g["tp_ev3"] = evt[3]
    t_xnf = sc.dma("y", lambda e, _i=i: e.dma_start(
        out=o_xnf.rearrange("p (kc i t) -> p kc i t", kc=8, i=NTILE)[:, :, _i, :],
        in_=sb["xnT"][:, :].rearrange("p (kc t) -> p kc t", kc=8)),
        waits=[("v", evt[3])])
    g["xnf_dma"] = t_xnf

    t_q_ev = {}
    for mc in range(8):
        slot = mc % 2

        def mm(e, _mc=mc, _s=slot):
            last = None
            for kc in range(8):
                last = e.matmul(
                    sb["pqk"][:, _s * 512:_s * 512 + 256],
                    sb["wqT"][:, kc * 1024 + _mc * 128: kc * 1024 + (_mc + 1) * 128],
                    sb["xnT"][:, kc * 256:(kc + 1) * 256],
                    start=(kc == 0), stop=(kc == 7))
            return last
        w = [("v", evt[3])]
        if mc >= 2:
            w.append(("v", t_q_ev[mc - 2]))
        elif pg:
            w.append(("v", pg["q_done"]))
        t_mm = sc.add("t", mm, inc=True, waits=w)
        wev = [("t", t_mm)]
        if pg and mc < 2:
            wev.append((sc.dy, pg["qh_dma"]))
        t_ev = sc.add("v", lambda e, _mc=mc, _s=slot: (
            sc.G(e, e.tensor_scalar(sb["qf"][:, _mc * 256:(_mc + 1) * 256],
                                    sb["pqk"][:, _s * 512:_s * 512 + 256],
                                    sb["bqc"][:, _mc:_mc + 1], None, ALU.add)),
            e.tensor_scalar(sb["ze"][:, _mc * 256:(_mc + 1) * 256],
                            sb["qf"][:, _mc * 256:(_mc + 1) * 256],
                            0.0, None, ALU.min))[-1],
            inc=True, waits=wev)
        t_ex = sc.add("s", lambda e, _mc=mc: e.activation(
            sb["ze"][:, _mc * 256:(_mc + 1) * 256],
            sb["ze"][:, _mc * 256:(_mc + 1) * 256], AF.Exp),
            inc=True, waits=[("v", t_ev)])
        t_q_ev[mc] = sc.add("v", lambda e, _mc=mc: e.scalar_tensor_tensor(
            sb["qeb"][:, _mc * 256:(_mc + 1) * 256],
            sb["qf"][:, _mc * 256:(_mc + 1) * 256], 0.0,
            sb["ze"][:, _mc * 256:(_mc + 1) * 256],
            ALU.max, ALU.add), inc=True, waits=[("s", t_ex)])
    g["q_done"] = t_q_ev[7]
    t_qh = sc.dma("y", lambda e, _i=i: e.dma_start(
        out=o_qh.rearrange("p (mc i t) -> p mc i t", mc=8, i=NTILE)[:, :, _i, :],
        in_=sb["qeb"][:, :].rearrange("p (mc t) -> p mc t", mc=8)),
        waits=[("v", t_q_ev[7])])
    g["qh_dma"] = t_qh

    t_kv_ready = {}
    for (j, (wname, bname, oname)) in enumerate((("wkT", "bkb", "kb"),
                                                 ("wvT", "bvb", "vb"))):
        for tc in range(2):
            def mm(e, _w=wname, _tc=tc):
                last = None
                for nc_ in range(2):
                    for kc in range(8):
                        last = e.matmul(
                            sb["pbig"][:, nc_ * 512:(nc_ + 1) * 512],
                            sb["xnT"][:, kc * 256 + _tc * 128: kc * 256 + _tc * 128 + 128],
                            sb[_w][:, kc * 1024 + nc_ * 512: kc * 1024 + (nc_ + 1) * 512],
                            start=(kc == 0), stop=(kc == 7))
                return last
            idx = j * 2 + tc
            w = [("v", evt[3])]
            if idx == 0:
                if pg:
                    w.append(("t", pg["kv_t"]))
                    w.append(("v", pg["kv_done"]))
            else:
                w.append(("v", t_kv_ready[idx - 1]))
            t_mm = sc.add("t", mm, inc=True, waits=w)
            if j == 0:
                t_ev = sc.add("v", lambda e, _tc=tc: (
                    sc.G(e, e.tensor_tensor(sb["ktmp"][:, :],
                                            sb["pbig"][:, 0:1024],
                                            sb["bkb"][:, :], ALU.add)),
                    e.tensor_scalar(sb["ktm2"][:, :], sb["ktmp"][:, :],
                                    0.0, None, ALU.min))[-1],
                    inc=True, waits=[("t", t_mm)])
                t_ex = sc.add("s", lambda e: e.activation(
                    sb["ktm2"][:, :], sb["ktm2"][:, :], AF.Exp),
                    inc=True, waits=[("v", t_ev)])
                t_kv_ready[idx] = sc.add("v", lambda e, _tc=tc: e.scalar_tensor_tensor(
                    sb["kb"][:, _tc * 1024:(_tc + 1) * 1024],
                    sb["ktmp"][:, :], 0.0, sb["ktm2"][:, :],
                    ALU.max, ALU.add), inc=True, waits=[("s", t_ex)])
            else:
                t_kv_ready[idx] = sc.add("v", lambda e, _tc=tc: e.tensor_tensor(
                    sb["vb"][:, _tc * 1024:(_tc + 1) * 1024],
                    sb["pbig"][:, 0:1024], sb["bvb"][:, :], ALU.add),
                    inc=True, waits=[("t", t_mm)])
    g["kv_done"] = t_kv_ready[3]

    def kvacc(e):
        last = None
        for tc in range(2):
            for h in range(H):
                first = (i == 0 and tc == 0 and h in (0, 8))
                firstks = (i == 0 and tc == 0 and h == 0)
                lastmm = (i == NTILE - 1 and tc == 1)
                last = e.matmul(
                    sb["pkv"][:, h * 64:(h + 1) * 64],
                    sb["kb"][:, tc * 1024 + h * 64: tc * 1024 + (h + 1) * 64],
                    sb["vb"][:, tc * 1024 + h * 64: tc * 1024 + (h + 1) * 64],
                    start=first, stop=lastmm, skip_group_check=True)
                last = e.matmul(
                    sb["pks"][:, h:h + 1],
                    sb["kb"][:, tc * 1024 + h * 64: tc * 1024 + (h + 1) * 64],
                    sb["ones128"][:, :],
                    start=firstks, stop=(lastmm and h == H - 1),
                    skip_group_check=True)
        return last
    g["kv_t"] = sc.add("t", kvacc, inc=True,
                       waits=[("v", t_kv_ready[1]), ("v", t_kv_ready[3])])

    def gmm(e):
        last = None
        for tc in range(2):
            for kc in range(8):
                last = e.matmul(
                    sb["pqk"][0:128, 0:8] if False else
                    sb["pqk"][:, tc * 512:tc * 512 + 8],
                    sb["xnT"][:, kc * 256 + tc * 128: kc * 256 + tc * 128 + 128],
                    sb["wg"][:, kc * 8:(kc + 1) * 8],
                    start=(kc == 0), stop=(kc == 7))
        return last
    t_g_mm = sc.add("t", gmm, inc=True,
                    waits=[("v", t_q_ev[7]), ("v", t_q_ev[6])])
    g["lg_v"] = sc.add("v", lambda e, _i=i: (
        sc.G(e, e.tensor_tensor(sb["lg"][:, (2 * _i) * 8:(2 * _i + 1) * 8],
                                sb["pqk"][:, 0:8], sb["bgb"][:, :], ALU.add)),
        e.tensor_tensor(sb["lg"][:, (2 * _i + 1) * 8:(2 * _i + 2) * 8],
                        sb["pqk"][:, 512:520], sb["bgb"][:, :], ALU.add))[-1],
        inc=True, waits=[("t", t_g_mm)])
    g["lastuse_t"] = t_g_mm
    g["done"] = g["lg_v"]
    return g


def _emit_gate(nc, sc, sb, o_w, wait):
    lg3 = sb["lg"][:, :].rearrange("p (b e) -> p b e", b=32)
    pr3 = sb["pr"][:, :].rearrange("p (b e) -> p b e", b=32)

    def soft1(e):
        sc.G(e, e.reduce_max(sb["tc1"][:, :], lg3, AX.X))
        sc.G(e, e.tensor_tensor(pr3, lg3,
                                sb["tc1"][:, :].unsqueeze(2).broadcast_to([128, 32, 8]),
                                ALU.subtract))
        return e.engine_nop()
    t1 = sc.add("v", soft1, inc=True, waits=[wait])
    t2 = sc.add("s", lambda e: e.activation(sb["pr"][:, :], sb["pr"][:, :], AF.Exp),
                inc=True, waits=[("v", t1)])

    def soft2(e):
        sc.G(e, e.reduce_sum(sb["tc1"][:, :], pr3, AX.X))
        sc.G(e, e.reciprocal(sb["tc2"][:, :], sb["tc1"][:, :]))
        return e.tensor_tensor(pr3, pr3,
                               sb["tc2"][:, :].unsqueeze(2).broadcast_to([128, 32, 8]),
                               ALU.mult)
    t3 = sc.add("v", soft2, inc=True, waits=[("s", t2)])

    def top2(e):
        for j in range(E):
            prj = pr3[:, :, j:j + 1].squeeze(2)
            first = True
            for ii in range(E):
                if ii == j:
                    continue
                op = ALU.is_ge if ii < j else ALU.is_gt
                pri = pr3[:, :, ii:ii + 1].squeeze(2)
                if first:
                    sc.G(e, e.tensor_tensor(
                        sb["cnt"][:, :].rearrange("p (b e) -> p b e", b=32)[:, :, j:j + 1].squeeze(2),
                        pri, prj, op))
                    first = False
                else:
                    sc.G(e, e.tensor_tensor(sb["tc1"][:, :], pri, prj, op))
                    sc.G(e, e.tensor_tensor(
                        sb["cnt"][:, :].rearrange("p (b e) -> p b e", b=32)[:, :, j:j + 1].squeeze(2),
                        sb["cnt"][:, :].rearrange("p (b e) -> p b e", b=32)[:, :, j:j + 1].squeeze(2),
                        sb["tc1"][:, :], ALU.add))
        sc.G(e, e.tensor_scalar(sb["cnt"][:, :], sb["cnt"][:, :], 2.0, None, ALU.is_lt))
        sc.G(e, e.tensor_tensor(sb["pr"][:, :], sb["pr"][:, :], sb["cnt"][:, :], ALU.mult))
        sc.G(e, e.reduce_sum(sb["tc1"][:, :], pr3, AX.X))
        sc.G(e, e.tensor_scalar(sb["tc1"][:, :], sb["tc1"][:, :], 1e-8, None, ALU.add))
        sc.G(e, e.reciprocal(sb["tc2"][:, :], sb["tc1"][:, :]))
        return e.tensor_tensor(pr3, pr3,
                               sb["tc2"][:, :].unsqueeze(2).broadcast_to([128, 32, 8]),
                               ALU.mult)
    t4 = sc.add("v", top2, inc=True, waits=[("v", t3)])
    sc.dma("y", lambda e: e.dma_start(out=o_w, in_=sb["pr"][:, :]),
           waits=[("v", t4)])


def build_moe():
    nc = bass.Bass("TRN2", target_bir_lowering=False)
    ap = {}
    for name, shape, dt in [
        ("xnf", [128, 8 * NT], BF16),
        ("wgate", [128, 32 * 8], F32),
        ("w1", [E, D, HDIM], F32), ("w2", [E, HDIM, D], F32),
        ("b1c", [128, E * 32], F32), ("b2r", [1, E * D], F32),
    ]:
        ap[name] = nc.dram_tensor(name, shape, dt, kind="ExternalInput").ap()
    o_g = nc.dram_tensor("g", [NT, D], F32, kind="ExternalOutput").ap()

    with contextlib.ExitStack() as stack:
        sb = {}

        def S(name, shape, dt=F32):
            sb[name] = stack.enter_context(nc.sbuf_tensor("sb_" + name, shape, dt))

        def P(name, shape, dt=F32):
            sb[name] = stack.enter_context(nc.psum_tensor("ps_" + name, shape, dt))

        S("xnf", [128, 8 * NT], BF16)
        S("gh", [128, 16 * 1024])
        S("w1a", [128, 8 * 512], BF16); S("w1b", [128, 8 * 512], BF16)
        S("w2a", [128, 4 * 1024], BF16); S("w2b", [128, 4 * 1024], BF16)
        S("hb", [128, 2 * 256], BF16)
        S("wgate", [128, 32 * 8])
        S("b1c", [128, E * 32]); S("b2r", [1, E * D], BF16)
        S("onesb", [1, 128], BF16)
        P("ph", [128, 1024])
        P("pgA", [128, 1024]); P("pgB", [128, 1024])

        sc = Sched(stack, nc)
        _emit_moe(nc, sc, sb, ap, o_g)
    return nc


def _emit_moe(nc, sc, sb, ap, o_g):
    with nc.Block() as blk:
        sc.dma("g", lambda e: e.dma_start(
            out=sb["xnf"][:, :], in_=ap["xnf"]))
        sc.dma("y", lambda e: e.dma_start(out=sb["wgate"][:, :], in_=ap["wgate"]))
        t1 = sc.dma("y", lambda e: e.dma_start(out=sb["b1c"][:, :], in_=ap["b1c"]))
        t0 = sc.dma("g", lambda e: e.dma_start(out=sb["b2r"][:, :], in_=ap["b2r"]))
        t2 = sc.add("g", lambda e: e.memset(sb["onesb"][:, :], 1.0), inc=True)
        sc.wait("y", [(sc.dy, t1), (sc.dg, t0)])
        sc.wait("g", [(sc.dy, t1)])
        sc.emit(blk)

    for half in range(2):
        with nc.Block() as blk:
            _emit_moe_half(nc, sc, sb, ap, o_g, half)
            sc.emit(blk)


def _emit_moe_half(nc, sc, sb, ap, o_g, half):
    bufs = [("w1a", "w2a"), ("w1b", "w2b")]
    t_wload = {}
    t_lastuse = {}
    units = [(ei, s) for ei in range(E) for s in range(8)]

    def load_unit(uidx):
        ei, s = units[uidx]
        w1b, w2b = bufs[uidx % 2]
        w = []
        if uidx - 2 >= 0:
            w.append(("t", t_lastuse[uidx - 2]))
        t_a = sc.dma("g", lambda e, _e=ei, _s=s, _b=w1b: e.dma_start(
            out=sb[_b][:, :].rearrange("p (kc j) -> p kc j", kc=8),
            in_=ap["w1"][_e][:, _s * 512:(_s + 1) * 512]
            .rearrange("(kc p) j -> p kc j", p=128)), waits=w)
        t_b = sc.dma("g", lambda e, _e=ei, _s=s, _b=w2b: e.dma_start(
            out=sb[_b][:, :].rearrange("p (jc f) -> p jc f", jc=4),
            in_=ap["w2"][_e][_s * 512:(_s + 1) * 512, :]
            .rearrange("(jc p) f -> p jc f", p=128)))
        t_wload[uidx] = max(t_a, t_b)

    load_unit(0)
    load_unit(1)
    t_stt = {}
    t_gelu = {}
    for uidx, (ei, s) in enumerate(units):
        w1b, w2b = bufs[uidx % 2]
        last_mm = None

        def emit_second(tt, jc, s_, w2name, extra_w):
            nonlocal last_mm

            def secondmm(e, _jc=jc, _sl=jc % 2, _w2=w2name, _s=s_):
                last = None
                for tc in range(2):
                    pg = sb["pgA"] if tc == 0 else sb["pgB"]
                    for fc in range(2):
                        last = e.matmul(
                            pg[:, fc * 512:(fc + 1) * 512],
                            sb["hb"][:, _sl * 256 + tc * 128: _sl * 256 + tc * 128 + 128],
                            sb[_w2][:, _jc * 1024 + fc * 512: _jc * 1024 + (fc + 1) * 512],
                            start=(_jc == 0 and _s > 0), stop=(_jc == 3),
                            skip_group_check=True)
                return last
            w2w = [("s", t_gelu[(uidx, tt * 4 + jc)])] + list(extra_w)
            if jc == 0:
                wfree = []
                if (uidx, tt - 1, 1) in t_stt:
                    wfree.append(("v", t_stt[(uidx, tt - 1, 1)]))
                elif uidx >= 1 and (uidx - 1, 7, 1) in t_stt:
                    wfree.append(("v", t_stt[(uidx - 1, 7, 1)]))
                if s_ == 0:
                    def b2mm(e, _e=ei):
                        last = None
                        for tc in range(2):
                            pg = sb["pgA"] if tc == 0 else sb["pgB"]
                            for fc in range(2):
                                last = e.matmul(
                                    pg[:, fc * 512:(fc + 1) * 512],
                                    sb["onesb"][:, :],
                                    sb["b2r"][:, _e * 1024 + fc * 512: _e * 1024 + (fc + 1) * 512],
                                    start=True, stop=False, skip_group_check=True)
                        return last
                    sc.add("t", b2mm, inc=False, waits=wfree)
                else:
                    w2w = w2w + wfree
            last_mm = sc.add("t", secondmm, inc=(jc == 3), waits=w2w)
            if jc == 3:
                for tc in range(2):
                    blkid = half * 16 + tt * 2 + tc
                    first_unit = (uidx == 0)

                    def stt(e, _tt=tt, _tc=tc, _b=blkid, _e=ei, _fu=first_unit):
                        pg = sb["pgA"] if _tc == 0 else sb["pgB"]
                        dst = sb["gh"][:, (_tt * 2 + _tc) * 1024:(_tt * 2 + _tc + 1) * 1024]
                        wcol = sb["wgate"][:, _b * 8 + _e: _b * 8 + _e + 1]
                        if _fu:
                            return e.tensor_scalar(dst, pg[:, 0:1024], wcol,
                                                   None, ALU.mult)
                        return e.scalar_tensor_tensor(dst, pg[:, 0:1024], wcol,
                                                      dst, ALU.mult, ALU.add)
                    t_stt[(uidx, tt, tc)] = sc.add("v", stt, inc=True,
                                                   waits=[("t", last_mm)])

        pending = None
        for tt in range(8):
            for jc in range(4):
                slot = jc % 2

                def firstmm(e, _s=s, _tt=tt, _jc=jc, _sl=slot, _w1=w1b, _h=half):
                    last = None
                    for kc in range(8):
                        last = e.matmul(
                            sb["ph"][:, _sl * 512:_sl * 512 + 256],
                            sb[_w1][:, kc * 512 + _jc * 128: kc * 512 + (_jc + 1) * 128],
                            sb["xnf"][:, :].rearrange("p (kc t) -> p kc t", kc=8)
                            [:, kc, _h * 2048 + _tt * 256: _h * 2048 + (_tt + 1) * 256],
                            start=(kc == 0), stop=(kc == 7))
                    return last
                w = [(sc.dg, t_wload[uidx])]
                stepid = tt * 4 + jc
                if stepid >= 2:
                    w.append(("s", t_gelu[(uidx, stepid - 2)]))
                elif uidx >= 1:
                    w.append(("s", t_gelu[(uidx - 1, 31 - (1 - stepid))]))
                t_fm = sc.add("t", firstmm, inc=True, waits=w)
                t_gelu[(uidx, stepid)] = sc.add(
                    "s", lambda e, _e=ei, _s=s, _jc=jc, _sl=slot: e.activation(
                        sb["hb"][:, _sl * 256:(_sl + 1) * 256],
                        sb["ph"][:, _sl * 512:_sl * 512 + 256],
                        AF.Gelu_apprx_tanh,
                        bias=sb["b1c"][:, _e * 32 + _s * 4 + _jc: _e * 32 + _s * 4 + _jc + 1]),
                    inc=True, waits=[("t", t_fm)])
                if pending is not None:
                    emit_second(*pending)
                pending = (tt, jc, s, w2b, ())
        emit_second(*pending)
        pending = None
        t_lastuse[uidx] = last_mm
        if uidx + 2 < len(units):
            load_unit(uidx + 2)
    for chunk in range(4):
        t_si = sc.add("s", lambda e, _c=chunk: e.activation(
            sb["gh"][:, _c * 4096:(_c + 1) * 4096],
            sb["gh"][:, _c * 4096:(_c + 1) * 4096], AF.Silu),
            inc=True,
            waits=[("v", t_stt[(63, chunk * 2, 0)]),
                   ("v", t_stt[(63, chunk * 2, 1)]),
                   ("v", t_stt[(63, chunk * 2 + 1, 0)]),
                   ("v", t_stt[(63, chunk * 2 + 1, 1)])])
        sc.dma("y", lambda e, _c=chunk, _h=half: e.dma_start(
            out=o_g.rearrange("(hh c b p) f -> hh c p b f", hh=2, c=4, p=128)[_h, _c],
            in_=sb["gh"][:, _c * 4096:(_c + 1) * 4096]
            .rearrange("p (b f) -> p b f", b=4)),
            waits=[("s", t_si)])
    sc.wait("y", [(sc.dy, sc.dy.n)])
    sc.wait("g", [(sc.dg, sc.dg.n)])


def build_fin(debug=False):
    nc = bass.Bass("TRN2", target_bir_lowering=False)
    ap = {}
    for name, shape, dt in [
        ("x1", [NT, D], F32), ("qh", [128, 8 * NT], BF16),
        ("g", [NT, D], F32),
        ("kvp0", [64, 1040], F32), ("kvp1", [64, 1040], F32),
        ("woT", [D, D], F32), ("bo_b", [128, D], F32),
        ("indT16", [16, 1024], F32),
    ]:
        ap[name] = nc.dram_tensor(name, shape, dt, kind="ExternalInput").ap()
    o_out = nc.dram_tensor("out", [NT, D], F32, kind="ExternalOutput").ap()

    with contextlib.ExitStack() as stack:
        sb = {}

        def S(name, shape, dt=F32):
            sb[name] = stack.enter_context(nc.sbuf_tensor("sb_" + name, shape, dt))

        def P(name, shape, dt=F32):
            sb[name] = stack.enter_context(nc.psum_tensor("ps_" + name, shape, dt))

        S("woT", [128, 8 * 1024], BF16)
        S("bo_b", [128, D]); S("indT16", [16, 1024])
        S("kv0", [64, 1040]); S("kv1", [64, 1040])
        S("kv2", [128, 1040], BF16)
        S("ksi", [128, 16 * 16], BF16)
        S("qt", [128, 8 * 256], BF16)
        S("sx", [128, 2048]); S("gt", [128, 2048])
        S("attnf", [128, 8 * 256], BF16); S("odds", [64, 8 * 256], BF16)
        S("rden", [16, 256]); S("bcs", [128, 2 * 256])
        S("s_sb", [128, 2048])
        P("pnum", [128, 2 * 512])
        P("pnumO", [64, 2 * 512])
        P("pdenA", [16, 256]); P("pdenB", [16, 256])
        P("pbig", [128, 1024])

        sc = Sched(stack, nc)
        _emit_fin(nc, sc, sb, ap, o_out)
        if debug:
            with nc.Block() as blk:
                for nm in ("kv2", "ksi", "rden", "attnf", "odds", "bcs",
                           "s_sb", "qt", "gt", "sx"):
                    t = sb[nm]
                    o = nc.dram_tensor("dbg_" + nm, list(t.ap().shape),
                                       t.ap().dtype, kind="ExternalOutput").ap()
                    sc.dma("y", lambda e, _t=t, _o=o: e.dma_start(out=_o, in_=_t[:, :]))
                sc.wait("y", [(sc.dy, sc.dy.n)])
                sc.emit(blk)
    return nc


def _emit_fin(nc, sc, sb, ap, o_out):
    with nc.Block() as blk:
        t1 = sc.dma("y", lambda e: e.dma_start(out=sb["kv0"][0:64, :], in_=ap["kvp0"]))
        t2 = sc.dma("y", lambda e: e.dma_start(out=sb["kv1"][0:64, :], in_=ap["kvp1"]))
        sc.dma("y", lambda e: e.dma_start(out=sb["bo_b"][:, :], in_=ap["bo_b"]))
        t3 = sc.dma("y", lambda e: e.dma_start(out=sb["indT16"][:, :], in_=ap["indT16"]))
        t4 = sc.dma("g", lambda e: e.dma_start(
            out=sb["woT"][:, :].rearrange("p (kc n) -> p kc n", kc=8),
            in_=ap["woT"].rearrange("(kc p) n -> p kc n", p=128)))
        t5 = sc.add("v", lambda e: (
            sc.G(e, e.tensor_tensor(sb["kv0"][0:64, :], sb["kv0"][0:64, :],
                                    sb["kv1"][0:64, :], ALU.add)),
            e.tensor_copy(sb["kv2"][0:64, :], sb["kv0"][0:64, :]))[-1],
            inc=True, waits=[(sc.dy, max(t1, t2))])
        t6 = sc.add("g", lambda e: e.memset(sb["ksi"][:, :], 0.0), inc=True)
        t7 = sc.dma("g", lambda e: e.dma_start(
            out=sb["kv2"][64:128, :], in_=sb["kv2"][0:64, :]),
            waits=[("v", t5)])
        def ksi(e):
            last = None
            for h in range(H):
                last = e.tensor_copy(
                    sb["ksi"][0:64, h * 16 + h: h * 16 + h + 1],
                    sb["kv2"][0:64, 1024 + h: 1025 + h])
            return last
        t8 = sc.add("v", ksi, inc=True, waits=[("v", t5), ("g", t6)])
        t9 = sc.dma("g", lambda e: e.dma_start(
            out=sb["ksi"][64:128, :], in_=sb["ksi"][0:64, :]),
            waits=[("v", t8)])
        sc.wait("t", [(sc.dg, max(t4, t7, t9)), (sc.dy, t3)])
        sc.wait("y", [(sc.dy, sc.dy.n)])
        sc.wait("g", [(sc.dg, sc.dg.n)])
        sc.emit(blk)

    with nc.Block() as blk:
        guards = {}
        for i in range(NTILE):
            w_load = []
            if i - 1 in guards:
                w_load.append(("v", guards[i - 1]["done"]))
            t_q = sc.dma("y", lambda e, _i=i: e.dma_start(
                out=sb["qt"][:, :].rearrange("p (mc t) -> p mc t", mc=8),
                in_=ap["qh"].rearrange("p (mc i t) -> p mc i t", mc=8, i=NTILE)[:, :, _i, :]),
                waits=w_load)
            t_x = sc.dma("y", lambda e, _i=i: e.dma_start(
                out=sb["sx"][:, :].rearrange("p (b f) -> p b f", b=2),
                in_=ap["x1"].rearrange("(i b p) f -> i p b f", b=2, p=128)[_i]),
                waits=w_load)
            t_g = sc.dma("y", lambda e, _i=i: e.dma_start(
                out=sb["gt"][:, :].rearrange("p (b f) -> p b f", b=2),
                in_=ap["g"].rearrange("(i b p) f -> i p b f", b=2, p=128)[_i]),
                waits=w_load)
            guards[i] = _fin_tile(nc, sc, sb, ap, i, t_q, t_x, t_g, o_out,
                                  guards.get(i - 1))
        sc.wait("y", [(sc.dy, sc.dy.n)])
        sc.emit(blk)


def _fin_tile(nc, sc, sb, ap, i, t_q, t_x, t_g, o_out, pg):
    g = {}
    def denmm(e):
        last = None
        for h in range(H):
            hb = (h % 2) * 64
            pd = sb["pdenA"] if h % 2 == 0 else sb["pdenB"]
            last = e.matmul(
                pd[:, :],
                sb["ksi"][hb:hb + 64, h * 16:(h + 1) * 16],
                sb["qt"][hb:hb + 64, (h // 2) * 256:(h // 2 + 1) * 256],
                start=(h < 2), stop=(h >= H - 2))
        return last
    w = [(sc.dy, t_q)]
    if pg:
        w.append(("v", pg["rden_v"]))
    t_den = sc.add("t", denmm, inc=True, waits=w)
    t_rden = sc.add("v", lambda e: (
        sc.G(e, e.tensor_copy(sb["rden"][:, :], sb["pdenA"][:, :])),
        sc.G(e, e.tensor_tensor(sb["rden"][:, :], sb["pdenB"][:, :],
                                sb["rden"][:, :], ALU.add)),
        sc.G(e, e.tensor_scalar(sb["rden"][:, :], sb["rden"][:, :], 1e-6,
                                None, ALU.max)),
        e.reciprocal(sb["rden"][:, :], sb["rden"][:, :]))[-1],
        inc=True, waits=[("t", t_den)])
    g["rden_v"] = t_rden

    t_at = {}
    for mc in range(8):
        slot = mc % 2

        def mm(e, _mc=mc, _s=slot):
            last = None
            h0, h1 = 2 * _mc, 2 * _mc + 1
            e.matmul(sb["pnum"][0:64, _s * 512: _s * 512 + 256],
                     sb["kv2"][0:64, h0 * 64: h0 * 64 + 64],
                     sb["qt"][0:64, _mc * 256:(_mc + 1) * 256],
                     start=True, stop=True)
            e.matmul(sb["pnumO"][0:64, _s * 512: _s * 512 + 256],
                     sb["kv2"][64:128, h1 * 64: h1 * 64 + 64],
                     sb["qt"][64:128, _mc * 256:(_mc + 1) * 256],
                     start=True, stop=True)
            last = e.matmul(
                sb["pnum"][:, _s * 512 + 256: _s * 512 + 512],
                sb["indT16"][:, _mc * 128:(_mc + 1) * 128],
                sb["rden"][:, :], start=True, stop=True)
            return last
        w = [("v", t_rden)]
        if mc >= 2:
            w.append(("v", t_at[mc - 2]))
        elif pg:
            w.append(("v", pg["at_done"]))
            w.append((sc.dg, pg["odd_dma"]))
        t_mm = sc.add("t", mm, inc=True, waits=w)

        def ev(e, _mc=mc, _s=slot):
            sc.G(e, e.tensor_copy(sb["bcs"][:, _s * 256:(_s + 1) * 256],
                                  sb["pnum"][:, _s * 512 + 256:_s * 512 + 512]))
            sc.G(e, e.tensor_tensor(
                sb["attnf"][0:64, _mc * 256:(_mc + 1) * 256],
                sb["pnum"][0:64, _s * 512:_s * 512 + 256],
                sb["bcs"][0:64, _s * 256:(_s + 1) * 256], ALU.mult))
            return e.tensor_tensor(
                sb["odds"][0:64, _mc * 256:(_mc + 1) * 256],
                sb["pnumO"][0:64, _s * 512:_s * 512 + 256],
                sb["bcs"][64:128, _s * 256:(_s + 1) * 256], ALU.mult)
        t_at[mc] = sc.add("v", ev, inc=True, waits=[("t", t_mm)])
    g["at_done"] = t_at[7]
    w_odd = [("v", t_at[7])]
    if pg:
        w_odd.append(("t", pg["smm_t"]))
    t_odd = sc.dma("g", lambda e: e.dma_start(
        out=sb["attnf"][64:128, :], in_=sb["odds"][0:64, :]), waits=w_odd)
    g["odd_dma"] = t_odd

    t_s_ev = {}
    for tc in range(2):
        def mm(e, _tc=tc):
            last = None
            for nc_ in range(2):
                for kc in range(8):
                    last = e.matmul(
                        sb["pbig"][:, nc_ * 512:(nc_ + 1) * 512],
                        sb["attnf"][:, kc * 256 + _tc * 128: kc * 256 + _tc * 128 + 128],
                        sb["woT"][:, kc * 1024 + nc_ * 512: kc * 1024 + (nc_ + 1) * 512],
                        start=(kc == 0), stop=(kc == 7))
            return last
        w = [(sc.dg, t_odd)]
        if tc == 1:
            w.append(("v", t_s_ev[0]))
        elif pg:
            w.append(("v", pg["done"]))
        t_mm = sc.add("t", mm, inc=True, waits=w)
        g["smm_t"] = t_mm
        wse = [("t", t_mm)]
        if tc == 0 and pg:
            wse.append((sc.dy, pg["out_dma"]))
        t_s_ev[tc] = sc.add("v", lambda e, _tc=tc: e.tensor_tensor(
            sb["s_sb"][:, _tc * 1024:(_tc + 1) * 1024],
            sb["pbig"][:, 0:1024], sb["bo_b"][:, :], ALU.add),
            inc=True, waits=wse)

    def gate(e):
        sc.G(e, e.tensor_tensor(sb["s_sb"][:, 0:2048], sb["s_sb"][:, 0:2048],
                                sb["gt"][:, 0:2048], ALU.mult))
        return e.tensor_tensor(sb["s_sb"][:, 0:2048], sb["s_sb"][:, 0:2048],
                               sb["sx"][:, 0:2048], ALU.add)
    t_gate = sc.add("v", gate, inc=True,
                    waits=[("v", t_s_ev[1]), (sc.dy, t_x), (sc.dy, t_g)])
    g["done"] = t_gate
    g["out_dma"] = sc.dma("y", lambda e, _i=i: e.dma_start(
        out=o_out.rearrange("(i b p) f -> i p b f", b=2, p=128)[_i],
        in_=sb["s_sb"][:, 0:2048].rearrange("p (b f) -> p b f", b=2)),
        waits=[("v", t_gate)])
    return g


def _l1a_inmaps(inputs, x1s):
    Wq = _np32(inputs["L1_attn_Wq"]); Wk = _np32(inputs["L1_attn_Wk"])
    Wv = _np32(inputs["L1_attn_Wv"])
    bq = _np32(inputs["L1_attn_bq"]); bk = _np32(inputs["L1_attn_bk"])
    bv = _np32(inputs["L1_attn_bv"])
    Wg = _np32(inputs["L1_ffn_Wg"]); bg = _np32(inputs["L1_ffn_bg"])
    shared = dict(
        wqT=np.ascontiguousarray(Wq.T), wkT=np.ascontiguousarray(Wk.T),
        wvT=np.ascontiguousarray(Wv.T),
        bqc=np.ascontiguousarray(bq.reshape(8, 128).T),
        bkb=np.ascontiguousarray(np.broadcast_to(bk, (128, D))),
        bvb=np.ascontiguousarray(np.broadcast_to(bv, (128, D))),
        wg=Wg, bgb=np.ascontiguousarray(np.broadcast_to(bg, (128, 8))),
        ident=CONSTS["ident"],
        ones128=np.ones((128, 1), NPBF16),
    )
    return [dict(shared, x1=x1s[c]) for c in range(8)]


def _moe_inmaps(inputs, xnfs, wgates):
    W1 = _np32(inputs["L1_ffn_W1"]); W2 = _np32(inputs["L1_ffn_W2"])
    b1 = _np32(inputs["L1_ffn_b1"]); b2 = _np32(inputs["L1_ffn_b2"])
    b1c = np.ascontiguousarray(
        b1.reshape(E, 32, 128).transpose(2, 0, 1).reshape(128, E * 32))
    b2r = np.ascontiguousarray(b2.reshape(1, E * D))
    shared = dict(w1=W1, w2=W2, b1c=b1c, b2r=b2r)
    return [dict(shared, xnf=xnfs[c], wgate=wgates[c]) for c in range(8)]


def _fin_inmaps(inputs, x1s, qhs, gs, kvps):
    Wo = _np32(inputs["L1_attn_Wo"]); bo = _np32(inputs["L1_attn_bo"])
    shared = dict(
        woT=np.ascontiguousarray(Wo.T),
        bo_b=np.ascontiguousarray(np.broadcast_to(bo, (128, D))),
        indT16=CONSTS["indT16"],
    )
    maps = []
    for c in range(8):
        partner = c ^ 1
        maps.append(dict(shared, x1=x1s[c], qh=qhs[c], g=gs[c],
                         kvp0=kvps[c], kvp1=kvps[partner]))
    return maps


def kernel(**inputs):
    core_ids = list(range(8))
    res_a = run_bass_kernel_spmd(build_l0(), _l0_inmaps(inputs), core_ids=core_ids)
    x1s = [np.asarray(res_a.results[c]["x1"]) for c in range(8)]
    res_b = run_bass_kernel_spmd(build_l1a(), _l1a_inmaps(inputs, x1s),
                                 core_ids=core_ids)
    xnfs = [np.asarray(res_b.results[c]["xnf"]) for c in range(8)]
    qhs = [np.asarray(res_b.results[c]["qh"]) for c in range(8)]
    kvps = [np.asarray(res_b.results[c]["kvp"]) for c in range(8)]
    wgates = [np.asarray(res_b.results[c]["wgate"]) for c in range(8)]
    res_c = run_bass_kernel_spmd(build_moe(), _moe_inmaps(inputs, xnfs, wgates),
                                 core_ids=core_ids)
    gs = [np.asarray(res_c.results[c]["g"]) for c in range(8)]
    res_d = run_bass_kernel_spmd(build_fin(),
                                 _fin_inmaps(inputs, x1s, qhs, gs, kvps),
                                 core_ids=core_ids)
    out = np.zeros((B, L, D), np.float32)
    for c in range(8):
        b, hh = c // 2, c % 2
        out[b, hh * NT:(hh + 1) * NT] = np.asarray(res_d.results[c]["out"])
    return out


# version 03
# speedup vs baseline: 1.0367x; 1.0047x over previous
import contextlib
import numpy as np
import ml_dtypes
import concourse.bass as bass
import concourse.mybir as mybir
from concourse.bass_utils import run_bass_kernel_spmd

F32 = mybir.dt.float32
BF16 = mybir.dt.bfloat16
I32 = mybir.dt.int32
AF = mybir.ActivationFunctionType
ALU = mybir.AluOpType
AX = mybir.AxisListType
NPBF16 = ml_dtypes.bfloat16

B, L, D = 4, 8192, 1024
H, DK, LOW, LEN = 16, 64, 20, 200
E, HDIM = 8, 4096
NT = 4096
TT = 256
NTILE = NT // TT
IDX200 = np.linspace(np.float32(0.0), np.float32(L - 1), LEN,
                     dtype=np.float32).astype(np.int32)


class MS:
    def __init__(self, stack, nc, name, step=1):
        self.sem = stack.enter_context(nc.semaphore(name))
        self.step = step
        self.n = 0


class Sched:

    def __init__(self, stack, nc):
        self.nc = nc
        self.ops = {e: [] for e in ("t", "v", "s", "g", "y")}
        self.ms = {e: MS(stack, nc, f"ms_{e}") for e in ("t", "v", "s", "g")}
        self.dy = MS(stack, nc, "dma_y", 16)
        self.dg = MS(stack, nc, "dma_g", 16)
        self.chain = {e: MS(stack, nc, f"chain_{e}") for e in ("v", "s")}
        self._cur_eng = None
        self._seen = {}

    def G(self, e, instr):
        cs = self.chain[self._cur_eng]
        cs.n += 1
        instr.then_inc(cs.sem, 1)
        e.wait_ge(cs.sem, cs.n)
        return instr

    def _waits(self, e, waits):
        seen = self._seen.setdefault(self._cur_eng, {})
        for (st, val) in waits:
            stream = st if isinstance(st, MS) else self.ms[st]
            k = id(stream)
            if seen.get(k, -1) >= val:
                continue
            seen[k] = val
            e.wait_ge(stream.sem, val)

    def add(self, eng, fn, inc=False, waits=()):
        ms = self.ms[eng]
        tok = None
        if inc:
            ms.n += 1
            tok = ms.n
        wl = [w for w in waits if w is not None]

        def run(e, _fn=fn, _tok=tok, _wl=wl, _ms=ms, _eng=eng):
            self._cur_eng = _eng
            self._waits(e, _wl)
            last = _fn(e)
            if _tok is not None:
                last.then_inc(_ms.sem, 1)
                if _eng in ("v", "s"):
                    e.wait_ge(_ms.sem, _tok)
                    self._seen.setdefault(_eng, {})[id(_ms)] = _tok
            elif _eng in ("v", "s"):
                self.G(e, last)
        self.ops[eng].append(run)
        return tok

    def dma(self, eng, fn, waits=()):
        stream = self.dy if eng == "y" else self.dg
        stream.n += 16
        tok = stream.n
        wl = [w for w in waits if w is not None]

        def run(e, _fn=fn, _wl=wl, _st=stream, _eng=eng):
            self._cur_eng = _eng
            self._waits(e, _wl)
            _fn(e).then_inc(_st.sem, 16)
        self.ops[eng].append(run)
        return tok

    def wait(self, eng, waits):
        wl = [w for w in waits if w is not None]

        def run(e, _wl=wl, _eng=eng):
            self._cur_eng = _eng
            self._waits(e, _wl)
        self.ops[eng].append(run)

    def emit(self, blk):
        for eng, meth in (("t", blk.tensor), ("v", blk.vector),
                          ("s", blk.scalar), ("g", blk.gpsimd),
                          ("y", blk.sync)):
            ops = self.ops[eng]
            if not ops:
                continue

            def body(e, _ops=tuple(ops)):
                for fn in _ops:
                    fn(e)
            meth(body)
        self.ops = {e: [] for e in self.ops}


def _consts():
    ind16 = np.zeros((128, 8 * 16), np.float32)
    indT16 = np.zeros((16, 8 * 128), np.float32)
    for mc in range(8):
        for p in range(128):
            j = 2 * mc + p // 64
            ind16[p, mc * 16 + j] = 1.0
            indT16[j, mc * 128 + p] = 1.0
    return dict(ident=np.eye(128, dtype=NPBF16), ind16=ind16, indT16=indT16)


CONSTS = _consts()


def ln_tile(sc, sb, src, n_blk, out_f32, out_bf16, waits, src_off=0,
            extra_fin_waits=()):
    st = sb["st"]
    nb = n_blk

    def x3():
        return sb[src][:, src_off:src_off + nb * 1024].rearrange(
            "p (b f) -> p b f", b=nb)
    t_s1 = sc.add("v", lambda e: e.reduce_sum(st[:, 0:nb], x3(), AX.X),
                  inc=True, waits=waits)
    t_sq = sc.add("s", lambda e: e.activation(
        sb["sqx"][:, 0:nb * 1024].rearrange("p (b f) -> p b f", b=nb),
        x3(), AF.Square), inc=True, waits=waits)

    def stats_ops(e):
        sc.G(e, e.reduce_sum(st[:, nb:2 * nb],
                             sb["sqx"][:, 0:nb * 1024].rearrange("p (b f) -> p b f", b=nb),
                             AX.X))
        sc.G(e, e.tensor_scalar(st[:, 2 * nb:3 * nb], st[:, 0:nb], 1.0 / 1024, None, ALU.mult))
        sc.G(e, e.tensor_scalar(st[:, 3 * nb:4 * nb], st[:, nb:2 * nb], 1.0 / 1024, None, ALU.mult))
        sc.G(e, e.tensor_tensor(st[:, 4 * nb:5 * nb], st[:, 2 * nb:3 * nb],
                                st[:, 2 * nb:3 * nb], ALU.mult))
        sc.G(e, e.tensor_tensor(st[:, 3 * nb:4 * nb], st[:, 3 * nb:4 * nb],
                                st[:, 4 * nb:5 * nb], ALU.subtract))
        return e.tensor_scalar(st[:, 3 * nb:4 * nb], st[:, 3 * nb:4 * nb],
                               1e-5, None, ALU.add)
    t_v2 = sc.add("v", stats_ops, inc=True, waits=[("s", t_sq), ("v", t_s1)])
    t_sd = sc.add("s", lambda e: e.activation(st[:, 4 * nb:5 * nb],
                                              st[:, 3 * nb:4 * nb], AF.Sqrt),
                  inc=True, waits=[("v", t_v2)])

    def fin(e):
        sc.G(e, e.reciprocal(st[:, 5 * nb:6 * nb], st[:, 4 * nb:5 * nb]))
        last = None
        for b in range(nb):
            dst = (sb[out_f32] if out_f32 else sb[out_bf16])
            last = e.tensor_scalar(
                dst[:, b * 1024:(b + 1) * 1024],
                sb[src][:, src_off + b * 1024: src_off + (b + 1) * 1024],
                st[:, 2 * nb + b:2 * nb + b + 1],
                st[:, 5 * nb + b:5 * nb + b + 1],
                ALU.subtract, ALU.mult)
        if out_f32 and out_bf16:
            sc.G(e, last)
            last = e.tensor_copy(sb[out_bf16][:, 0:nb * 1024],
                                 sb[out_f32][:, 0:nb * 1024])
        return last
    return sc.add("v", fin, inc=True,
                  waits=[("s", t_sd)] + [w for w in extra_fin_waits if w])


def quant_1024(sc, sb, src, src_off, dst, dst_off, wait_first):
    def s3(n, o=0):
        return sb[n][:, o:o + 1024].rearrange("p (a b) -> p a b", a=32)

    def v1(e):
        sc.G(e, e.reduce_max(sb["qsc"][:, 32:64], s3(src, src_off), AX.X,
                             apply_absolute_value=True))
        sc.G(e, e.tensor_scalar(sb["qsc"][:, 0:32], sb["qsc"][:, 32:64],
                                1.0 / 7.5, 1e-30, ALU.mult, ALU.max))
        return e.reciprocal(sb["qrc"][:, 0:32], sb["qsc"][:, 0:32])
    t1 = sc.add("v", v1, inc=True, waits=wait_first)

    def v2(e):
        z2 = sb["z2"][:, 0:1024]; z3 = sb["z3"][:, 0:1024]
        z4 = sb["z4"][:, 0:1024]; z5 = sb["z5"][:, 0:1024]
        zi = z3.bitcast(I32)
        sc.G(e, e.tensor_tensor(s3("z2"), s3(src, src_off),
                                sb["qrc"][:, 0:32].unsqueeze(2)
                                .broadcast_to([128, 32, 32]), ALU.mult))
        sc.G(e, e.scalar_tensor_tensor(z3, z2, -1.0, z2, ALU.mult, ALU.max))
        sc.G(e, e.tensor_scalar(z4, z3, 2.0, None, ALU.is_lt))
        sc.G(e, e.tensor_scalar(z5, z3, 4.0, None, ALU.is_lt))
        sc.G(e, e.tensor_scalar(z3, z5, -0.25, 0.5, ALU.mult, ALU.add))
        sc.G(e, e.scalar_tensor_tensor(z3, z4, -0.125, z3, ALU.mult, ALU.add))
        sc.G(e, e.tensor_scalar(z4, z4, 4.0, None, ALU.mult))
        sc.G(e, e.tensor_scalar(z5, z5, 2.0, 2.0, ALU.mult, ALU.add))
        sc.G(e, e.tensor_tensor(z5, z4, z5, ALU.add))
        sc.G(e, e.tensor_tensor(z2, z2, z5, ALU.mult))
        zi = z4.bitcast(I32)
        sc.G(e, e.tensor_copy(zi, z2))
        sc.G(e, e.tensor_copy(z2, zi))
        sc.G(e, e.tensor_tensor(z2, z2, z3, ALU.mult))
        return e.tensor_tensor(
            sb[dst][:, dst_off:dst_off + 1024].rearrange("p (a b) -> p a b", a=32),
            s3("z2"),
            sb["qsc"][:, 0:32].unsqueeze(2).broadcast_to([128, 32, 32]),
            ALU.mult)
    return sc.add("v", v2, inc=True, waits=[("v", t1)])


def transpose_16(sc, sb, src, dst, wait_first, guard_slots, ident="ident",
                 nslots=2):
    evt = {}
    last_t = None
    for g4 in range(4):
        slot = g4 % nslots

        def tp(e, _g=g4, _s=slot):
            last = None
            for j in range(4):
                bi = _g * 4 + j
                blkT, kc = bi // 8, bi % 8
                last = e.transpose(
                    sb["ptp"][:, _s * 1024 + j * 128:_s * 1024 + (j + 1) * 128],
                    sb[src][:, blkT * 1024 + kc * 128: blkT * 1024 + (kc + 1) * 128],
                    sb[ident][:, :])
            return last
        w = list(wait_first)
        if g4 >= nslots:
            w.append(("v", evt[g4 - nslots]))
        elif guard_slots.get(slot) is not None:
            w.append(("v", guard_slots[slot]))
        last_t = sc.add("t", tp, inc=True, waits=w)

        def ev(e, _g=g4, _s=slot):
            last = None
            for j in range(4):
                bi = _g * 4 + j
                blkT, kc = bi // 8, bi % 8
                last = e.tensor_copy(
                    sb[dst][:, kc * 256 + blkT * 128: kc * 256 + (blkT + 1) * 128],
                    sb["ptp"][:, _s * 1024 + j * 128:_s * 1024 + (j + 1) * 128])
            return last
        evt[g4] = sc.add("v", ev, inc=True, waits=[("t", last_t)])
    return evt, last_t


def build_l0(debug=False):
    nc = bass.Bass("TRN2", target_bir_lowering=False)
    ap = {}
    for name, shape, dt in [
        ("x", [NT, D], F32), ("xs", [256, D], F32),
        ("wq", [D, D], F32), ("wk", [D, D], F32), ("wo", [D, D], F32),
        ("wep", [256, H * LOW], F32), ("wrp", [256, H * LOW], F32),
        ("wcz_a", [64, DK], F32), ("wcz_b", [64, DK], F32),
        ("bqc", [128, 8], F32), ("bkc", [128, 8], F32),
        ("bc2", [128, 1], F32), ("bob", [128, D], F32), ("fbb", [128, D], F32),
        ("fw", [D, D], F32),
        ("ident", [128, 128], BF16), ("ind16", [128, 128], F32),
        ("indT16", [16, 1024], F32),
    ]:
        ap[name] = nc.dram_tensor(name, shape, dt, kind="ExternalInput").ap()
    x1_out = nc.dram_tensor("x1", [NT, D], F32, kind="ExternalOutput").ap()

    with contextlib.ExitStack() as stack:
        sb = {}

        def S(name, shape, dt=F32):
            sb[name] = stack.enter_context(nc.sbuf_tensor("sb_" + name, shape, dt))

        def P(name, shape, dt=F32):
            sb[name] = stack.enter_context(nc.psum_tensor("ps_" + name, shape, dt))

        S("wq", [128, 8 * 1024], BF16); S("wk", [128, 8 * 1024], BF16)
        S("wo", [128, 8 * 1024], BF16)
        S("wd", [128, 8 * 1024], BF16); S("wdT", [128, 8 * 1024], BF16)
        S("webd", [128, 8 * 128], BF16); S("wrbd", [128, 8 * 128], BF16)
        S("wcz_a", [64, DK], BF16); S("wcz_b", [64, DK], BF16)
        S("bqc", [128, 8]); S("bkc", [128, 8])
        S("bc2", [128, 1]); S("bob", [128, D]); S("fbb", [128, D])
        S("ident", [128, 128], BF16)
        S("ind16", [128, 128]); S("indT16", [16, 1024])
        S("st", [128, 16])
        S("rn", [16, 256]); S("nrm", [16, 256])
        P("ptp", [128, 2 * 1024], BF16)
        P("pqk", [128, 2 * 512])
        P("pbig", [128, 1024])

        S("s_sb", [128, 2048]); S("sqx", [128, 2048])
        sc = Sched(stack, nc)
        with contextlib.ExitStack() as setup_stack:
            for (nm, shp, dt) in (("xs", [128, 2 * 1024], F32),
                                  ("xsn", [128, 2 * 1024], BF16),
                                  ("wetmp", [64, 2 * 320], BF16),
                                  ("wep", [128, 2 * 320], BF16),
                                  ("wrp", [128, 2 * 320], BF16),
                                  ("z2", [128, 1024], F32),
                                  ("z3", [128, 1024], F32),
                                  ("z4", [128, 1024], F32),
                                  ("z5", [128, 1024], F32),
                                  ("qsc", [128, 64], F32),
                                  ("qrc", [128, 64], F32)):
                sb[nm] = setup_stack.enter_context(
                    nc.sbuf_tensor("sbs_" + nm, shp, dt))
            _emit_l0_setup(nc, sc, sb, ap)
        S("sx", [128, 2048])
        S("xn", [128, 2048]); S("xnb", [128, 2048], BF16)
        S("xnT", [128, 8 * 256], BF16)
        S("qfm", [128, 8 * 256]); S("kfm", [128, 8 * 256])
        S("qnb", [128, 8 * 256], BF16); S("knb", [128, 8 * 256], BF16)
        S("esc", [64, 8 * 256], BF16); S("rsc", [64, 8 * 256], BF16)
        S("attnf", [128, 8 * 256], BF16); S("odds", [64, 8 * 256], BF16)
        S("g_sb", [128, 2048])
        S("z2", [128, 1024]); S("z3", [128, 1024])
        S("z4", [128, 1024]); S("z5", [128, 1024])
        S("xdb", [128, 2048], BF16)
        S("qsc", [128, 64]); S("qrc", [128, 64])
        _emit_l0_main(nc, sc, sb, ap, x1_out)
        if debug:
            with nc.Block() as blk:
                for nm in ("xn", "xnb", "xnT", "qfm", "kfm", "qnb", "knb",
                           "esc", "rsc", "attnf", "s_sb", "g_sb", "xdb",
                           "wd", "wdT", "webd", "wrbd"):
                    t = sb[nm]
                    o = nc.dram_tensor("dbg_" + nm, list(t.ap().shape),
                                       t.ap().dtype, kind="ExternalOutput").ap()
                    sc.dma("y", lambda e, _t=t, _o=o: e.dma_start(out=_o, in_=_t[:, :]))
                sc.wait("y", [(sc.dy, sc.dy.n)])
                sc.emit(blk)
    return nc


def _emit_l0_setup(nc, sc, sb, ap):
    with nc.Block() as blk:
        loads_y = [
            lambda e: e.dma_start(out=sb["xs"][:, :].rearrange("p (c f) -> p c f", c=2),
                                  in_=ap["xs"].rearrange("(c p) f -> p c f", p=128)),
            lambda e: e.dma_start(out=sb["bqc"][:, :], in_=ap["bqc"]),
            lambda e: e.dma_start(out=sb["bkc"][:, :], in_=ap["bkc"]),
            lambda e: e.dma_start(out=sb["bc2"][:, :], in_=ap["bc2"]),
            lambda e: e.dma_start(out=sb["bob"][:, :], in_=ap["bob"]),
            lambda e: e.dma_start(out=sb["fbb"][:, :], in_=ap["fbb"]),
            lambda e: e.dma_start(out=sb["ident"][:, :], in_=ap["ident"]),
            lambda e: e.dma_start(out=sb["ind16"][:, :], in_=ap["ind16"]),
            lambda e: e.dma_start(out=sb["indT16"][:, :], in_=ap["indT16"]),
        ]
        for f in loads_y:
            t_ldf = sc.dma("y", f)
        loads_g = [
            lambda e: e.dma_start(out=sb["wq"][:, :].rearrange("p (kc n) -> p kc n", kc=8),
                                  in_=ap["wq"].rearrange("(kc p) n -> p kc n", p=128)),
            lambda e: e.dma_start(out=sb["wk"][:, :].rearrange("p (kc n) -> p kc n", kc=8),
                                  in_=ap["wk"].rearrange("(kc p) n -> p kc n", p=128)),
            lambda e: e.dma_start(out=sb["wo"][:, :].rearrange("p (kc n) -> p kc n", kc=8),
                                  in_=ap["wo"].rearrange("(kc p) n -> p kc n", p=128)),
            lambda e: e.dma_start(out=sb["wep"][:, :].rearrange("p (c f) -> p c f", c=2),
                                  in_=ap["wep"].rearrange("(c p) f -> p c f", p=128)),
            lambda e: e.dma_start(out=sb["wrp"][:, :].rearrange("p (c f) -> p c f", c=2),
                                  in_=ap["wrp"].rearrange("(c p) f -> p c f", p=128)),
            lambda e: e.dma_start(out=sb["wcz_a"][:, :], in_=ap["wcz_a"]),
            lambda e: e.dma_start(out=sb["wcz_b"][:, :], in_=ap["wcz_b"]),
        ]
        for f in loads_g:
            t_ldc = sc.dma("g", f)
        sc.wait("y", [(sc.dy, t_ldf)])
        sc.wait("g", [(sc.dg, t_ldc)])
        sc.wait("v", [(sc.dy, t_ldf), (sc.dg, t_ldc)])
        sc.emit(blk)

    with nc.Block() as blk:
        t_ln = ln_tile(sc, sb, src="xs", n_blk=2, out_f32=None, out_bf16="xsn",
                       waits=[])

        def we_mms(e, wsrc):
            last = None
            for h in range(H):
                for lc in range(2):
                    last = e.matmul(
                        sb["pqk"][0:64, h * 20:(h + 1) * 20],
                        sb["xsn"][:, lc * 1024 + h * 64: lc * 1024 + (h + 1) * 64],
                        sb[wsrc][:, lc * 320 + h * 20: lc * 320 + (h + 1) * 20],
                        start=(lc == 0), stop=(lc == 1))
            return last
        t_we = sc.add("t", lambda e: we_mms(e, "wep"), inc=True,
                      waits=[("v", t_ln)])
        t_wev = sc.add("v", lambda e: e.tensor_copy(sb["wetmp"][0:64, 0:320],
                                                    sb["pqk"][0:64, 0:320]),
                       inc=True, waits=[("t", t_we)])
        t_wr = sc.add("t", lambda e: we_mms(e, "wrp"), inc=True,
                      waits=[("v", t_wev)])
        t_wrv = sc.add("v", lambda e: e.tensor_copy(sb["wetmp"][0:64, 320:640],
                                                    sb["pqk"][0:64, 0:320]),
                       inc=True, waits=[("t", t_wr)])
        t_z = sc.add("g", lambda e: (e.memset(sb["webd"][:, :], 0.0),
                                     e.memset(sb["wrbd"][:, :], 0.0))[-1],
                     inc=True)
        scat_toks = []
        for (src_off, dst) in ((0, "webd"), (320, "wrbd")):
            for par in range(2):
                def one(e, _so=src_off, _d=dst, _p=par):
                    return e.dma_start(
                        out=sb[_d][_p * 64:(_p + 1) * 64, :]
                        .rearrange("p (mc c) -> p mc c", mc=8)[:, :, _p * 32:_p * 32 + 20],
                        in_=sb["wetmp"][0:64, _so:_so + 320]
                        .rearrange("p (mc two c) -> p mc two c", mc=8, two=2)[:, :, _p, :],
                        allow_slow_non_contiguous=True)
                scat_toks.append(sc.dma("g", one,
                                        waits=[("v", t_wrv), ("g", t_z)]))

        for oc in range(8):
            prev_v = sc.ms["v"].n
            t_l = sc.dma("y", lambda e, _oc=oc: e.dma_start(
                out=sb["s_sb"][:, 0:1024],
                in_=ap["fw"].rearrange("(oc p) i -> oc p i", p=128)[_oc]),
                waits=[("v", prev_v)])
            quant_1024(sc, sb, src="s_sb", src_off=0, dst="wd",
                       dst_off=oc * 1024, wait_first=[(sc.dy, t_l)])
        evt = {}
        for g4 in range(16):
            slot = g4 % 2

            def tp(e, _g=g4, _s=slot):
                last = None
                for j in range(4):
                    bi = _g * 4 + j
                    oc, icc = bi // 8, bi % 8
                    last = e.transpose(
                        sb["ptp"][:, _s * 1024 + j * 128:_s * 1024 + (j + 1) * 128],
                        sb["wd"][:, oc * 1024 + icc * 128: oc * 1024 + (icc + 1) * 128],
                        sb["ident"][:, :])
                return last
            w = [("v", sc.ms["v"].n)] if g4 < 2 else [("v", evt[g4 - 2])]
            t_t = sc.add("t", tp, inc=True, waits=w)

            def ev(e, _g=g4, _s=slot):
                last = None
                for j in range(4):
                    bi = _g * 4 + j
                    oc, icc = bi // 8, bi % 8
                    last = e.tensor_copy(
                        sb["wdT"][:, icc * 1024 + oc * 128: icc * 1024 + (oc + 1) * 128],
                        sb["ptp"][:, _s * 1024 + j * 128:_s * 1024 + (j + 1) * 128])
                return last
            evt[g4] = sc.add("v", ev, inc=True, waits=[("t", t_t)])
        sc.wait("t", [("v", sc.ms["v"].n)] +
                [(sc.dg, max(scat_toks))])
        sc.wait("v", [("g", sc.ms["g"].n), (sc.dg, max(scat_toks))])
        sc.wait("s", [("v", sc.ms["v"].n)])
        sc.emit(blk)


def _emit_l0_main(nc, sc, sb, ap, x1_out):
    with nc.Block() as blk:
        guards = {}
        for i in range(NTILE):
            off = 0
            w_load = []
            if i - 1 in guards:
                w_load.append(("v", guards[i - 1]["gate"]))
            t_x = sc.dma("y", lambda e, _i=i, _o=off: e.dma_start(
                out=sb["sx"][:, _o:_o + 2048].rearrange("p (b f) -> p b f", b=2),
                in_=ap["x"].rearrange("(i b p) f -> i p b f", b=2, p=128)[_i]),
                waits=w_load)
            guards[i] = _l0_tile(sc, sb, i, off, t_x, x1_out,
                                 guards.get(i - 1))
        sc.wait("y", [(sc.dy, sc.dy.n)])
        sc.emit(blk)


def _l0_tile(sc, sb, i, off, t_x, x1_out, pg):
    g = {}
    extra = []
    if pg:
        extra.append(("t", pg["ssq_t"]))
        extra.append(("t", pg["xd_tp_t"]))
    t_ln = ln_tile(sc, sb, src="sx", n_blk=2, out_f32="xn", out_bf16="xnb",
                   waits=[(sc.dy, t_x)], src_off=off, extra_fin_waits=extra)
    g["ln"] = t_ln

    tp_guard = {0: pg["xd_ev2"] if pg else None, 1: pg["xd_ev3"] if pg else None}
    w0 = [("v", t_ln)]
    if pg:
        w0.append(("v", pg["proj_last"]))
        w0.append(("t", pg["ffn_t"]))
    evt, t_tp_last = transpose_16(sc, sb, "xnb", "xnT", w0, tp_guard)
    g["tp_ev2"], g["tp_ev3"] = evt[2], evt[3]

    t_proj_ev = {}
    t_proj_mm = {}
    idx = 0
    for (wname, bname, oname) in (("wq", "bqc", "qfm"), ("wk", "bkc", "kfm")):
        for mc in range(8):
            slot = idx % 2

            def mm(e, _w=wname, _mc=mc, _s=slot):
                last = None
                for kc in range(8):
                    last = e.matmul(
                        sb["pqk"][:, _s * 512:_s * 512 + 256],
                        sb[_w][:, kc * 1024 + _mc * 128: kc * 1024 + (_mc + 1) * 128],
                        sb["xnT"][:, kc * 256:(kc + 1) * 256],
                        start=(kc == 0), stop=(kc == 7))
                return last
            w = [("v", evt[3])]
            if idx >= 2:
                w.append(("v", t_proj_ev[idx - 2]))
            elif pg:
                w.append(("v", pg["at_ev"]))
                w.append(("v", pg["qn_v"]))
            t_proj_mm[idx] = sc.add("t", mm, inc=True, waits=w)
            t_proj_ev[idx] = sc.add(
                "v", lambda e, _b=bname, _o=oname, _mc=mc, _s=slot: e.tensor_scalar(
                    sb[_o][:, _mc * 256:(_mc + 1) * 256],
                    sb["pqk"][:, _s * 512:_s * 512 + 256],
                    sb[_b][:, _mc:_mc + 1], None, ALU.add),
                inc=True, waits=[("t", t_proj_mm[idx])])
            idx += 1
    g["proj_last"] = t_proj_ev[15]
    g["proj_t"] = t_proj_mm[15]

    t_nrm = {}
    t_ssq_all = None
    for (j, (src, dstb)) in enumerate((("qfm", "qnb"), ("kfm", "knb"))):
        wait_src = t_proj_ev[7] if j == 0 else t_proj_ev[15]
        w_sq = [("v", wait_src), ("v", t_ln)]
        if j == 1:
            w_sq.append(("t", t_ssq_all))
        t_sq = sc.add("s", lambda e, _s=src: e.activation(
            sb["sqx"][:, 0:2048], sb[_s][:, 0:2048], AF.Square),
            inc=True, waits=w_sq)
        w_ssq = [("s", t_sq)]
        if j == 0:
            w_ssq.append(("v", t_proj_ev[14]))
        else:
            w_ssq.append(("v", t_nrm[(0, 7)]))

        def ssq(e):
            last = None
            for mc in range(8):
                last = e.matmul(
                    sb["pqk"][0:16, 0:256],
                    sb["ind16"][:, mc * 16:(mc + 1) * 16],
                    sb["sqx"][:, mc * 256:(mc + 1) * 256],
                    start=(mc == 0), stop=(mc == 7))
            return last
        t_ssq = sc.add("t", ssq, inc=True, waits=w_ssq)
        t_ssq_all = t_ssq
        t_n = sc.add("s", lambda e: e.activation(
            sb["nrm"][:, 0:256], sb["pqk"][0:16, 0:256], AF.Sqrt),
            inc=True, waits=[("t", t_ssq)])
        t_rn = sc.add("v", lambda e: (
            sc.G(e, e.tensor_scalar(sb["nrm"][:, 0:256], sb["nrm"][:, 0:256],
                                    1e-12, None, ALU.max)),
            e.reciprocal(sb["rn"][:, 0:256], sb["nrm"][:, 0:256]))[-1],
            inc=True, waits=[("s", t_n)])
        for mc in range(8):
            slot = 1 - (mc % 2)

            def bc(e, _mc=mc, _s=slot):
                return e.matmul(sb["pqk"][:, _s * 512:_s * 512 + 256],
                                sb["indT16"][:, _mc * 128:(_mc + 1) * 128],
                                sb["rn"][:, 0:256], start=True, stop=True)
            w = [("v", t_rn)]
            if mc >= 2:
                w.append(("v", t_nrm[(j, mc - 2)]))
            elif mc == 1 and j == 0:
                pass
            t_bc = sc.add("t", bc, inc=True, waits=w)
            t_nrm[(j, mc)] = sc.add(
                "v", lambda e, _src=src, _d=dstb, _mc=mc, _s=slot: e.tensor_tensor(
                    sb[_d][:, _mc * 256:(_mc + 1) * 256],
                    sb[_src][:, _mc * 256:(_mc + 1) * 256],
                    sb["pqk"][:, _s * 512:_s * 512 + 256], ALU.mult),
                inc=True, waits=[("t", t_bc)])
    g["qn_v"] = t_nrm[(1, 7)]
    g["ssq_t"] = t_ssq_all

    t_esc_ev = {}
    t_esc_mm = {}
    for (j, (wbd, qsrc, dst)) in enumerate((("webd", "qnb", "esc"),
                                            ("wrbd", "knb", "rsc"))):
        wait_src = t_nrm[(0, 7)] if j == 0 else t_nrm[(1, 7)]
        for pair in range(4):
            slot = pair % 2

            def mm(e, _w=wbd, _q=qsrc, _p=pair, _s=slot):
                last = None
                for h2 in range(2):
                    mc = _p * 2 + h2
                    last = e.matmul(
                        sb["pqk"][0:64, _s * 512 + h2 * 256:_s * 512 + (h2 + 1) * 256],
                        sb[_w][:, mc * 128:mc * 128 + 64],
                        sb[_q][:, mc * 256:(mc + 1) * 256],
                        start=True, stop=True)
                return last
            w = [("v", wait_src)]
            if (j, pair) == (0, 0):
                w.append(("v", t_nrm[(1, 7)]))
            if pair >= 2:
                w.append(("v", t_esc_ev[(j, pair - 2)]))
            elif j == 1:
                w.append(("v", t_esc_ev[(0, 2 + pair)]))
            t_esc_mm[(j, pair)] = sc.add("t", mm, inc=True, waits=w)
            t_esc_ev[(j, pair)] = sc.add(
                "v", lambda e, _d=dst, _p=pair, _s=slot: e.tensor_copy(
                    sb[_d][0:64, _p * 512:(_p + 1) * 512],
                    sb["pqk"][0:64, _s * 512:_s * 512 + 512]),
                inc=True, waits=[("t", t_esc_mm[(j, pair)])])
    g["esc_t"] = t_esc_mm[(1, 3)]

    t_at_ev = {}
    t_at_mm = {}
    for mc in range(8):
        def mm(e, _mc=mc):
            last = None
            for par in range(2):
                rb = par * 32
                out = sb["pqk"][0:64, par * 512 + (_mc % 2) * 256:
                                par * 512 + (_mc % 2) * 256 + 256]
                for (wz, ssrc) in (("wcz_a", "esc"), ("wcz_b", "rsc")):
                    last = e.matmul(
                        out, sb[wz][rb:rb + 20, :],
                        sb[ssrc][rb:rb + 20, _mc * 256:(_mc + 1) * 256],
                        start=(wz == "wcz_a"), stop=(wz == "wcz_b"))
            return last
        w = [("v", t_esc_ev[(0, mc // 2)]), ("v", t_esc_ev[(1, mc // 2)])]
        if mc == 0:
            w.append(("v", t_esc_ev[(1, 2)]))
            w.append(("v", t_esc_ev[(1, 3)]))
        else:
            w.append(("v", t_at_ev[mc - 1]))
        if mc == 0 and pg:
            w.append((sc.dg, pg["odd_dma"]))
        t_at_mm[mc] = sc.add("t", mm, inc=True, waits=w)

        def ev(e, _mc=mc):
            e.tensor_scalar(sb["attnf"][0:64, _mc * 256:(_mc + 1) * 256],
                            sb["pqk"][0:64, (_mc % 2) * 256:(_mc % 2) * 256 + 256],
                            sb["bc2"][0:64, 0:1], None, ALU.add)
            return e.tensor_scalar(
                sb["odds"][0:64, _mc * 256:(_mc + 1) * 256],
                sb["pqk"][0:64, 512 + (_mc % 2) * 256:512 + (_mc % 2) * 256 + 256],
                sb["bc2"][0:64, 0:1], None, ALU.add)
        t_at_ev[mc] = sc.add("v", ev, inc=True, waits=[("t", t_at_mm[mc])])
    g["at_ev"] = t_at_ev[7]
    w_odd = [("v", t_at_ev[7])]
    if pg:
        w_odd.append(("t", pg["smm_t"]))
    t_odd = sc.dma("g", lambda e: e.dma_start(
        out=sb["attnf"][64:128, :], in_=sb["odds"][0:64, :]), waits=w_odd)
    g["odd_dma"] = t_odd

    t_s_ev = {}
    t_s_mm = {}
    for tc in range(2):
        def mm(e, _tc=tc):
            last = None
            for nc_ in range(2):
                for kc in range(8):
                    last = e.matmul(
                        sb["pbig"][:, nc_ * 512:(nc_ + 1) * 512],
                        sb["attnf"][:, kc * 256 + _tc * 128: kc * 256 + _tc * 128 + 128],
                        sb["wo"][:, kc * 1024 + nc_ * 512: kc * 1024 + (nc_ + 1) * 512],
                        start=(kc == 0), stop=(kc == 7))
            return last
        w = [(sc.dg, t_odd)]
        if tc == 1:
            w.append(("v", t_s_ev[0]))
        elif pg:
            w.append(("s", pg["ffn_s"]))
        t_s_mm[tc] = sc.add("t", mm, inc=True, waits=w)
        wse = [("t", t_s_mm[tc])]
        if tc == 0 and pg:
            wse.append((sc.dy, pg["x1_dma"]))
        t_s_ev[tc] = sc.add("v", lambda e, _tc=tc: e.tensor_tensor(
            sb["s_sb"][:, _tc * 1024:(_tc + 1) * 1024],
            sb["pbig"][:, 0:1024], sb["bob"][:, :], ALU.add),
            inc=True, waits=wse)
    g["smm_t"] = t_s_mm[1]

    t_q1 = quant_1024(sc, sb, src="xn", src_off=0, dst="xdb", dst_off=0,
                      wait_first=[("v", t_ln)] +
                      ([("t", pg["xd_tp_t"])] if pg else []))
    t_q2 = quant_1024(sc, sb, src="xn", src_off=1024, dst="xdb", dst_off=1024,
                      wait_first=[("v", t_q1)])
    xd_guard = {0: evt[2], 1: evt[3]}
    w_xd = [("v", t_q2), ("t", g["proj_t"])]
    xevt, t_xd_tp = transpose_16(sc, sb, "xdb", "xnT", w_xd, xd_guard)
    g["xd_ev2"], g["xd_ev3"] = xevt[2], xevt[3]
    g["xd_tp_t"] = t_xd_tp

    t_ff_s = None
    for tc in range(2):
        def mm(e, _tc=tc):
            last = None
            for nc_ in range(2):
                for ic in range(8):
                    last = e.matmul(
                        sb["pbig"][:, nc_ * 512:(nc_ + 1) * 512],
                        sb["xnT"][:, ic * 256 + _tc * 128: ic * 256 + _tc * 128 + 128],
                        sb["wdT"][:, ic * 1024 + nc_ * 512: ic * 1024 + (nc_ + 1) * 512],
                        start=(ic == 0), stop=(ic == 7))
            return last
        w = [("v", xevt[3])]
        w.append(("v", t_s_ev[1]) if tc == 0 else ("v", t_ff_v))
        t_mm = sc.add("t", mm, inc=True, waits=w)
        g["ffn_t"] = t_mm
        t_ff_v = sc.add("v", lambda e, _tc=tc: e.tensor_tensor(
            sb["g_sb"][:, _tc * 1024:(_tc + 1) * 1024],
            sb["pbig"][:, 0:1024], sb["fbb"][:, :], ALU.add),
            inc=True, waits=[("t", t_mm)])
        t_ff_s = sc.add("s", lambda e, _tc=tc: e.activation(
            sb["g_sb"][:, _tc * 1024:(_tc + 1) * 1024],
            sb["g_sb"][:, _tc * 1024:(_tc + 1) * 1024], AF.Silu),
            inc=True, waits=[("v", t_ff_v)])
    g["ffn_s"] = t_ff_s

    def gate(e):
        sc.G(e, e.tensor_tensor(sb["s_sb"][:, 0:2048], sb["s_sb"][:, 0:2048],
                                sb["g_sb"][:, 0:2048], ALU.mult))
        return e.tensor_tensor(sb["s_sb"][:, 0:2048], sb["s_sb"][:, 0:2048],
                               sb["sx"][:, off:off + 2048], ALU.add)
    w_gate = [("s", t_ff_s)]
    if pg:
        w_gate.append((sc.dy, pg["x1_dma"]))
    t_gate = sc.add("v", gate, inc=True, waits=w_gate)
    g["gate"] = t_gate
    g["x1_dma"] = sc.dma("y", lambda e, _i=i: e.dma_start(
        out=x1_out.rearrange("(i b p) f -> i p b f", b=2, p=128)[_i],
        in_=sb["s_sb"][:, 0:2048].rearrange("p (b f) -> p b f", b=2)),
        waits=[("v", t_gate)])
    return g


def _np32(a):
    return np.ascontiguousarray(np.asarray(a, dtype=np.float32))


def _l0_inmaps(inputs):
    x = _np32(inputs["L0_attn_x"])
    Wq = _np32(inputs["L0_attn_Wq"]); Wk = _np32(inputs["L0_attn_Wk"])
    Wo = _np32(inputs["L0_attn_Wo"])
    We = _np32(inputs["L0_attn_We"]); Wr = _np32(inputs["L0_attn_Wr"])
    Wc = _np32(inputs["L0_attn_Wc"])
    bq = _np32(inputs["L0_attn_bq"]); bk = _np32(inputs["L0_attn_bk"])
    bc = _np32(inputs["L0_attn_bc"]); bo = _np32(inputs["L0_attn_bo"])
    fw = _np32(inputs["L0_ffn_weight"]); fb = _np32(inputs["L0_ffn_bias"])

    wep = np.zeros((256, H * LOW), np.float32)
    wep[:LEN] = We.transpose(1, 0, 2).reshape(LEN, H * LOW)
    wrp = np.zeros((256, H * LOW), np.float32)
    wrp[:LEN] = Wr.transpose(1, 0, 2).reshape(LEN, H * LOW)
    wcz_a = np.zeros((64, DK), np.float32)
    wcz_a[0:20] = Wc[0:20]; wcz_a[32:52] = Wc[0:20]
    wcz_b = np.zeros((64, DK), np.float32)
    wcz_b[0:20] = Wc[20:40]; wcz_b[32:52] = Wc[20:40]
    shared = dict(
        wq=Wq, wk=Wk, wo=Wo, wep=wep, wrp=wrp, wcz_a=wcz_a, wcz_b=wcz_b,
        bqc=np.ascontiguousarray(bq.reshape(8, 128).T),
        bkc=np.ascontiguousarray(bk.reshape(8, 128).T),
        bc2=np.ascontiguousarray(np.tile(bc, 2)[:, None]),
        bob=np.ascontiguousarray(np.broadcast_to(bo, (128, D))),
        fbb=np.ascontiguousarray(np.broadcast_to(fb, (128, D))),
        fw=fw, ident=CONSTS["ident"], ind16=CONSTS["ind16"],
        indT16=CONSTS["indT16"],
    )
    maps = []
    for c in range(8):
        b, hh = c // 2, c % 2
        m = dict(shared)
        m["x"] = np.ascontiguousarray(x[b, hh * NT:(hh + 1) * NT])
        xs = np.zeros((256, D), np.float32)
        xs[:LEN] = x[b, IDX200]
        m["xs"] = xs
        maps.append(m)
    return maps


def build_l1a():
    nc = bass.Bass("TRN2", target_bir_lowering=False)
    ap = {}
    for name, shape, dt in [
        ("x1", [NT, D], F32),
        ("wqT", [D, D], F32), ("wkT", [D, D], F32), ("wvT", [D, D], F32),
        ("bqc", [128, 8], F32),
        ("bkb", [128, D], F32), ("bvb", [128, D], F32),
        ("wg", [D, 8], F32), ("bgb", [128, 8], F32),
        ("ident", [128, 128], BF16), ("ones128", [128, 1], BF16),
    ]:
        ap[name] = nc.dram_tensor(name, shape, dt, kind="ExternalInput").ap()
    o_xnf = nc.dram_tensor("xnf", [128, 8 * NT], BF16, kind="ExternalOutput").ap()
    o_qh = nc.dram_tensor("qh", [128, 8 * NT], BF16, kind="ExternalOutput").ap()
    o_kvp = nc.dram_tensor("kvp", [64, 1040], F32, kind="ExternalOutput").ap()
    o_w = nc.dram_tensor("wgate", [128, 32 * 8], F32, kind="ExternalOutput").ap()

    with contextlib.ExitStack() as stack:
        sb = {}

        def S(name, shape, dt=F32):
            sb[name] = stack.enter_context(nc.sbuf_tensor("sb_" + name, shape, dt))

        def P(name, shape, dt=F32):
            sb[name] = stack.enter_context(nc.psum_tensor("ps_" + name, shape, dt))

        S("wqT", [128, 8 * 1024], BF16); S("wkT", [128, 8 * 1024], BF16)
        S("wvT", [128, 8 * 1024], BF16)
        S("wg", [128, 8 * 8], BF16)
        S("bqc", [128, 8]); S("bkb", [128, D]); S("bvb", [128, D])
        S("bgb", [128, 8])
        S("ident", [128, 128], BF16); S("ones128", [128, 1], BF16)
        S("st", [128, 16]); S("sqx", [128, 2048])
        S("sx", [128, 2048])
        S("xn", [128, 2048]); S("xnb", [128, 2048], BF16)
        S("xnT", [128, 8 * 256], BF16)
        S("qf", [128, 8 * 256])
        S("ze", [128, 8 * 256])
        S("qeb", [128, 8 * 256], BF16)
        S("kb", [128, 2 * 1024], BF16); S("vb", [128, 2 * 1024], BF16)
        S("ktmp", [128, 1024]); S("ktm2", [128, 1024])
        S("lg", [128, 32 * 8])
        S("pr", [128, 32 * 8])
        S("cnt", [128, 32 * 8])
        S("tc1", [128, 32]); S("tc2", [128, 32])
        S("kvs", [64, 1040])
        P("ptp", [128, 1024], BF16)
        P("pqk", [128, 2 * 512])
        P("pbig", [128, 1024])
        P("pkv", [64, 1024])
        P("pks", [64, 16])

        sc = Sched(stack, nc)
        _emit_l1a(nc, sc, sb, ap, o_xnf, o_qh, o_kvp, o_w)
    return nc


def _emit_l1a(nc, sc, sb, ap, o_xnf, o_qh, o_kvp, o_w):
    with nc.Block() as blk:
        for f in [
            lambda e: e.dma_start(out=sb["bqc"][:, :], in_=ap["bqc"]),
            lambda e: e.dma_start(out=sb["bkb"][:, :], in_=ap["bkb"]),
            lambda e: e.dma_start(out=sb["bvb"][:, :], in_=ap["bvb"]),
            lambda e: e.dma_start(out=sb["bgb"][:, :], in_=ap["bgb"]),
            lambda e: e.dma_start(out=sb["ident"][:, :], in_=ap["ident"]),
            lambda e: e.dma_start(out=sb["ones128"][:, :], in_=ap["ones128"]),
        ]:
            t_ldf = sc.dma("y", f)
        for nm in ("wqT", "wkT", "wvT"):
            t_ldc = sc.dma("g", lambda e, _n=nm: e.dma_start(
                out=sb[_n][:, :].rearrange("p (kc n) -> p kc n", kc=8),
                in_=ap[_n].rearrange("(kc p) n -> p kc n", p=128)))
        t_ldc = sc.dma("g", lambda e: e.dma_start(
            out=sb["wg"][:, :].rearrange("p (kc n) -> p kc n", kc=8),
            in_=ap["wg"].rearrange("(kc p) n -> p kc n", p=128)))
        sc.wait("y", [(sc.dy, t_ldf)])
        sc.wait("g", [(sc.dg, t_ldc)])
        sc.wait("v", [(sc.dy, t_ldf), (sc.dg, t_ldc)])
        sc.emit(blk)

    with nc.Block() as blk:
        guards = {}
        for i in range(NTILE):
            w_load = []
            if i - 1 in guards:
                w_load.append(("v", guards[i - 1]["done"]))
            t_x = sc.dma("y", lambda e, _i=i: e.dma_start(
                out=sb["sx"][:, :].rearrange("p (b f) -> p b f", b=2),
                in_=ap["x1"].rearrange("(i b p) f -> i p b f", b=2, p=128)[_i]),
                waits=w_load)
            guards[i] = _l1a_tile(nc, sc, sb, ap, i, t_x, o_xnf, o_qh,
                                  guards.get(i - 1))
        t_kvev = sc.add("v", lambda e: (
            sc.G(e, e.tensor_copy(sb["kvs"][0:64, 0:1024], sb["pkv"][:, :])),
            e.tensor_copy(sb["kvs"][0:64, 1024:1040], sb["pks"][:, :]))[-1],
            inc=True, waits=[("t", guards[NTILE - 1]["kv_t"])])
        sc.dma("y", lambda e: e.dma_start(out=o_kvp, in_=sb["kvs"][0:64, :]),
               waits=[("v", t_kvev)])
        _emit_gate(nc, sc, sb, o_w, wait=("v", guards[NTILE - 1]["lg_v"]))
        sc.wait("y", [(sc.dy, sc.dy.n)])
        sc.wait("g", [(sc.dg, sc.dg.n)])
        sc.emit(blk)


def _l1a_tile(nc, sc, sb, ap, i, t_x, o_xnf, o_qh, pg):
    g = {}
    extra = []
    if pg:
        extra.append(("t", pg["lastuse_t"]))
    t_ln = ln_tile(sc, sb, src="sx", n_blk=2, out_f32="xn", out_bf16="xnb",
                   waits=[(sc.dy, t_x)], src_off=0, extra_fin_waits=extra)
    tp_guard = {0: pg["tp_ev3"] if pg else None}
    w0 = [("v", t_ln)]
    if pg:
        w0.append(("t", pg["lastuse_t"]))
        w0.append((sc.dy, pg["xnf_dma"]))
    evt, t_tp_last = transpose_16(sc, sb, "xnb", "xnT", w0, tp_guard, nslots=1)
    g["tp_ev3"] = evt[3]
    t_xnf = sc.dma("y", lambda e, _i=i: e.dma_start(
        out=o_xnf.rearrange("p (kc i t) -> p kc i t", kc=8, i=NTILE)[:, :, _i, :],
        in_=sb["xnT"][:, :].rearrange("p (kc t) -> p kc t", kc=8)),
        waits=[("v", evt[3])])
    g["xnf_dma"] = t_xnf

    t_q_ev = {}
    for mc in range(8):
        slot = mc % 2

        def mm(e, _mc=mc, _s=slot):
            last = None
            for kc in range(8):
                last = e.matmul(
                    sb["pqk"][:, _s * 512:_s * 512 + 256],
                    sb["wqT"][:, kc * 1024 + _mc * 128: kc * 1024 + (_mc + 1) * 128],
                    sb["xnT"][:, kc * 256:(kc + 1) * 256],
                    start=(kc == 0), stop=(kc == 7))
            return last
        w = [("v", evt[3])]
        if mc >= 2:
            w.append(("v", t_q_ev[mc - 2]))
        elif pg:
            w.append(("v", pg["q_done"]))
        t_mm = sc.add("t", mm, inc=True, waits=w)
        wev = [("t", t_mm)]
        if pg and mc < 2:
            wev.append((sc.dy, pg["qh_dma"]))
        t_ev = sc.add("v", lambda e, _mc=mc, _s=slot: (
            sc.G(e, e.tensor_scalar(sb["qf"][:, _mc * 256:(_mc + 1) * 256],
                                    sb["pqk"][:, _s * 512:_s * 512 + 256],
                                    sb["bqc"][:, _mc:_mc + 1], None, ALU.add)),
            e.tensor_scalar(sb["ze"][:, _mc * 256:(_mc + 1) * 256],
                            sb["qf"][:, _mc * 256:(_mc + 1) * 256],
                            0.0, None, ALU.min))[-1],
            inc=True, waits=wev)
        t_ex = sc.add("s", lambda e, _mc=mc: e.activation(
            sb["ze"][:, _mc * 256:(_mc + 1) * 256],
            sb["ze"][:, _mc * 256:(_mc + 1) * 256], AF.Exp),
            inc=True, waits=[("v", t_ev)])
        t_q_ev[mc] = sc.add("v", lambda e, _mc=mc: e.scalar_tensor_tensor(
            sb["qeb"][:, _mc * 256:(_mc + 1) * 256],
            sb["qf"][:, _mc * 256:(_mc + 1) * 256], 0.0,
            sb["ze"][:, _mc * 256:(_mc + 1) * 256],
            ALU.max, ALU.add), inc=True, waits=[("s", t_ex)])
    g["q_done"] = t_q_ev[7]
    t_qh = sc.dma("y", lambda e, _i=i: e.dma_start(
        out=o_qh.rearrange("p (mc i t) -> p mc i t", mc=8, i=NTILE)[:, :, _i, :],
        in_=sb["qeb"][:, :].rearrange("p (mc t) -> p mc t", mc=8)),
        waits=[("v", t_q_ev[7])])
    g["qh_dma"] = t_qh

    t_kv_ready = {}
    for (j, (wname, bname, oname)) in enumerate((("wkT", "bkb", "kb"),
                                                 ("wvT", "bvb", "vb"))):
        for tc in range(2):
            def mm(e, _w=wname, _tc=tc):
                last = None
                for nc_ in range(2):
                    for kc in range(8):
                        last = e.matmul(
                            sb["pbig"][:, nc_ * 512:(nc_ + 1) * 512],
                            sb["xnT"][:, kc * 256 + _tc * 128: kc * 256 + _tc * 128 + 128],
                            sb[_w][:, kc * 1024 + nc_ * 512: kc * 1024 + (nc_ + 1) * 512],
                            start=(kc == 0), stop=(kc == 7))
                return last
            idx = j * 2 + tc
            w = [("v", evt[3])]
            if idx == 0:
                if pg:
                    w.append(("t", pg["kv_t"]))
                    w.append(("v", pg["kv_done"]))
            else:
                w.append(("v", t_kv_ready[idx - 1]))
            t_mm = sc.add("t", mm, inc=True, waits=w)
            if j == 0:
                t_ev = sc.add("v", lambda e, _tc=tc: (
                    sc.G(e, e.tensor_tensor(sb["ktmp"][:, :],
                                            sb["pbig"][:, 0:1024],
                                            sb["bkb"][:, :], ALU.add)),
                    e.tensor_scalar(sb["ktm2"][:, :], sb["ktmp"][:, :],
                                    0.0, None, ALU.min))[-1],
                    inc=True, waits=[("t", t_mm)])
                t_ex = sc.add("s", lambda e: e.activation(
                    sb["ktm2"][:, :], sb["ktm2"][:, :], AF.Exp),
                    inc=True, waits=[("v", t_ev)])
                t_kv_ready[idx] = sc.add("v", lambda e, _tc=tc: e.scalar_tensor_tensor(
                    sb["kb"][:, _tc * 1024:(_tc + 1) * 1024],
                    sb["ktmp"][:, :], 0.0, sb["ktm2"][:, :],
                    ALU.max, ALU.add), inc=True, waits=[("s", t_ex)])
            else:
                t_kv_ready[idx] = sc.add("v", lambda e, _tc=tc: e.tensor_tensor(
                    sb["vb"][:, _tc * 1024:(_tc + 1) * 1024],
                    sb["pbig"][:, 0:1024], sb["bvb"][:, :], ALU.add),
                    inc=True, waits=[("t", t_mm)])
    g["kv_done"] = t_kv_ready[3]

    def kvacc(e):
        last = None
        for tc in range(2):
            for h in range(H):
                first = (i == 0 and tc == 0 and h in (0, 8))
                firstks = (i == 0 and tc == 0 and h == 0)
                lastmm = (i == NTILE - 1 and tc == 1)
                last = e.matmul(
                    sb["pkv"][:, h * 64:(h + 1) * 64],
                    sb["kb"][:, tc * 1024 + h * 64: tc * 1024 + (h + 1) * 64],
                    sb["vb"][:, tc * 1024 + h * 64: tc * 1024 + (h + 1) * 64],
                    start=first, stop=lastmm, skip_group_check=True)
                last = e.matmul(
                    sb["pks"][:, h:h + 1],
                    sb["kb"][:, tc * 1024 + h * 64: tc * 1024 + (h + 1) * 64],
                    sb["ones128"][:, :],
                    start=firstks, stop=(lastmm and h == H - 1),
                    skip_group_check=True)
        return last
    g["kv_t"] = sc.add("t", kvacc, inc=True,
                       waits=[("v", t_kv_ready[1]), ("v", t_kv_ready[3])])

    def gmm(e):
        last = None
        for tc in range(2):
            for kc in range(8):
                last = e.matmul(
                    sb["pqk"][0:128, 0:8] if False else
                    sb["pqk"][:, tc * 512:tc * 512 + 8],
                    sb["xnT"][:, kc * 256 + tc * 128: kc * 256 + tc * 128 + 128],
                    sb["wg"][:, kc * 8:(kc + 1) * 8],
                    start=(kc == 0), stop=(kc == 7))
        return last
    t_g_mm = sc.add("t", gmm, inc=True,
                    waits=[("v", t_q_ev[7]), ("v", t_q_ev[6])])
    g["lg_v"] = sc.add("v", lambda e, _i=i: (
        sc.G(e, e.tensor_tensor(sb["lg"][:, (2 * _i) * 8:(2 * _i + 1) * 8],
                                sb["pqk"][:, 0:8], sb["bgb"][:, :], ALU.add)),
        e.tensor_tensor(sb["lg"][:, (2 * _i + 1) * 8:(2 * _i + 2) * 8],
                        sb["pqk"][:, 512:520], sb["bgb"][:, :], ALU.add))[-1],
        inc=True, waits=[("t", t_g_mm)])
    g["lastuse_t"] = t_g_mm
    g["done"] = g["lg_v"]
    return g


def _emit_gate(nc, sc, sb, o_w, wait):
    lg3 = sb["lg"][:, :].rearrange("p (b e) -> p b e", b=32)
    pr3 = sb["pr"][:, :].rearrange("p (b e) -> p b e", b=32)

    def soft1(e):
        sc.G(e, e.reduce_max(sb["tc1"][:, :], lg3, AX.X))
        sc.G(e, e.tensor_tensor(pr3, lg3,
                                sb["tc1"][:, :].unsqueeze(2).broadcast_to([128, 32, 8]),
                                ALU.subtract))
        return e.engine_nop()
    t1 = sc.add("v", soft1, inc=True, waits=[wait])
    t2 = sc.add("s", lambda e: e.activation(sb["pr"][:, :], sb["pr"][:, :], AF.Exp),
                inc=True, waits=[("v", t1)])

    def soft2(e):
        sc.G(e, e.reduce_sum(sb["tc1"][:, :], pr3, AX.X))
        sc.G(e, e.reciprocal(sb["tc2"][:, :], sb["tc1"][:, :]))
        return e.tensor_tensor(pr3, pr3,
                               sb["tc2"][:, :].unsqueeze(2).broadcast_to([128, 32, 8]),
                               ALU.mult)
    t3 = sc.add("v", soft2, inc=True, waits=[("s", t2)])

    def top2(e):
        for j in range(E):
            prj = pr3[:, :, j:j + 1].squeeze(2)
            first = True
            for ii in range(E):
                if ii == j:
                    continue
                op = ALU.is_ge if ii < j else ALU.is_gt
                pri = pr3[:, :, ii:ii + 1].squeeze(2)
                if first:
                    sc.G(e, e.tensor_tensor(
                        sb["cnt"][:, :].rearrange("p (b e) -> p b e", b=32)[:, :, j:j + 1].squeeze(2),
                        pri, prj, op))
                    first = False
                else:
                    sc.G(e, e.tensor_tensor(sb["tc1"][:, :], pri, prj, op))
                    sc.G(e, e.tensor_tensor(
                        sb["cnt"][:, :].rearrange("p (b e) -> p b e", b=32)[:, :, j:j + 1].squeeze(2),
                        sb["cnt"][:, :].rearrange("p (b e) -> p b e", b=32)[:, :, j:j + 1].squeeze(2),
                        sb["tc1"][:, :], ALU.add))
        sc.G(e, e.tensor_scalar(sb["cnt"][:, :], sb["cnt"][:, :], 2.0, None, ALU.is_lt))
        sc.G(e, e.tensor_tensor(sb["pr"][:, :], sb["pr"][:, :], sb["cnt"][:, :], ALU.mult))
        sc.G(e, e.reduce_sum(sb["tc1"][:, :], pr3, AX.X))
        sc.G(e, e.tensor_scalar(sb["tc1"][:, :], sb["tc1"][:, :], 1e-8, None, ALU.add))
        sc.G(e, e.reciprocal(sb["tc2"][:, :], sb["tc1"][:, :]))
        return e.tensor_tensor(pr3, pr3,
                               sb["tc2"][:, :].unsqueeze(2).broadcast_to([128, 32, 8]),
                               ALU.mult)
    t4 = sc.add("v", top2, inc=True, waits=[("v", t3)])
    sc.dma("y", lambda e: e.dma_start(out=o_w, in_=sb["pr"][:, :]),
           waits=[("v", t4)])


def build_moe():
    nc = bass.Bass("TRN2", target_bir_lowering=False)
    ap = {}
    for name, shape, dt in [
        ("xnf", [128, 8 * NT], BF16),
        ("wgate", [128, 32 * 8], F32),
        ("w1", [E, D, HDIM], F32), ("w2", [E, HDIM, D], F32),
        ("b1c", [128, E * 32], F32), ("b2r", [1, E * D], F32),
    ]:
        ap[name] = nc.dram_tensor(name, shape, dt, kind="ExternalInput").ap()
    o_g = nc.dram_tensor("g", [NT, D], F32, kind="ExternalOutput").ap()

    with contextlib.ExitStack() as stack:
        sb = {}

        def S(name, shape, dt=F32):
            sb[name] = stack.enter_context(nc.sbuf_tensor("sb_" + name, shape, dt))

        def P(name, shape, dt=F32):
            sb[name] = stack.enter_context(nc.psum_tensor("ps_" + name, shape, dt))

        S("xnf", [128, 8 * NT], BF16)
        S("gh", [128, 16 * 1024])
        S("w1a", [128, 8 * 512], BF16); S("w1b", [128, 8 * 512], BF16)
        S("w2a", [128, 4 * 1024], BF16); S("w2b", [128, 4 * 1024], BF16)
        S("hb", [128, 2 * 256], BF16)
        S("wgate", [128, 32 * 8])
        S("b1c", [128, E * 32]); S("b2r", [1, E * D], BF16)
        S("onesb", [1, 128], BF16)
        P("ph", [128, 1024])
        P("pgA", [128, 1024]); P("pgB", [128, 1024])

        sc = Sched(stack, nc)
        _emit_moe(nc, sc, sb, ap, o_g)
    return nc


def _emit_moe(nc, sc, sb, ap, o_g):
    with nc.Block() as blk:
        sc.dma("g", lambda e: e.dma_start(
            out=sb["xnf"][:, :], in_=ap["xnf"]))
        sc.dma("y", lambda e: e.dma_start(out=sb["wgate"][:, :], in_=ap["wgate"]))
        t1 = sc.dma("y", lambda e: e.dma_start(out=sb["b1c"][:, :], in_=ap["b1c"]))
        t0 = sc.dma("g", lambda e: e.dma_start(out=sb["b2r"][:, :], in_=ap["b2r"]))
        t2 = sc.add("g", lambda e: e.memset(sb["onesb"][:, :], 1.0), inc=True)
        sc.wait("y", [(sc.dy, t1), (sc.dg, t0)])
        sc.wait("g", [(sc.dy, t1)])
        sc.emit(blk)

    for half in range(2):
        with nc.Block() as blk:
            _emit_moe_half(nc, sc, sb, ap, o_g, half)
            sc.emit(blk)


def _emit_moe_half(nc, sc, sb, ap, o_g, half):
    bufs = [("w1a", "w2a"), ("w1b", "w2b")]
    t_wload = {}
    t_lastuse = {}
    units = [(ei, s) for ei in range(E) for s in range(8)]

    def load_unit(uidx):
        ei, s = units[uidx]
        w1b, w2b = bufs[uidx % 2]
        w = []
        if uidx - 2 >= 0:
            w.append(("t", t_lastuse[uidx - 2]))
        t_a = sc.dma("g", lambda e, _e=ei, _s=s, _b=w1b: e.dma_start(
            out=sb[_b][:, :].rearrange("p (kc j) -> p kc j", kc=8),
            in_=ap["w1"][_e][:, _s * 512:(_s + 1) * 512]
            .rearrange("(kc p) j -> p kc j", p=128)), waits=w)
        t_b = sc.dma("g", lambda e, _e=ei, _s=s, _b=w2b: e.dma_start(
            out=sb[_b][:, :].rearrange("p (jc f) -> p jc f", jc=4),
            in_=ap["w2"][_e][_s * 512:(_s + 1) * 512, :]
            .rearrange("(jc p) f -> p jc f", p=128)))
        t_wload[uidx] = max(t_a, t_b)

    load_unit(0)
    load_unit(1)
    t_stt = {}
    t_gelu = {}
    for uidx, (ei, s) in enumerate(units):
        w1b, w2b = bufs[uidx % 2]
        last_mm = None

        def emit_second(tt, jc, s_, w2name, extra_w):
            nonlocal last_mm

            def secondmm(e, _jc=jc, _sl=jc % 2, _w2=w2name, _s=s_):
                last = None
                for tc in range(2):
                    pg = sb["pgA"] if tc == 0 else sb["pgB"]
                    for fc in range(2):
                        last = e.matmul(
                            pg[:, fc * 512:(fc + 1) * 512],
                            sb["hb"][:, _sl * 256 + tc * 128: _sl * 256 + tc * 128 + 128],
                            sb[_w2][:, _jc * 1024 + fc * 512: _jc * 1024 + (fc + 1) * 512],
                            start=(_jc == 0 and _s > 0), stop=(_jc == 3),
                            skip_group_check=True)
                return last
            w2w = [("s", t_gelu[(uidx, tt * 4 + jc)])] + list(extra_w)
            if jc == 0:
                wfree = []
                if (uidx, tt - 1, 1) in t_stt:
                    wfree.append(("v", t_stt[(uidx, tt - 1, 1)]))
                elif uidx >= 1 and (uidx - 1, 7, 1) in t_stt:
                    wfree.append(("v", t_stt[(uidx - 1, 7, 1)]))
                if s_ == 0:
                    def b2mm(e, _e=ei):
                        last = None
                        for tc in range(2):
                            pg = sb["pgA"] if tc == 0 else sb["pgB"]
                            for fc in range(2):
                                last = e.matmul(
                                    pg[:, fc * 512:(fc + 1) * 512],
                                    sb["onesb"][:, :],
                                    sb["b2r"][:, _e * 1024 + fc * 512: _e * 1024 + (fc + 1) * 512],
                                    start=True, stop=False, skip_group_check=True)
                        return last
                    sc.add("t", b2mm, inc=False, waits=wfree)
                else:
                    w2w = w2w + wfree
            last_mm = sc.add("t", secondmm, inc=(jc == 3), waits=w2w)
            if jc == 3:
                for tc in range(2):
                    blkid = half * 16 + tt * 2 + tc
                    first_unit = (uidx == 0)

                    def stt(e, _tt=tt, _tc=tc, _b=blkid, _e=ei, _fu=first_unit):
                        pg = sb["pgA"] if _tc == 0 else sb["pgB"]
                        dst = sb["gh"][:, (_tt * 2 + _tc) * 1024:(_tt * 2 + _tc + 1) * 1024]
                        wcol = sb["wgate"][:, _b * 8 + _e: _b * 8 + _e + 1]
                        if _fu:
                            return e.tensor_scalar(dst, pg[:, 0:1024], wcol,
                                                   None, ALU.mult)
                        return e.scalar_tensor_tensor(dst, pg[:, 0:1024], wcol,
                                                      dst, ALU.mult, ALU.add)
                    t_stt[(uidx, tt, tc)] = sc.add("v", stt, inc=True,
                                                   waits=[("t", last_mm)])

        pending = None
        for tt in range(8):
            for jc in range(4):
                slot = jc % 2

                def firstmm(e, _s=s, _tt=tt, _jc=jc, _sl=slot, _w1=w1b, _h=half):
                    last = None
                    for kc in range(8):
                        last = e.matmul(
                            sb["ph"][:, _sl * 512:_sl * 512 + 256],
                            sb[_w1][:, kc * 512 + _jc * 128: kc * 512 + (_jc + 1) * 128],
                            sb["xnf"][:, :].rearrange("p (kc t) -> p kc t", kc=8)
                            [:, kc, _h * 2048 + _tt * 256: _h * 2048 + (_tt + 1) * 256],
                            start=(kc == 0), stop=(kc == 7))
                    return last
                w = [(sc.dg, t_wload[uidx])]
                stepid = tt * 4 + jc
                if stepid >= 2:
                    w.append(("s", t_gelu[(uidx, stepid - 2)]))
                elif uidx >= 1:
                    w.append(("s", t_gelu[(uidx - 1, 31 - (1 - stepid))]))
                t_fm = sc.add("t", firstmm, inc=True, waits=w)
                t_gelu[(uidx, stepid)] = sc.add(
                    "s", lambda e, _e=ei, _s=s, _jc=jc, _sl=slot: e.activation(
                        sb["hb"][:, _sl * 256:(_sl + 1) * 256],
                        sb["ph"][:, _sl * 512:_sl * 512 + 256],
                        AF.Gelu_apprx_tanh,
                        bias=sb["b1c"][:, _e * 32 + _s * 4 + _jc: _e * 32 + _s * 4 + _jc + 1]),
                    inc=True, waits=[("t", t_fm)])
                if pending is not None:
                    emit_second(*pending)
                pending = (tt, jc, s, w2b, ())
        emit_second(*pending)
        pending = None
        t_lastuse[uidx] = last_mm
        if uidx + 2 < len(units):
            load_unit(uidx + 2)
    for chunk in range(4):
        t_si = sc.add("s", lambda e, _c=chunk: e.activation(
            sb["gh"][:, _c * 4096:(_c + 1) * 4096],
            sb["gh"][:, _c * 4096:(_c + 1) * 4096], AF.Silu),
            inc=True,
            waits=[("v", t_stt[(63, chunk * 2, 0)]),
                   ("v", t_stt[(63, chunk * 2, 1)]),
                   ("v", t_stt[(63, chunk * 2 + 1, 0)]),
                   ("v", t_stt[(63, chunk * 2 + 1, 1)])])
        sc.dma("y", lambda e, _c=chunk, _h=half: e.dma_start(
            out=o_g.rearrange("(hh c b p) f -> hh c p b f", hh=2, c=4, p=128)[_h, _c],
            in_=sb["gh"][:, _c * 4096:(_c + 1) * 4096]
            .rearrange("p (b f) -> p b f", b=4)),
            waits=[("s", t_si)])
    sc.wait("y", [(sc.dy, sc.dy.n)])
    sc.wait("g", [(sc.dg, sc.dg.n)])


def build_fin(debug=False):
    nc = bass.Bass("TRN2", target_bir_lowering=False)
    ap = {}
    for name, shape, dt in [
        ("x1", [NT, D], F32), ("qh", [128, 8 * NT], BF16),
        ("g", [NT, D], F32),
        ("kvp0", [64, 1040], F32), ("kvp1", [64, 1040], F32),
        ("woT", [D, D], F32), ("bo_b", [128, D], F32),
        ("indT16", [16, 1024], F32),
    ]:
        ap[name] = nc.dram_tensor(name, shape, dt, kind="ExternalInput").ap()
    o_out = nc.dram_tensor("out", [NT, D], F32, kind="ExternalOutput").ap()

    with contextlib.ExitStack() as stack:
        sb = {}

        def S(name, shape, dt=F32):
            sb[name] = stack.enter_context(nc.sbuf_tensor("sb_" + name, shape, dt))

        def P(name, shape, dt=F32):
            sb[name] = stack.enter_context(nc.psum_tensor("ps_" + name, shape, dt))

        S("woT", [128, 8 * 1024], BF16)
        S("bo_b", [128, D]); S("indT16", [16, 1024])
        S("kv0", [64, 1040]); S("kv1", [64, 1040])
        S("kv2", [128, 1040], BF16)
        S("ksi", [128, 16 * 16], BF16)
        S("qt", [128, 8 * 256], BF16)
        S("sx", [128, 2048]); S("gt", [128, 2048])
        S("attnf", [128, 8 * 256], BF16); S("odds", [64, 8 * 256], BF16)
        S("rden", [16, 256]); S("bcs", [128, 2 * 256])
        S("s_sb", [128, 2048])
        P("pnum", [128, 2 * 512])
        P("pnumO", [64, 2 * 512])
        P("pdenA", [16, 256]); P("pdenB", [16, 256])
        P("pbig", [128, 1024])

        sc = Sched(stack, nc)
        _emit_fin(nc, sc, sb, ap, o_out)
        if debug:
            with nc.Block() as blk:
                for nm in ("kv2", "ksi", "rden", "attnf", "odds", "bcs",
                           "s_sb", "qt", "gt", "sx"):
                    t = sb[nm]
                    o = nc.dram_tensor("dbg_" + nm, list(t.ap().shape),
                                       t.ap().dtype, kind="ExternalOutput").ap()
                    sc.dma("y", lambda e, _t=t, _o=o: e.dma_start(out=_o, in_=_t[:, :]))
                sc.wait("y", [(sc.dy, sc.dy.n)])
                sc.emit(blk)
    return nc


def _emit_fin(nc, sc, sb, ap, o_out):
    with nc.Block() as blk:
        t1 = sc.dma("y", lambda e: e.dma_start(out=sb["kv0"][0:64, :], in_=ap["kvp0"]))
        t2 = sc.dma("y", lambda e: e.dma_start(out=sb["kv1"][0:64, :], in_=ap["kvp1"]))
        sc.dma("y", lambda e: e.dma_start(out=sb["bo_b"][:, :], in_=ap["bo_b"]))
        t3 = sc.dma("y", lambda e: e.dma_start(out=sb["indT16"][:, :], in_=ap["indT16"]))
        t4 = sc.dma("g", lambda e: e.dma_start(
            out=sb["woT"][:, :].rearrange("p (kc n) -> p kc n", kc=8),
            in_=ap["woT"].rearrange("(kc p) n -> p kc n", p=128)))
        t5 = sc.add("v", lambda e: (
            sc.G(e, e.tensor_tensor(sb["kv0"][0:64, :], sb["kv0"][0:64, :],
                                    sb["kv1"][0:64, :], ALU.add)),
            e.tensor_copy(sb["kv2"][0:64, :], sb["kv0"][0:64, :]))[-1],
            inc=True, waits=[(sc.dy, max(t1, t2))])
        t6 = sc.add("g", lambda e: e.memset(sb["ksi"][:, :], 0.0), inc=True)
        t7 = sc.dma("g", lambda e: e.dma_start(
            out=sb["kv2"][64:128, :], in_=sb["kv2"][0:64, :]),
            waits=[("v", t5)])
        def ksi(e):
            last = None
            for h in range(H):
                last = e.tensor_copy(
                    sb["ksi"][0:64, h * 16 + h: h * 16 + h + 1],
                    sb["kv2"][0:64, 1024 + h: 1025 + h])
            return last
        t8 = sc.add("v", ksi, inc=True, waits=[("v", t5), ("g", t6)])
        t9 = sc.dma("g", lambda e: e.dma_start(
            out=sb["ksi"][64:128, :], in_=sb["ksi"][0:64, :]),
            waits=[("v", t8)])
        sc.wait("t", [(sc.dg, max(t4, t7, t9)), (sc.dy, t3)])
        sc.wait("y", [(sc.dy, sc.dy.n)])
        sc.wait("g", [(sc.dg, sc.dg.n)])
        sc.emit(blk)

    with nc.Block() as blk:
        guards = {}
        for i in range(NTILE):
            w_load = []
            if i - 1 in guards:
                w_load.append(("v", guards[i - 1]["done"]))
            t_q = sc.dma("y", lambda e, _i=i: e.dma_start(
                out=sb["qt"][:, :].rearrange("p (mc t) -> p mc t", mc=8),
                in_=ap["qh"].rearrange("p (mc i t) -> p mc i t", mc=8, i=NTILE)[:, :, _i, :]),
                waits=w_load)
            t_x = sc.dma("y", lambda e, _i=i: e.dma_start(
                out=sb["sx"][:, :].rearrange("p (b f) -> p b f", b=2),
                in_=ap["x1"].rearrange("(i b p) f -> i p b f", b=2, p=128)[_i]),
                waits=w_load)
            t_g = sc.dma("y", lambda e, _i=i: e.dma_start(
                out=sb["gt"][:, :].rearrange("p (b f) -> p b f", b=2),
                in_=ap["g"].rearrange("(i b p) f -> i p b f", b=2, p=128)[_i]),
                waits=w_load)
            guards[i] = _fin_tile(nc, sc, sb, ap, i, t_q, t_x, t_g, o_out,
                                  guards.get(i - 1))
        sc.wait("y", [(sc.dy, sc.dy.n)])
        sc.emit(blk)


def _fin_tile(nc, sc, sb, ap, i, t_q, t_x, t_g, o_out, pg):
    g = {}
    def denmm(e):
        last = None
        for h in range(H):
            hb = (h % 2) * 64
            pd = sb["pdenA"] if h % 2 == 0 else sb["pdenB"]
            last = e.matmul(
                pd[:, :],
                sb["ksi"][hb:hb + 64, h * 16:(h + 1) * 16],
                sb["qt"][hb:hb + 64, (h // 2) * 256:(h // 2 + 1) * 256],
                start=(h < 2), stop=(h >= H - 2))
        return last
    w = [(sc.dy, t_q)]
    if pg:
        w.append(("v", pg["rden_v"]))
    t_den = sc.add("t", denmm, inc=True, waits=w)
    t_rden = sc.add("v", lambda e: (
        sc.G(e, e.tensor_copy(sb["rden"][:, :], sb["pdenA"][:, :])),
        sc.G(e, e.tensor_tensor(sb["rden"][:, :], sb["pdenB"][:, :],
                                sb["rden"][:, :], ALU.add)),
        sc.G(e, e.tensor_scalar(sb["rden"][:, :], sb["rden"][:, :], 1e-6,
                                None, ALU.max)),
        e.reciprocal(sb["rden"][:, :], sb["rden"][:, :]))[-1],
        inc=True, waits=[("t", t_den)])
    g["rden_v"] = t_rden

    t_at = {}
    for mc in range(8):
        slot = mc % 2

        def mm(e, _mc=mc, _s=slot):
            last = None
            h0, h1 = 2 * _mc, 2 * _mc + 1
            e.matmul(sb["pnum"][0:64, _s * 512: _s * 512 + 256],
                     sb["kv2"][0:64, h0 * 64: h0 * 64 + 64],
                     sb["qt"][0:64, _mc * 256:(_mc + 1) * 256],
                     start=True, stop=True)
            e.matmul(sb["pnumO"][0:64, _s * 512: _s * 512 + 256],
                     sb["kv2"][64:128, h1 * 64: h1 * 64 + 64],
                     sb["qt"][64:128, _mc * 256:(_mc + 1) * 256],
                     start=True, stop=True)
            last = e.matmul(
                sb["pnum"][:, _s * 512 + 256: _s * 512 + 512],
                sb["indT16"][:, _mc * 128:(_mc + 1) * 128],
                sb["rden"][:, :], start=True, stop=True)
            return last
        w = [("v", t_rden)]
        if mc >= 2:
            w.append(("v", t_at[mc - 2]))
        elif pg:
            w.append(("v", pg["at_done"]))
            w.append((sc.dg, pg["odd_dma"]))
        t_mm = sc.add("t", mm, inc=True, waits=w)

        def ev(e, _mc=mc, _s=slot):
            sc.G(e, e.tensor_copy(sb["bcs"][:, _s * 256:(_s + 1) * 256],
                                  sb["pnum"][:, _s * 512 + 256:_s * 512 + 512]))
            sc.G(e, e.tensor_tensor(
                sb["attnf"][0:64, _mc * 256:(_mc + 1) * 256],
                sb["pnum"][0:64, _s * 512:_s * 512 + 256],
                sb["bcs"][0:64, _s * 256:(_s + 1) * 256], ALU.mult))
            return e.tensor_tensor(
                sb["odds"][0:64, _mc * 256:(_mc + 1) * 256],
                sb["pnumO"][0:64, _s * 512:_s * 512 + 256],
                sb["bcs"][64:128, _s * 256:(_s + 1) * 256], ALU.mult)
        t_at[mc] = sc.add("v", ev, inc=True, waits=[("t", t_mm)])
    g["at_done"] = t_at[7]
    w_odd = [("v", t_at[7])]
    if pg:
        w_odd.append(("t", pg["smm_t"]))
    t_odd = sc.dma("g", lambda e: e.dma_start(
        out=sb["attnf"][64:128, :], in_=sb["odds"][0:64, :]), waits=w_odd)
    g["odd_dma"] = t_odd

    t_s_ev = {}
    for tc in range(2):
        def mm(e, _tc=tc):
            last = None
            for nc_ in range(2):
                for kc in range(8):
                    last = e.matmul(
                        sb["pbig"][:, nc_ * 512:(nc_ + 1) * 512],
                        sb["attnf"][:, kc * 256 + _tc * 128: kc * 256 + _tc * 128 + 128],
                        sb["woT"][:, kc * 1024 + nc_ * 512: kc * 1024 + (nc_ + 1) * 512],
                        start=(kc == 0), stop=(kc == 7))
            return last
        w = [(sc.dg, t_odd)]
        if tc == 1:
            w.append(("v", t_s_ev[0]))
        elif pg:
            w.append(("v", pg["done"]))
        t_mm = sc.add("t", mm, inc=True, waits=w)
        g["smm_t"] = t_mm
        wse = [("t", t_mm)]
        if tc == 0 and pg:
            wse.append((sc.dy, pg["out_dma"]))
        t_s_ev[tc] = sc.add("v", lambda e, _tc=tc: e.tensor_tensor(
            sb["s_sb"][:, _tc * 1024:(_tc + 1) * 1024],
            sb["pbig"][:, 0:1024], sb["bo_b"][:, :], ALU.add),
            inc=True, waits=wse)

    def gate(e):
        sc.G(e, e.tensor_tensor(sb["s_sb"][:, 0:2048], sb["s_sb"][:, 0:2048],
                                sb["gt"][:, 0:2048], ALU.mult))
        return e.tensor_tensor(sb["s_sb"][:, 0:2048], sb["s_sb"][:, 0:2048],
                               sb["sx"][:, 0:2048], ALU.add)
    t_gate = sc.add("v", gate, inc=True,
                    waits=[("v", t_s_ev[1]), (sc.dy, t_x), (sc.dy, t_g)])
    g["done"] = t_gate
    g["out_dma"] = sc.dma("y", lambda e, _i=i: e.dma_start(
        out=o_out.rearrange("(i b p) f -> i p b f", b=2, p=128)[_i],
        in_=sb["s_sb"][:, 0:2048].rearrange("p (b f) -> p b f", b=2)),
        waits=[("v", t_gate)])
    return g


def _l1a_inmaps(inputs, x1s):
    Wq = _np32(inputs["L1_attn_Wq"]); Wk = _np32(inputs["L1_attn_Wk"])
    Wv = _np32(inputs["L1_attn_Wv"])
    bq = _np32(inputs["L1_attn_bq"]); bk = _np32(inputs["L1_attn_bk"])
    bv = _np32(inputs["L1_attn_bv"])
    Wg = _np32(inputs["L1_ffn_Wg"]); bg = _np32(inputs["L1_ffn_bg"])
    shared = dict(
        wqT=np.ascontiguousarray(Wq.T), wkT=np.ascontiguousarray(Wk.T),
        wvT=np.ascontiguousarray(Wv.T),
        bqc=np.ascontiguousarray(bq.reshape(8, 128).T),
        bkb=np.ascontiguousarray(np.broadcast_to(bk, (128, D))),
        bvb=np.ascontiguousarray(np.broadcast_to(bv, (128, D))),
        wg=Wg, bgb=np.ascontiguousarray(np.broadcast_to(bg, (128, 8))),
        ident=CONSTS["ident"],
        ones128=np.ones((128, 1), NPBF16),
    )
    return [dict(shared, x1=x1s[c]) for c in range(8)]


def _moe_inmaps(inputs, xnfs, wgates):
    W1 = _np32(inputs["L1_ffn_W1"]); W2 = _np32(inputs["L1_ffn_W2"])
    b1 = _np32(inputs["L1_ffn_b1"]); b2 = _np32(inputs["L1_ffn_b2"])
    b1c = np.ascontiguousarray(
        b1.reshape(E, 32, 128).transpose(2, 0, 1).reshape(128, E * 32))
    b2r = np.ascontiguousarray(b2.reshape(1, E * D))
    shared = dict(w1=W1, w2=W2, b1c=b1c, b2r=b2r)
    return [dict(shared, xnf=xnfs[c], wgate=wgates[c]) for c in range(8)]


def _fin_inmaps(inputs, x1s, qhs, gs, kvps):
    Wo = _np32(inputs["L1_attn_Wo"]); bo = _np32(inputs["L1_attn_bo"])
    shared = dict(
        woT=np.ascontiguousarray(Wo.T),
        bo_b=np.ascontiguousarray(np.broadcast_to(bo, (128, D))),
        indT16=CONSTS["indT16"],
    )
    maps = []
    for c in range(8):
        partner = c ^ 1
        maps.append(dict(shared, x1=x1s[c], qh=qhs[c], g=gs[c],
                         kvp0=kvps[c], kvp1=kvps[partner]))
    return maps


def kernel(**inputs):
    core_ids = list(range(8))
    res_a = run_bass_kernel_spmd(build_l0(), _l0_inmaps(inputs), core_ids=core_ids)
    x1s = [np.asarray(res_a.results[c]["x1"]) for c in range(8)]
    res_b = run_bass_kernel_spmd(build_l1a(), _l1a_inmaps(inputs, x1s),
                                 core_ids=core_ids)
    xnfs = [np.asarray(res_b.results[c]["xnf"]) for c in range(8)]
    qhs = [np.asarray(res_b.results[c]["qh"]) for c in range(8)]
    kvps = [np.asarray(res_b.results[c]["kvp"]) for c in range(8)]
    wgates = [np.asarray(res_b.results[c]["wgate"]) for c in range(8)]
    res_c = run_bass_kernel_spmd(build_moe(), _moe_inmaps(inputs, xnfs, wgates),
                                 core_ids=core_ids)
    gs = [np.asarray(res_c.results[c]["g"]) for c in range(8)]
    res_d = run_bass_kernel_spmd(build_fin(),
                                 _fin_inmaps(inputs, x1s, qhs, gs, kvps),
                                 core_ids=core_ids)
    out = np.zeros((B, L, D), np.float32)
    for c in range(8):
        b, hh = c // 2, c % 2
        out[b, hh * NT:(hh + 1) * NT] = np.asarray(res_d.results[c]["out"])
    return out
